# version 10
# baseline (speedup 1.0000x reference)
import numpy as np
import ml_dtypes
import concourse.bass as bass
import concourse.mybir as mybir
from concourse.bass_utils import run_bass_kernel_spmd

F32 = mybir.dt.float32
BF16 = mybir.dt.bfloat16
AF = mybir.ActivationFunctionType
ALU = mybir.AluOpType
AX = mybir.AxisListType

B, L, D = 4, 4096, 1024
T = 2048
SUB, NSUB = 512, 4
CH = 128
MAC, NMAC = 256, 8
CPM = MAC // CH
H, K, P = 32, 32, 64
DIN = 2 * D
CONVD = DIN + 4 * K
DPROJ, DPROJP = 4256, 4352
NDB, NIB, NCB, NPB = 8, 16, 17, 34
E = 8
LORA = 48
RMS_EPS, LN_EPS = 1e-5, 1e-5
PIH = float(np.pi / 2)
MAGIC = 12582912.0

ARENA = 144 * 1024

OF = dict(xres=0, xhat=65536, wexp0=98304, wexp1=114688, poolb=131072,
          wip0=139264, wip1=141312, stg=143360)
OA = dict(th=0, dtB=4096, lzr=8192, Rg=12288, Fg=20480,
          rl0=28672, flc=30720, coss=34816, sins=36864, pmag=38912,
          qmag=40960, cer=43008, cei=45056, qr=47104, qin=49152,
          prc=51200, pic=53248, bvr=55296, bvn=57344, crep=59392,
          xd=60416, bvrT=64512, bvnT=66560, xss=68608, ytile=76800,
          xbct=84992,
          qr1=102400, qr2=104448, qr3=106496,
          qin1=108544, qin2=110592, qin3=112640,
          rl=114688, pex=118784, qex=122880)
OB = dict(opw=0, in0=32768, in1=65536,
          cosg=131072, sing=135168, cegr=139264, cegi=143360,
          gt=131072, sq=139264,
          xh2=114688, pb2=135168, sq2=139264)
PRC2, PIC2, CER2, CEI2 = 126976, 129024, 131072, 133120
GM8 = 135168
XD2 = 143360
IB_YT, IB_ZT, IB_RG, IB_FG, IB_XR = 0, 8192, 16384, 20480, 24576
IB_SZ = 32768
DIS_OFF = 93952


def _perm_rows():
    rows = list(range(0, DIN)) + list(range(DIN, 2 * DIN))
    bb = 2 * DIN
    rows += [bb + 2 * j for j in range(K)] + [bb + 2 * j + 1 for j in range(K)]
    cb = 2 * DIN + 2 * K
    rows += [cb + 2 * j for j in range(K)] + [cb + 2 * j + 1 for j in range(K)]
    rows += list(range(DPROJ - H, DPROJ))
    return np.array(rows)


def _col128(vec):
    v = np.asarray(vec)
    n = v.shape[0] // 128
    return np.ascontiguousarray(v.reshape(n, 128).T)


def prep_shared(W):
    d = {}
    perm = _perm_rows()
    wip = np.asarray(W["L0_attn_in_proj_w"])[perm]
    wip = np.concatenate([wip, np.zeros((DPROJP - DPROJ, D), np.float32)], 0)
    d["w_inproj"] = np.ascontiguousarray(wip.T).astype(ml_dtypes.bfloat16)
    xperm = perm[DIN:DIN + CONVD] - DIN
    cw = np.asarray(W["L0_attn_conv_w"])[xperm]
    cb = np.asarray(W["L0_attn_conv_b"])[xperm]
    d["conv_w"] = np.ascontiguousarray(
        cw.reshape(NCB, 128, 4).transpose(1, 0, 2).reshape(128, NCB * 4)).astype(np.float32)
    d["conv_b"] = _col128(cb).astype(np.float32)
    d["w_exp"] = np.ascontiguousarray(
        np.asarray(W["L0_ffn_expert_w"]).transpose(0, 2, 1)).astype(ml_dtypes.bfloat16)
    d["b_exp"] = np.asarray(W["L0_ffn_expert_b"]).astype(np.float32)
    d["w_rout"] = np.ascontiguousarray(np.asarray(W["L0_ffn_router_w"]).T).astype(np.float32)
    d["b_rout"] = np.asarray(W["L0_ffn_router_b"]).reshape(E, 1).astype(np.float32)
    d["w_oproj"] = np.ascontiguousarray(np.asarray(W["L0_attn_out_proj_w"]).T).astype(ml_dtypes.bfloat16)
    d["w_th1"] = np.asarray(W["L0_attn_theta_w1"]).astype(np.float32)
    d["w_th2"] = np.asarray(W["L0_attn_theta_w2"]).astype(np.float32)
    d["thbase"] = _col128(np.asarray(W["L0_attn_theta_base"]).reshape(H * K)).astype(np.float32)
    d["lamb"] = _col128(np.asarray(W["L0_attn_lambda_base"]).reshape(H * K)).astype(np.float32)
    d["eta"] = _col128(np.asarray(W["L0_attn_viscosity_eta"]).reshape(H * K)).astype(np.float32)
    d["dt_bias"] = np.asarray(W["L0_attn_dt_bias"]).reshape(H, 1).astype(np.float32)
    Dv = np.asarray(W["L0_attn_D"])
    dcol = np.zeros((128, NIB), np.float32)
    for j in range(NIB):
        for r in range(128):
            dcol[r, j] = Dv[(128 * j + r) // 64]
    d["Dcol"] = dcol
    drow = np.zeros((4, 1024), np.float32)
    for s in range(4):
        for jj in range(8):
            drow[s, jj * CH:(jj + 1) * CH] = Dv[4 * jj + s]
    d["Drow"] = drow.astype(ml_dtypes.bfloat16)
    d["norm_w"] = _col128(np.asarray(W["L0_attn_norm_w"])).astype(np.float32)

    d["identity"] = np.eye(128, dtype=ml_dtypes.bfloat16)
    m = np.triu(np.ones((128, 128), np.float32))
    d["mask4"] = np.ascontiguousarray(np.tile(m, (1, 4))).astype(ml_dtypes.bfloat16)
    sel = np.zeros((32, 8 * 128), np.float32)
    for gb in range(8):
        for r in range(128):
            sel[4 * gb + r // 32, gb * 128 + r] = 1.0
    d["seldt"] = sel.astype(ml_dtypes.bfloat16)

    def _rep(off):
        mm = np.zeros((128, 128), np.float32)
        for r in range(128):
            mm[off + (r % 32), r] = 1.0
        return mm.astype(ml_dtypes.bfloat16)
    d["repA"], d["repB"] = _rep(0), _rep(32)
    d["repC"], d["repD"] = _rep(64), _rep(96)
    d["ones_col"] = np.ones((128, 1), np.float32)
    d["ones_row"] = np.ones((1, 128), np.float32)
    d["onesb_row"] = np.ones((1, 128), dtype=ml_dtypes.bfloat16)
    d["onesb_col"] = np.ones((128, 1), dtype=ml_dtypes.bfloat16)
    d["i4"] = np.ascontiguousarray(
        np.tile(np.eye(128, dtype=np.float32), (1, 4))).astype(ml_dtypes.bfloat16)
    oh = np.zeros((8, 8 * 128), np.float32)
    for e in range(8):
        oh[e, e * 128:(e + 1) * 128] = 1.0
    d["oh8"] = oh.astype(ml_dtypes.bfloat16)
    d["i8f"] = np.eye(8, dtype=np.float32)
    d["ones8x128"] = np.ones((8, 128), np.float32)
    return d


def prep_core(W, core):
    b, hf = core // 2, core % 2
    x = np.asarray(W["L0_attn_x"])[b, hf * T:(hf + 1) * T, :]
    d = {"x_in": np.ascontiguousarray(x.T).astype(np.float32)}
    sel2 = np.zeros((128, 2), np.float32)
    if hf == 1:
        sel2[:, 0] = 1.0
    d["sel2"] = sel2
    return d


class Bld:
    def __init__(self, stop_after=None, dbg=False):
        self.nc = bass.Bass("TRN2", target_bir_lowering=False, debug=False)
        self.sems, self.cnt = {}, {}
        self.dram, self.sb, self.ps = {}, {}, {}
        self.bankguard = {}
        self.wtok, self.wldtok, self.wbtok = {}, {}, {}
        self.tk2 = {}
        self.stop_after = stop_after
        self.dbg = dbg

    def sem(self, name):
        if name not in self.sems:
            self.sems[name] = self.nc.alloc_semaphore(name)
            self.cnt[name] = 0
        return self.sems[name]

    def inc(self, op, name, k=1):
        op.then_inc(self.sem(name), k)
        self.cnt[name] += k
        return self.cnt[name]

    def tok(self, name):
        self.sem(name)
        return self.cnt[name]

    def wait(self, eng, name, val=None):
        self.sem(name)
        eng.wait_ge(self.sems[name], self.cnt[name] if val is None else val)

    def vsync(self, v, op):
        self.inc(op, "vw")
        self.wait(v, "vw")

    def ssync(self, s, op):
        self.inc(op, "sw")
        self.wait(s, "sw")

    def guard(self, t, psname):
        g = self.bankguard.get(psname)
        if g is not None:
            self.wait(t, g[0], g[1])

    def setguard(self, psname, tag):
        self.bankguard[psname] = (tag, self.tok(tag))

    def ptv(self, c0, c1, parts=128):
        return self.ps["wide"][0:parts, c0 // 2:(c1 + 1) // 2].bitcast(BF16)[:, 0:c1 - c0] \
            if c0 % 2 else self.ps["wide"][0:parts, c0 // 2:(c1 + 1) // 2].bitcast(BF16)[:, 0:c1 - c0]

    def mm(self, t, out, lhsT, rhs, start, stop, lb=0, ob=0):
        if lb == 96 or ob == 96:
            return t.matmul(out, lhsT, rhs, start=start, stop=stop,
                            tile_position=(lb, ob))
        return t.matmul(out, lhsT, rhs, start=start, stop=stop)

    def V(self, off, parts, cols, dtype=F32):
        nb = cols * (4 if dtype == F32 else 2)
        assert off % 4 == 0 and off + nb <= ARENA, (off, nb)
        ap = self.sb["arena"][0:parts, off // 4:(off + ((nb + 3) // 4) * 4) // 4]
        if dtype != F32:
            ap = ap.bitcast(dtype)[:, 0:cols]
        return ap

    def declare(self):
        nc = self.nc

        def di(name, shape, dt=F32):
            self.dram[name] = nc.dram_tensor(name, list(shape), dt,
                                             kind="ExternalInput").ap()
        di("x_in", [D, T]); di("w_inproj", [D, DPROJP], BF16)
        di("conv_w", [128, NCB * 4]); di("conv_b", [128, NCB])
        di("w_exp", [E, D, D], BF16); di("b_exp", [E, D])
        di("w_rout", [D, E]); di("b_rout", [E, 1])
        di("w_oproj", [DIN, D], BF16)
        di("w_th1", [D, LORA]); di("w_th2", [LORA, H * K])
        di("thbase", [128, 8]); di("lamb", [128, 8]); di("eta", [128, 8])
        di("dt_bias", [H, 1]); di("Dcol", [128, NIB]); di("Drow", [4, 1024], BF16)
        di("norm_w", [128, NIB])
        di("identity", [128, 128], BF16); di("mask4", [128, 512], BF16)
        di("seldt", [32, 8 * 128], BF16)
        for nm in ["repA", "repB", "repC", "repD"]:
            di(nm, [128, 128], BF16)
        di("ones_col", [128, 1]); di("ones_row", [1, 128])
        di("onesb_row", [1, 128], BF16); di("onesb_col", [128, 1], BF16)
        di("i4", [128, 512], BF16)
        di("oh8", [8, 8 * 128], BF16)
        di("i8f", [8, 8]); di("ones8x128", [8, 128])
        di("sel2", [128, 2])

        self.dram["out_x"] = nc.dram_tensor("out_x", [D, T], F32,
                                            kind="ExternalOutput").ap()
        skind = "ExternalOutput" if self.dbg else "Internal"

        def sc(name, shape, dt):
            h = nc.dram_tensor(name, list(shape), dt, kind=skind)
            self.dram[name] = h.ap() if skind == "ExternalOutput" else h.ap()
        sc("sc_xr", [128, NDB * T], F32)
        sc("sc_xh", [128, NDB * T], BF16)
        sc("sc_z", [128, NIB * T], BF16)
        sc("sc_xbc", [128, NCB * T], BF16)
        sc("sc_y", [128, NIB * T], BF16)
        sc("sc_rg", [128, 8 * T], BF16)
        sc("sc_fg", [128, 8 * T], BF16)
        self.dram["cc1_in"] = nc.dram_tensor("cc1_in", [128, NCB * 3], F32).ap()
        self.dram["cc1_out"] = nc.dram_tensor("cc1_out", [256, NCB * 3], F32).ap()
        self.dram["cc2_in"] = nc.dram_tensor("cc2_in", [128, 1024], F32).ap()
        self.dram["cc2_out"] = nc.dram_tensor("cc2_out", [256, 1024], F32).ap()

        A = nc.alloc_sbuf_tensor
        S = self.sb
        S["arena"] = A("arena", [128, ARENA // 4], F32)
        S["consts"] = A("consts", [128, 8], F32)
        S["o_row"] = A("o_row", [1, 128], F32)
        S["ob_row"] = A("ob_row", [1, 128], BF16)
        S["cbf"] = A("cbf", [128, 1280], BF16)
        S["i4c"] = A("i4c", [128, 512], BF16)
        S["oh8c"] = A("oh8c", [8, 8 * 128], BF16)
        S["i8f"] = A("i8f_sb", [8, 8], F32)
        S["o8x"] = A("o8x_sb", [8, 128], F32)
        S["selc"] = A("selc", [32, 8 * 128], BF16)
        S["routw"] = A("routw", [128, 8 * 8], BF16)
        S["routwf"] = A("routwf", [128, 8 * 8], F32)
        S["wsb"] = A("wsb", [128, 8], F32)
        S["biasB"] = A("biasB", [128, 8], F32)
        S["routb"] = A("routb", [8, 1], F32)
        S["bexp"] = A("bexp", [8, D], BF16)
        S["dcolb"] = A("dcolb", [128, NIB], F32)
        S["nwcol"] = A("nwcol", [128, NIB], F32)
        S["lamn"] = A("lamn", [128, 8], F32)
        S["etan"] = A("etan", [128, 8], F32)
        S["thb"] = A("thb", [128, 8], F32)
        S["dtbias"] = A("dtbias", [H, 1], F32)
        S["sel2c"] = A("sel2c", [128, 2], F32)
        S["drow"] = A("drow", [4, 1024], BF16)
        S["convw"] = A("convw", [128, NCB * 4], F32)
        S["convb"] = A("convb", [128, NCB], F32)
        S["th1sb"] = A("th1sb", [128, 8 * LORA], BF16)
        S["th2sb"] = A("th2sb", [LORA, H * K], BF16)
        S["gates"] = A("gates", [E, T], BF16)
        S["grow"] = A("grow", [128, 96], F32)
        S["growb"] = A("growb", [128, 32], BF16)
        S["stat"] = A("stat", [1, 2048], F32)
        S["gm"] = A("gm", [128, 512], BF16)
        S["halo"] = A("halo", [128, NCB * 3], F32)
        S["state0"] = A("state0", [128, 1024], F32)
        S["state1"] = A("state1", [128, 1024], F32)
        S["statein"] = A("statein", [128, 1024], F32)
        S["stateb"] = A("stateb", [128, 1024], BF16)
        S["rcarry"] = A("rcarry", [128, 32], F32)
        S["tmpA"] = A("tmpA", [128, 512], F32)
        S["tmpB"] = A("tmpB", [128, 512], F32)
        S["dtb16"] = A("dtb16", [H, T], BF16)
        S["lorab"] = A("lorab", [LORA, T], BF16)
        S["bcall"] = A("bcall", [128, T], BF16)
        S["cols"] = A("cols", [128, 64], F32)
        S["etot"] = A("etot", [128, 32], F32)

        PA = nc.alloc_psum_tensor
        self.ps["st"] = PA("ps_st", [128, 1024], F32)
        self.ps["wide"] = PA("ps_wide", [128, 1024], F32)
        for nm in ["a", "b", "c", "d"]:
            self.ps[nm] = PA("ps_" + nm, [128, 512], F32)

    def preload(self, blk):
        sb, S, Dr = self, self.sb, self.dram

        @blk.sync
        def _(s):
            sb.inc(s.dma_start(
                out=self.V(OF["xres"], 128, NDB * T).rearrange("p (g t) -> p g t", g=NDB),
                in_=Dr["x_in"].rearrange("(g p) t -> p g t", p=128)), "ds", 16)
            for dst, src in [
                (S["consts"][:, 0:1], Dr["ones_col"]),
                (S["o_row"][:, :], Dr["ones_row"]),
                (S["ob_row"][:, :], Dr["onesb_row"]),
                (S["cbf"][:, 0:128], Dr["identity"]),
                (S["cbf"][:, 128:640], Dr["mask4"]),
                (S["cbf"][:, 640:768], Dr["repA"]),
                (S["cbf"][:, 768:896], Dr["repB"]),
                (S["cbf"][:, 896:1024], Dr["repC"]),
                (S["cbf"][:, 1024:1152], Dr["repD"]),
                (S["cbf"][:, 1152:1153], Dr["onesb_col"]),
                (S["i4c"][:, :], Dr["i4"]),
                (S["oh8c"][:, :], Dr["oh8"]),
                (S["i8f"][:, :], Dr["i8f"]),
                (S["o8x"][:, :], Dr["ones8x128"]),
                (S["selc"][:, :], Dr["seldt"]),
                (S["routb"][:, :], Dr["b_rout"]),
                (S["dcolb"][:, :], Dr["Dcol"]),
                (S["nwcol"][:, :], Dr["norm_w"]),
                (S["lamn"][:, :], Dr["lamb"]),
                (S["etan"][:, :], Dr["eta"]),
                (S["thb"][:, :], Dr["thbase"]),
                (S["dtbias"][:, :], Dr["dt_bias"]),
                (S["sel2c"][:, :], Dr["sel2"]),
                (S["drow"][:, :], Dr["Drow"]),
                (S["convw"][:, :], Dr["conv_w"]),
                (S["convb"][:, :], Dr["conv_b"]),
            ]:
                sb.inc(s.dma_start(out=dst, in_=src), "ds", 16)

        @blk.sync
        def _(s):
            sb.inc(s.dma_start(
                out=S["routwf"][:, :].rearrange("p (g e) -> p g e", g=NDB),
                in_=Dr["w_rout"].rearrange("(g p) e -> p g e", p=128)), "ds", 16)

        @blk.gpsimd
        def _(g):
            sb.inc(g.dma_start(
                out=S["routw"][:, :].rearrange("p (g e) -> p g e", g=NDB),
                in_=Dr["w_rout"].rearrange("(g p) e -> p g e", p=128)), "dg", 16)
            sb.inc(g.dma_start(out=S["bexp"][:, :], in_=Dr["b_exp"]), "dg", 16)
            sb.inc(g.dma_start(
                out=S["th1sb"][:, :].rearrange("p (g l) -> p g l", g=NDB),
                in_=Dr["w_th1"].rearrange("(g p) l -> p g l", p=128)), "dg", 16)
            sb.inc(g.dma_start(out=S["th2sb"][:, :], in_=Dr["w_th2"]), "dg", 16)
            sb.inc(g.dma_start(
                out=self.V(OF["wexp0"], 128, 8 * D, BF16).rearrange("p (g f) -> p g f", g=NDB),
                in_=Dr["w_exp"][0].rearrange("(g p) f -> p g f", p=128)), "dg0", 16)
            self.wldtok[0] = self.tok("dg0")

        @blk.vector
        def _(v):
            sb.wait(v, "ds")
            self.vsync(v, v.tensor_scalar_mul(S["lamn"][:, :], S["lamn"][:, :], -1.0))
            self.vsync(v, v.tensor_scalar_mul(S["etan"][:, :], S["etan"][:, :], -1.0))
            sb.inc(v.memset(S["consts"][:, 1:2], PIH), "vw")
            sb.inc(v.memset(S["consts"][:, 2:3], LN_EPS), "vw")
            sb.inc(v.memset(S["consts"][:, 3:4], RMS_EPS), "vw")
            sb.inc(v.memset(S["state0"][:, :], 0.0), "vw")
            sb.inc(v.memset(S["stateb"][:, :], 0.0), "vw")
            sb.inc(v.memset(S["halo"][:, :], 0.0), "vw")
            sb.inc(v.memset(S["rcarry"][:, :], 0.0), "vw")

        self._preload_router(blk)

    def _preload_router(self, blk):
        sb, S = self, self.sb
        PS = self.ps

        @blk.tensor
        def _(t):
            sb.wait(t, "ds")
            self.guard(t, "a")
            for bb in range(NDB):
                op = t.matmul(PS["a"][0:8, 0:1],
                              S["routwf"][:, bb * 8:(bb + 1) * 8],
                              S["consts"][:, 0:1],
                              start=(bb == 0), stop=(bb == NDB - 1))
            sb.inc(op, "pw")

        @blk.vector
        def _(v):
            sb.wait(v, "pw")
            self.vsync(v, v.tensor_scalar(S["tmpB"][0:8, 0:8], S["i8f"][:, :],
                                          PS["a"][0:8, 0:1], None, ALU.mult))
            self.vsync(v, v.tensor_scalar(S["tmpB"][0:8, 8:16], S["i8f"][:, :],
                                          S["routb"][:, 0:1], None, ALU.mult))
        self.setguard("a", "vw")

        @blk.tensor
        def _(t):
            sb.wait(t, "vw")
            self.guard(t, "c")
            t.matmul(PS["c"][:, 0:8], S["o8x"][:, :], S["tmpB"][0:8, 0:8],
                     start=True, stop=True)
            op = t.matmul(PS["c"][:, 8:16], S["o8x"][:, :], S["tmpB"][0:8, 8:16],
                          start=True, stop=True)
            sb.inc(op, "pw")

        @blk.vector
        def _(v):
            sb.wait(v, "pw")
            self.vsync(v, v.tensor_copy(S["wsb"][:, :], PS["c"][:, 0:8]))
            self.vsync(v, v.tensor_copy(S["biasB"][:, :], PS["c"][:, 8:16]))
        self.setguard("c", "vw")

    def ln_stats(self, blk, xoff, c0, width, nblk, sq_off, xcols=None):
        if xcols is None:
            xcols = T
        sb, S, PS = self, self.sb, self.ps

        @blk.scalar
        def _(s):
            sb.wait(s, "vw")
            for bb in range(nblk):
                op = s.activation(
                    self.V(sq_off, 128, nblk * width, BF16)[:, bb * width:(bb + 1) * width],
                    self.V(xoff, 128, nblk * xcols)[
                        :, bb * xcols + c0:bb * xcols + c0 + width],
                    AF.Square)
            sb.inc(op, "sw")

        @blk.tensor
        def _(t):
            sb.wait(t, "sw")
            self.guard(t, "a"); self.guard(t, "b")
            for bb in range(nblk):
                t.matmul(PS["a"][0:1, 0:width], S["consts"][:, 0:1],
                         self.V(xoff, 128, nblk * xcols)[
                             :, bb * xcols + c0:bb * xcols + c0 + width],
                         start=(bb == 0), stop=(bb == nblk - 1))
            for bb in range(nblk):
                op = t.matmul(PS["b"][0:1, 0:width], S["cbf"][:, 1152:1153],
                              self.V(sq_off, 128, nblk * width, BF16)[:, bb * width:(bb + 1) * width],
                              start=(bb == 0), stop=(bb == nblk - 1))
            sb.inc(op, "pw")

        @blk.vector
        def _(v):
            sb.wait(v, "pw")
            n = nblk * 128.0
            st = S["stat"]
            self.vsync(v, v.tensor_scalar_mul(st[:, 0:width], PS["a"][0:1, 0:width], 1.0 / n))
            self.vsync(v, v.tensor_scalar_mul(st[:, 1536:1536 + width],
                                              PS["b"][0:1, 0:width], 1.0 / n))
            self.vsync(v, v.tensor_mul(st[:, 1024:1024 + width],
                                       st[:, 0:width], st[:, 0:width]))
            self.vsync(v, v.tensor_sub(st[:, 1024:1024 + width],
                                       st[:, 1536:1536 + width], st[:, 1024:1024 + width]))
        self.setguard("a", "vw")
        self.setguard("b", "vw")

        @blk.scalar
        def _(s):
            sb.wait(s, "vw")
            st = S["stat"]
            self.ssync(s, s.activation(st[:, 512:512 + width], st[:, 1024:1024 + width],
                                       AF.Ln, bias=S["consts"][0:1, 2:3]))
            self.ssync(s, s.activation(st[:, 512:512 + width], st[:, 512:512 + width],
                                       AF.Exp, scale=-0.5))

    def ln_norm(self, blk, xoff, hatoff, c0, width, nblk, sq_off,
                xcols=None, hatcols=None):
        if xcols is None:
            xcols = T
        if hatcols is None:
            hatcols = T
        sb, S, PS = self, self.sb, self.ps

        @blk.tensor
        def _(t):
            sb.wait(t, "sw")
            self.guard(t, "c"); self.guard(t, "d")
            t.matmul(PS["c"][:, 0:width], S["o_row"][:, :], S["stat"][:, 0:width],
                     start=True, stop=True)
            op = t.matmul(PS["d"][:, 0:width], S["o_row"][:, :],
                          S["stat"][:, 512:512 + width], start=True, stop=True)
            sb.inc(op, "pw")

        @blk.vector
        def _(v):
            sb.wait(v, "pw")
            for bb in range(nblk):
                sb.inc(v.tensor_sub(
                    self.V(sq_off, 128, nblk * width, BF16)[:, bb * width:(bb + 1) * width],
                    self.V(xoff, 128, nblk * xcols)[
                        :, bb * xcols + c0:bb * xcols + c0 + width],
                    PS["c"][:, 0:width]), "vw")
            sb.wait(v, "vw")
            for bb in range(nblk):
                op = v.tensor_mul(
                    self.V(hatoff, 128, nblk * hatcols, BF16)[
                        :, bb * hatcols + c0:bb * hatcols + c0 + width],
                    self.V(sq_off, 128, nblk * width, BF16)[:, bb * width:(bb + 1) * width],
                    PS["d"][:, 0:width])
            sb.inc(op, "vw")
        self.setguard("c", "vw")
        self.setguard("d", "vw")

    def routing(self, blk, c0, width, gbase=0, xoff=None, xcols=None):
        sb, S, PS = self, self.sb, self.ps
        nq = width // 128
        if xoff is None:
            xoff = OF["xres"]
        if xcols is None:
            xcols = T

        @blk.tensor
        def _(t):
            sb.wait(t, "vw")
            sb.wait(t, "sw")
            self.guard(t, "a")
            for bb in range(NDB):
                op = t.matmul(PS["a"][0:8, 0:width],
                              S["routwf"][:, bb * 8:(bb + 1) * 8],
                              self.V(xoff, 128, NDB * xcols)[
                                  :, bb * xcols + c0:bb * xcols + c0 + width],
                              start=(bb == 0), stop=(bb == NDB - 1))
            sb.inc(op, "pw")

        @blk.vector
        def _(v):
            sb.wait(v, "pw")
            self.vsync(v, v.tensor_copy(S["tmpB"][0:8, 0:width], PS["a"][0:8, 0:width]))
        self.setguard("a", "vw")

        @blk.tensor
        def _(t):
            sb.wait(t, "vw")
            self.guard(t, "wide")
            for q in range(nq):
                t.transpose(PS["wide"][:, q * 8:(q + 1) * 8],
                            S["tmpB"][0:8, q * 128:(q + 1) * 128],
                            S["i8f"][:, :])
            for q in range(nq):
                t.transpose(PS["wide"][:, 32 + q:33 + q],
                            S["stat"][0:1, q * 128:(q + 1) * 128],
                            S["consts"][0:1, 0:1])
                op = t.transpose(PS["wide"][:, 36 + q:37 + q],
                                 S["stat"][0:1, 512 + q * 128:512 + (q + 1) * 128],
                                 S["consts"][0:1, 0:1])
            sb.inc(op, "pw")

        @blk.vector
        def _(v):
            sb.wait(v, "pw")
            gw = S["grow"]
            self.vsync(v, v.tensor_copy(gw[:, 0:40], PS["wide"][:, 0:40]))
            lg3 = gw[:, 0:nq * 8].rearrange("p (q e) -> p q e", q=nq)
            m4b = gw[:, 32:32 + nq].rearrange("p (q o) -> p q o", o=1).broadcast_to([128, nq, 8])
            r4b = gw[:, 36:36 + nq].rearrange("p (q o) -> p q o", o=1).broadcast_to([128, nq, 8])
            wsbb = S["wsb"][:, :].rearrange("p (o e) -> p o e", o=1).broadcast_to([128, nq, 8])
            bbb = S["biasB"][:, :].rearrange("p (o e) -> p o e", o=1).broadcast_to([128, nq, 8])
            t1 = gw[:, 40:40 + nq * 8].rearrange("p (q e) -> p q e", q=nq)
            self.vsync(v, v.tensor_tensor(t1, wsbb, m4b, ALU.mult))
            self.vsync(v, v.tensor_tensor(lg3, lg3, t1, ALU.subtract))
            self.vsync(v, v.tensor_tensor(lg3, lg3, r4b, ALU.mult))
            self.vsync(v, v.tensor_tensor(lg3, lg3, bbb, ALU.add))
            self.vsync(v, v.tensor_reduce(
                gw[:, 72:72 + nq].rearrange("p (q o) -> p q o", o=1),
                lg3, AX.X, ALU.max))
            mxb = gw[:, 72:72 + nq].rearrange("p (q o) -> p q o", o=1).broadcast_to([128, nq, 8])
            self.vsync(v, v.tensor_tensor(t1, lg3, mxb, ALU.is_equal))
            self.vsync(v, v.tensor_tensor(lg3, lg3, mxb, ALU.subtract))
        self.setguard("wide", "vw")

        @blk.scalar
        def _(s):
            sb.wait(s, "vw")
            self.ssync(s, s.activation(S["grow"][:, 0:nq * 8], S["grow"][:, 0:nq * 8],
                                       AF.Exp))

        @blk.vector
        def _(v):
            sb.wait(v, "sw")
            gw = S["grow"]
            lg3 = gw[:, 0:nq * 8].rearrange("p (q e) -> p q e", q=nq)
            t1 = gw[:, 40:40 + nq * 8].rearrange("p (q e) -> p q e", q=nq)
            self.vsync(v, v.tensor_reduce(
                gw[:, 76:76 + nq].rearrange("p (q o) -> p q o", o=1),
                lg3, AX.X, ALU.add))
            self.vsync(v, v.reciprocal(gw[:, 76:76 + nq], gw[:, 76:76 + nq]))
            wb = gw[:, 76:76 + nq].rearrange("p (q o) -> p q o", o=1).broadcast_to([128, nq, 8])
            self.vsync(v, v.tensor_tensor(t1, t1, wb, ALU.mult))
            self.vsync(v, v.tensor_scalar_mul(S["growb"][:, 0:nq * 8],
                                              gw[:, 40:40 + nq * 8], 0.5))

        @blk.tensor
        def _(t):
            sb.wait(t, "vw")
            self.guard(t, "wide")
            for q in range(nq):
                op = t.transpose(self.ptv(1024 + q * 128, 1024 + (q + 1) * 128, parts=8),
                                 S["growb"][:, q * 8:(q + 1) * 8],
                                 S["cbf"][:, 0:128])
            sb.inc(op, "pw")

        @blk.vector
        def _(v):
            sb.wait(v, "pw")
            sb.inc(v.tensor_copy(S["gates"][0:8, gbase + c0:gbase + c0 + width],
                                 self.ptv(1024, 1024 + width, parts=8)), "vw")
        self.setguard("wide", "vw")

    def ffn_experts(self, blk, which, hatoff, xroff, c0, width, gbase=0,
                    xr_cols=None, maskoff=None, first_sub=False,
                    hatcols=None):
        if hatcols is None:
            hatcols = T
        sb, S, PS = self, self.sb, self.ps
        Dr = self.dram
        if xr_cols is None:
            xr_cols = T
        if maskoff is None:
            maskoff = OF["poolb"]
        GBALL = 139264

        def bankap(do):
            if width == 512:
                if do < 4:
                    return PS[["a", "b", "c", "d"][do]][:, 0:512]
                if do < 6:
                    return PS["st"][:, (do - 4) * 512:(do - 3) * 512]
                return PS["wide"][:, (do - 6) * 512:(do - 5) * 512]
            if do < 4:
                return PS[["a", "b", "c", "d"][do]][:, 0:width]
            if do < 6:
                return PS["st"][:, (do - 4) * 512:(do - 4) * 512 + width]
            return PS["wide"][:, (do - 6) * 512:(do - 6) * 512 + width]

        npair = 1024 // width
        for eh in range(E // npair):

            @blk.tensor
            def _(t, eh=eh):
                sb.wait(t, "vw")
                self.guard(t, "wide")
                for i in range(npair):
                    e = eh * npair + i
                    op = t.matmul(PS["wide"][:, i * width:(i + 1) * width],
                                  S["oh8c"][:, e * 128:(e + 1) * 128],
                                  S["gates"][0:8, gbase + c0:gbase + c0 + width],
                                  start=True, stop=True)
                sb.inc(op, "pw")
            tk = self.tok("pw")

            @blk.vector
            def _(v, eh=eh, tk=tk):
                sb.wait(v, "pw", tk)
                op = v.tensor_copy(
                    self.V(GBALL, 128, 8 * width, BF16)[
                        :, eh * npair * width:(eh + 1) * npair * width],
                    PS["wide"][:, 0:npair * width])
                sb.inc(op, "vw")
            self.setguard("wide", "vw")

        for e in range(E):
            woff = OF["wexp0"] if e % 2 == 0 else OF["wexp1"]
            nxtoff = OF["wexp1"] if e % 2 == 0 else OF["wexp0"]
            src_e = (e + 1) % E

            @blk.gpsimd
            def _(g, e=e, src_e=src_e, nxtoff=nxtoff):
                sb.wait(g, "pw", self.wtok.get(("wexp", (e + 1) % 2), 0))
                sem = "dg0" if src_e % 2 == 0 else "dg1"
                sb.wait(g, sem)
                nxt = self.V(nxtoff, 128, 8 * D, BF16).rearrange("p (g f) -> p g f", g=NDB)
                op = g.dma_start(out=nxt, in_=Dr["w_exp"][src_e].rearrange(
                    "(g p) f -> p g f", p=128))
                sb.inc(op, sem, 16)
                self.wldtok[src_e] = self.tok(sem)

            vtok = [0, 0]
            for hf in range(2):
                @blk.vector
                def _(v, e=e, hf=hf):
                    sb.wait(v, "vw")
                    if e > 0:
                        sb.wait(v, "pw", self.masktok[hf])
                    hat = self.V(hatoff, 128, NDB * hatcols, BF16)
                    mask = self.V(maskoff, 128, NDB * width, BF16)
                    gb = self.V(GBALL, 128, 8 * width, BF16)[:, e * width:(e + 1) * width]
                    for bb in range(4 * hf, 4 * hf + 4):
                        op = v.tensor_mul(mask[:, bb * width:(bb + 1) * width],
                                          hat[:, bb * hatcols + c0:bb * hatcols + c0 + width],
                                          gb)
                    sb.inc(op, "vw")
                vtok[hf] = self.tok("vw")

            self.masktok = [0, 0]
            for hf in range(2):
                @blk.tensor
                def _(t, e=e, woff=woff, hf=hf, vtok=vtok):
                    sb.wait(t, "vw", vtok[hf])
                    if hf == 0:
                        sb.wait(t, "dg0" if e % 2 == 0 else "dg1", self.wldtok[e])
                    if e == 0 and hf == 0:
                        sb.wait(t, "dg")
                        for nm in ["a", "b", "c", "d", "st", "wide"]:
                            self.guard(t, nm)
                        for do in range(NDB):
                            t.matmul(bankap(do), S["bexp"][:, do * 128:(do + 1) * 128],
                                     S["gates"][0:8, gbase + c0:gbase + c0 + width],
                                     start=True, stop=False)
                    wbuf = self.V(woff, 128, 8 * D, BF16)
                    mask = self.V(maskoff, 128, NDB * width, BF16)
                    for di2 in range(4 * hf, 4 * hf + 4):
                        for do in range(NDB):
                            op = t.matmul(bankap(do),
                                          wbuf[:, di2 * D + do * 128:di2 * D + (do + 1) * 128],
                                          mask[:, di2 * width:(di2 + 1) * width],
                                          start=False,
                                          stop=(e == E - 1 and di2 == NDB - 1))
                    sb.inc(op, "pw")
                    self.masktok[hf] = self.tok("pw")
                if hf == 1:
                    self.wtok[("wexp", e % 2)] = self.tok("pw")

        @blk.vector
        def _(v):
            sb.wait(v, "pw")
            xr = self.V(xroff, 128, NDB * xr_cols)
            for do in range(NDB):
                op = v.tensor_add(xr[:, do * xr_cols + c0:do * xr_cols + c0 + width],
                                  xr[:, do * xr_cols + c0:do * xr_cols + c0 + width],
                                  bankap(do))
            sb.inc(op, "vw")
        for nm in ["a", "b", "c", "d", "st", "wide"]:
            self.setguard(nm, "vw")

    def ffn1(self, blk):
        for j in range(NSUB):
            c0 = j * SUB
            self.ln_stats(blk, OF["xres"], c0, SUB, NDB, OF["poolb"])
            self.ln_norm(blk, OF["xres"], OF["xhat"], c0, SUB, NDB, OF["poolb"])
            self.routing(blk, c0, SUB)
        for j in range(NSUB):
            self.ffn_experts(blk, 1, OF["xhat"], OF["xres"], j * SUB, SUB,
                             first_sub=(j == 0))

    def inproj(self, blk):
        sb, S, PS, Dr = self, self.sb, self.ps, self.dram

        for j in range(NSUB):
            c0 = j * SUB
            self.ln_stats(blk, OF["xres"], c0, SUB, NDB, OF["poolb"])
            self.ln_norm(blk, OF["xres"], OF["xhat"], c0, SUB, NDB, OF["poolb"])

        @blk.sync
        def _(s):
            sb.wait(s, "vw")
            sb.inc(s.dma_start(
                out=Dr["sc_xr"].rearrange("p (g t) -> p g t", g=NDB),
                in_=self.V(OF["xres"], 128, NDB * T).rearrange("p (g t) -> p g t", g=NDB)),
                "ds", 16)

        for j in range(NSUB):
            @blk.tensor
            def _(t, j=j):
                sb.wait(t, "vw")
                sb.wait(t, "dg")
                self.guard(t, "c")
                for bb in range(NDB):
                    op = t.matmul(PS["c"][0:LORA, 0:SUB],
                                  S["th1sb"][:, bb * LORA:(bb + 1) * LORA],
                                  self.V(OF["xhat"], 128, NDB * T, BF16)[
                                      :, bb * T + j * SUB:bb * T + (j + 1) * SUB],
                                  start=(bb == 0), stop=(bb == NDB - 1))
                sb.inc(op, "pw")
            tk = self.tok("pw")

            @blk.scalar
            def _(s, j=j, tk=tk):
                sb.wait(s, "pw", tk)
                sb.inc(s.activation(S["lorab"][:, j * SUB:(j + 1) * SUB],
                                    PS["c"][0:LORA, 0:SUB], AF.Tanh), "sw")
            self.setguard("c", "sw")

        for do in range(NPB):
            woff = OF["wip0"] if do % 2 == 0 else OF["wip1"]

            @blk.gpsimd
            def _(g, do=do, woff=woff):
                sb.wait(g, "pw", self.wtok.get(("wip", do % 2), 0))
                sem = "dg0" if do % 2 == 0 else "dg1"
                sb.wait(g, sem)
                op = g.dma_start(
                    out=self.V(woff, 128, 8 * 128, BF16).rearrange("p (g f) -> p g f", g=NDB),
                    in_=Dr["w_inproj"][:, do * 128:(do + 1) * 128].rearrange(
                        "(g p) f -> p g f", p=128))
                sb.inc(op, sem, 16)
                self.wldtok[("wip", do)] = self.tok(sem)

            for j in range(NSUB):
                bank = ["a", "b"][(do * NSUB + j) % 2]

                @blk.tensor
                def _(t, do=do, j=j, bank=bank, woff=woff):
                    sb.wait(t, "dg0" if do % 2 == 0 else "dg1",
                            self.wldtok[("wip", do)])
                    self.guard(t, bank)
                    if do == 0 and j == 0:
                        sb.wait(t, "vw")
                        sb.wait(t, "sw")
                    w = self.V(woff, 128, 8 * 128, BF16)
                    for bb in range(NDB):
                        op = t.matmul(PS[bank][:, 0:SUB],
                                      w[:, bb * 128:(bb + 1) * 128],
                                      self.V(OF["xhat"], 128, NDB * T, BF16)[
                                          :, bb * T + j * SUB:bb * T + (j + 1) * SUB],
                                      start=(bb == 0), stop=(bb == NDB - 1))
                    sb.inc(op, "pw")
                    if j == NSUB - 1:
                        self.wtok[("wip", do % 2)] = self.tok("pw")
                tk = self.tok("pw")
                stg = (do * NSUB + j) % 4
                stgoff = OF["stg"] + stg * 1024

                if do < NIB:
                    @blk.scalar
                    def _(s, do=do, j=j, bank=bank, tk=tk, stg=stg, stgoff=stgoff):
                        sb.wait(s, "pw", tk)
                        sb.wait(s, "ds")
                        op = s.activation(self.V(stgoff, 128, SUB, BF16),
                                          PS[bank][:, 0:SUB], AF.Silu)
                        sb.inc(op, "sw")
                    self.setguard(bank, "sw")

                    @blk.sync
                    def _(s, do=do, j=j, stg=stg, stgoff=stgoff):
                        sb.wait(s, "sw")
                        op = s.dma_start(
                            out=Dr["sc_z"][:, do * T + j * SUB:do * T + (j + 1) * SUB],
                            in_=self.V(stgoff, 128, SUB, BF16))
                        sb.inc(op, "ds", 16)
                elif do < NIB + NCB:
                    xb = do - NIB

                    @blk.vector
                    def _(v, do=do, j=j, bank=bank, tk=tk, stg=stg, stgoff=stgoff, xb=xb):
                        sb.wait(v, "pw", tk)
                        sb.wait(v, "ds")
                        op = v.tensor_copy(self.V(stgoff, 128, SUB, BF16),
                                           PS[bank][:, 0:SUB])
                        sb.inc(op, "vw")
                        if j == NSUB - 1:
                            op = v.tensor_copy(S["halo"][:, xb * 3:xb * 3 + 3],
                                               PS[bank][:, SUB - 3:SUB])
                            sb.inc(op, "vw")
                    self.setguard(bank, "vw")

                    @blk.sync
                    def _(s, xb=xb, j=j, stg=stg, stgoff=stgoff):
                        sb.wait(s, "vw")
                        op = s.dma_start(
                            out=Dr["sc_xbc"][:, xb * T + j * SUB:xb * T + (j + 1) * SUB],
                            in_=self.V(stgoff, 128, SUB, BF16))
                        sb.inc(op, "ds", 16)
                else:
                    @blk.vector
                    def _(v, j=j, bank=bank, tk=tk):
                        sb.wait(v, "pw", tk)
                        sb.wait(v, "vw")
                        sb.inc(v.tensor_copy(S["tmpA"][0:H, 0:SUB],
                                             PS[bank][0:H, 0:SUB]), "vw")
                    self.setguard(bank, "vw")

                    @blk.scalar
                    def _(s, j=j):
                        sb.wait(s, "vw")
                        tb = S["tmpB"]
                        self.ssync(s, s.activation(tb[0:H, 0:SUB], S["tmpA"][0:H, 0:SUB],
                                                   AF.Abs, bias=S["dtbias"][:, 0:1]))
                        self.ssync(s, s.activation(tb[0:H, 0:SUB], tb[0:H, 0:SUB],
                                                   AF.Exp, scale=-1.0))
                        self.ssync(s, s.activation(tb[0:H, 0:SUB], tb[0:H, 0:SUB],
                                                   AF.Ln, bias=S["consts"][0:H, 0:1]))
                        self.ssync(s, s.activation(S["tmpA"][0:H, 0:SUB],
                                                   S["tmpA"][0:H, 0:SUB],
                                                   AF.Relu, bias=S["dtbias"][:, 0:1]))

                    @blk.vector
                    def _(v, j=j):
                        sb.wait(v, "sw")
                        sb.inc(v.tensor_add(S["dtb16"][:, j * SUB:(j + 1) * SUB],
                                            S["tmpA"][0:H, 0:SUB],
                                            S["tmpB"][0:H, 0:SUB]), "vw")

        @blk.gpsimd
        def _(g):
            sb.wait(g, "vw")
            sb.inc(g.dma_start(out=Dr["cc1_in"], in_=S["halo"][:, :]), "dg", 16)
            sb.wait(g, "dg")
            op = g.collective_compute(
                "AllGather", ALU.bypass,
                replica_groups=[[0, 1], [2, 3], [4, 5], [6, 7]],
                ins=[Dr["cc1_in"].opt()],
                outs=[Dr["cc1_out"].opt()])
            sb.inc(op, "cc")
            sb.wait(g, "cc")
            sb.inc(g.dma_start(out=S["tmpA"][:, 0:NCB * 3],
                               in_=Dr["cc1_out"][0:128, :]), "dg", 16)
            sb.inc(g.dma_start(out=S["tmpB"][:, 0:NCB * 3],
                               in_=Dr["cc1_out"][128:256, :]), "dg", 16)

        @blk.vector
        def _(v):
            sb.wait(v, "dg")
            self.vsync(v, v.tensor_scalar(S["halo"][:, :], S["tmpA"][:, 0:NCB * 3],
                                          S["sel2c"][:, 0:1], None, ALU.mult))
            self.vsync(v, v.scalar_tensor_tensor(
                S["halo"][:, :], S["tmpB"][:, 0:NCB * 3],
                S["sel2c"][:, 1:2], S["halo"][:, :], ALU.mult, ALU.add))

        self.mark("lorab", "sw")
        self.mark("dtb16", "vw")
        self.mark("xbcsp", "ds")

    def passA_prep(self, blk):
        sb, S, PS = self, self.sb, self.ps
        dis = self.V(DIS_OFF, 128, 8 * 512, BF16)

        @blk.vector
        def _(v):
            sb.wait(v, "vw")
            for k in ["qr", "qr1", "qr2", "qr3", "qin", "qin1", "qin2", "qin3"]:
                sb.inc(v.memset(self.V(OA[k], 128, 8 * CH, BF16)[:, :], 0.0), "vw")

        for grp8 in range(8):
            s4, half = grp8 % 4, grp8 // 4
            bank = ["a", "b"][grp8 % 2]

            @blk.tensor
            def _(t, s4=s4, half=half, bank=bank):
                sb.wait(t, "vw")
                self.guard(t, bank)
                op = t.matmul(PS[bank][:, 0:512],
                              S["oh8c"][0:4, s4 * 128:(s4 + 1) * 128],
                              S["drow"][0:4, half * 512:(half + 1) * 512],
                              start=True, stop=True)
                sb.inc(op, "pw")
            tk = self.tok("pw")

            @blk.vector
            def _(v, grp8=grp8, bank=bank, tk=tk, dis=dis):
                sb.wait(v, "pw", tk)
                op = v.tensor_tensor(dis[:, grp8 * 512:(grp8 + 1) * 512],
                                     S["i4c"][:, :], PS[bank][:, 0:512], ALU.mult)
                sb.inc(op, "vw")
            self.setguard(bank, "vw")

    def mark(self, name, sem):
        self.tk2[name] = self.cnt.get(sem, 0)

    def wtk(self, eng, sem, name):
        self.sem(sem)
        eng.wait_ge(self.sems[sem], self.tk2.get(name, 0))

    def chvars(self, cidx):
        alt = cidx % 2
        d = {}
        d["prc"] = self.V(OA["prc"] if alt == 0 else PRC2, 128, 8 * CH, BF16)
        d["pic"] = self.V(OA["pic"] if alt == 0 else PIC2, 128, 8 * CH, BF16)
        d["cer"] = self.V(OA["cer"] if alt == 0 else CER2, 128, 8 * CH, BF16)
        d["cei"] = self.V(OA["cei"] if alt == 0 else CEI2, 128, 8 * CH, BF16)
        d["qrv"] = [self.V(OA[k2], 128, 8 * CH, BF16)
                    for k2 in ["qr", "qr1", "qr2", "qr3"]]
        d["qinv"] = [self.V(OA[k2], 128, 8 * CH, BF16)
                     for k2 in ["qin", "qin1", "qin2", "qin3"]]
        d["bvr"] = self.V(OA["bvr"], 128, 8 * CH, BF16)
        d["bvn"] = self.V(OA["bvn"], 128, 8 * CH, BF16)
        d["bvrT"] = self.V(OA["bvrT"], 128, 8 * CH, BF16)
        d["bvnT"] = self.V(OA["bvnT"], 128, 8 * CH, BF16)
        d["coss"] = self.V(OA["coss"], 128, 8 * CH, BF16)
        d["sins"] = self.V(OA["sins"], 128, 8 * CH, BF16)
        d["pmag"] = self.V(OA["pmag"], 128, 8 * CH, BF16)
        d["qmag"] = self.V(OA["qmag"], 128, 8 * CH, BF16)
        d["pex"] = self.V(OA["pex"], 128, 8 * CH)
        d["qex"] = self.V(OA["qex"], 128, 8 * CH)
        d["rl"] = self.V(OA["rl"], 128, 8 * CH)
        d["flc"] = self.V(OA["flc"], 128, 8 * CH)
        d["crep"] = self.V(OA["crep"], 128, 512, BF16)
        d["xd"] = self.V(OA["xd"] if alt == 0 else XD2, 128, NIB * CH, BF16)
        d["xss"] = self.V(OA["xss"], 128, NIB * MAC, BF16)
        d["ytile"] = self.V(OA["ytile"], 128, NIB * MAC, BF16)
        d["dtB"] = self.V(OA["dtB"], 128, 8 * MAC, BF16)
        d["wcrB"] = self.V(OA["lzr"], 128, 8 * CH, BF16)
        d["wciB"] = self.V(OA["lzr"] + 2048, 128, 8 * CH, BF16)
        return d

    def _xbct_v(self):
        return self.V(OA["xbct"], 128, NCB * 260, BF16).rearrange(
            "p (c w) -> p c w", c=NCB)

    def pA_xbct(self, blk, m):
        sb, Dr = self, self.dram
        c0 = m * MAC
        xbct = self._xbct_v

        @blk.gpsimd
        def _(g):
            self.wtk(g, "vw", ("conv", m - 1))
            self.wtk(g, "ds", "xbcsp")
            op = g.dma_start(
                out=xbct()[:, :, 3:3 + MAC],
                in_=Dr["sc_xbc"].rearrange("p (c t) -> p c t", c=NCB)[:, :, c0:c0 + MAC])
            sb.inc(op, "dgx", 16)
        self.mark(("xbct", m), "dgx")

    def pA_conv(self, blk, m):
        sb, S, PS, Dr = self, self.sb, self.ps, self.dram
        c0 = m * MAC
        xbct = self._xbct_v

        @blk.vector
        def _(v):
            sb.wait(v, "vw")
            sb.wait(v, "dgx", self.tk2[("xbct", m)])
            if m == 0:
                self.vsync(v, v.tensor_copy(
                    xbct()[:, :, 0:3],
                    S["halo"][:, :].rearrange("p (c w) -> p c w", c=NCB)))
            xb = xbct()
            for cb in range(NCB):
                acc = (S["tmpA"][:, 256:256 + MAC] if cb % 2 == 0
                       else S["tmpB"][:, 0:MAC])
                if cb >= 2:
                    self.wtk(v, "sw", ("silu", m, cb - 2))
                sb.inc(v.tensor_scalar(acc, xb[:, cb, 0:MAC],
                                       S["convw"][:, cb * 4:cb * 4 + 1],
                                       S["convb"][:, cb:cb + 1], ALU.mult, ALU.add), "vw")
                sb.wait(v, "vw")
                for k2 in range(1, 4):
                    sb.inc(v.scalar_tensor_tensor(
                        acc, xb[:, cb, k2:k2 + MAC],
                        S["convw"][:, cb * 4 + k2:cb * 4 + k2 + 1],
                        acc, ALU.mult, ALU.add), "vw")
                    sb.wait(v, "vw")
                tk = self.tok("vw")

                @blk.scalar
                def _(s, cb=cb, tk=tk, acc=acc):
                    sb.wait(s, "vw", tk)
                    if cb < NIB:
                        self.wtk(s, "pw", ("xdT", 2 * m - 1))
                        dst = self.V(OA["xss"], 128, NIB * MAC, BF16)[
                            :, cb * MAC:(cb + 1) * MAC]
                    else:
                        self.wtk(s, "pw", ("crepF", 2 * m - 1))
                        dst = S["bcall"][:, c0:c0 + MAC]
                    sb.inc(s.activation(dst, acc, AF.Silu), "sw")
                    self.mark(("silu", m, cb), "sw")
            op = v.tensor_copy(xb[:, :, 0:3], xb[:, :, MAC:MAC + 3])
            sb.inc(op, "vw")
        self.mark(("conv", m), "vw")
        self.mark(("convS", m), "sw")

    def pA_theta_scans(self, blk, m):
        sb, S, PS, Dr = self, self.sb, self.ps, self.dram
        c0 = m * MAC

        for half in range(4):
            @blk.tensor
            def _(t, half=half):
                self.wtk(t, "sw", "lorab")
                self.guard(t, "wideA")
                for q in range(4):
                    idx = half * 4 + q
                    gb, jj = idx // 2, idx % 2
                    op = t.matmul(PS["wide"][:, q * 128:(q + 1) * 128],
                                  S["th2sb"][:, gb * 128:(gb + 1) * 128],
                                  S["lorab"][:, c0 + jj * 128:c0 + (jj + 1) * 128],
                                  start=True, stop=True)
                sb.inc(op, "pw")
            tk = self.tok("pw")

            @blk.vector
            def _(v, half=half, tk=tk):
                sb.wait(v, "pw", tk)
                sb.wait(v, "vw")
                th = self.V(OA["th"], 128, 8 * MAC, BF16)
                for q in range(4):
                    idx = half * 4 + q
                    gb, jj = idx // 2, idx % 2
                    sb.inc(v.tensor_scalar(
                        th[:, gb * MAC + jj * 128:gb * MAC + (jj + 1) * 128],
                        PS["wide"][:, q * 128:(q + 1) * 128],
                        S["thb"][:, gb:gb + 1], PIH, ALU.add, ALU.min), "vw")
                sb.wait(v, "vw")
                for q in range(4):
                    idx = half * 4 + q
                    gb, jj = idx // 2, idx % 2
                    op = v.tensor_scalar_max(
                        th[:, gb * MAC + jj * 128:gb * MAC + (jj + 1) * 128],
                        th[:, gb * MAC + jj * 128:gb * MAC + (jj + 1) * 128], -PIH)
                sb.inc(op, "vw")
            self.setguard("wideA", "vw")
        self.mark(("thclip", m), "vw")

        for half in range(4):
            @blk.tensor
            def _(t, half=half):
                self.wtk(t, "vw", "dtb16")
                self.guard(t, "wideA")
                for q in range(2):
                    gb = half * 2 + q
                    op = t.matmul(PS["wide"][:, q * MAC:(q + 1) * MAC],
                                  S["selc"][:, gb * 128:(gb + 1) * 128],
                                  S["dtb16"][:, c0:c0 + MAC],
                                  start=True, stop=True)
                sb.inc(op, "pw")
            tk = self.tok("pw")

            @blk.scalar
            def _(s, half=half, tk=tk):
                sb.wait(s, "pw", tk)
                self.wtk(s, "vw", ("bldA", 2 * m - 1))
                dtB = self.V(OA["dtB"], 128, 8 * MAC, BF16)
                op = s.activation(dtB[:, half * 2 * MAC:(half + 1) * 2 * MAC],
                                  PS["wide"][:, 0:2 * MAC], AF.Copy)
                sb.inc(op, "sw")
            self.setguard("wideA", "sw")
        self.mark(("dtBS", m), "sw")

        @blk.scalar
        def _(s):
            self.wtk(s, "vw", ("thclip", m))
            self.wtk(s, "dgs", ("spill", m - 1))
            th = self.V(OA["th"], 128, 8 * MAC, BF16)
            t2 = self.V(OA["Rg"], 128, 8 * MAC)
            sb.inc(s.activation(t2[:, :], th[:, :], AF.Square), "sw")
        self.mark(("th2", m), "sw")

        @blk.vector
        def _(v):
            self.wtk(v, "sw", ("th2", m))
            self.wtk(v, "sw", ("dtBS", m))
            self.wtk(v, "sw", ("exp", 2 * m - 1))
            self.wtk(v, "dgs", ("spill", m - 1))
            sb.wait(v, "vw")
            t2 = self.V(OA["Rg"], 128, 8 * MAC)
            lzr = self.V(OA["lzr"], 128, 8 * MAC, BF16)
            dtB = self.V(OA["dtB"], 128, 8 * MAC, BF16)
            th = self.V(OA["th"], 128, 8 * MAC, BF16)
            for gb in range(8):
                sb.inc(v.tensor_scalar(lzr[:, gb * MAC:(gb + 1) * MAC],
                                       t2[:, gb * MAC:(gb + 1) * MAC],
                                       S["etan"][:, gb:gb + 1], S["lamn"][:, gb:gb + 1],
                                       ALU.mult, ALU.add), "vw")
            sb.wait(v, "vw")
            self.vsync(v, v.tensor_mul(lzr[:, :], lzr[:, :], dtB[:, :]))
            self.vsync(v, v.tensor_mul(th[:, :], th[:, :], dtB[:, :]))
            Rg = self.V(OA["Rg"], 128, 8 * MAC)
            Fg = self.V(OA["Fg"], 128, 8 * MAC)
            zed = S["tmpA"]
            sb.inc(v.memset(zed[:, 0:MAC], 0.0), "vw")
            sb.wait(v, "vw")
            for gb in range(8):
                sb.inc(v.tensor_tensor_scan(Rg[:, gb * MAC:(gb + 1) * MAC],
                                            lzr[:, gb * MAC:(gb + 1) * MAC],
                                            zed[:, 0:MAC],
                                            S["rcarry"][:, gb:gb + 1] if m > 0 else 0.0,
                                            ALU.add, ALU.add), "vw")
                sb.inc(v.tensor_tensor_scan(Fg[:, gb * MAC:(gb + 1) * MAC],
                                            th[:, gb * MAC:(gb + 1) * MAC],
                                            zed[:, 0:MAC],
                                            S["rcarry"][:, 8 + gb:9 + gb] if m > 0 else 0.0,
                                            ALU.add, ALU.add), "vw")
            sb.wait(v, "vw")
            self.vsync(v, v.tensor_copy(S["rcarry"][:, 0:8],
                          Rg.rearrange("p (g w) -> p g w", g=8)[:, :, MAC - 1]))
            self.vsync(v, v.tensor_copy(S["rcarry"][:, 8:16],
                               Fg.rearrange("p (g w) -> p g w", g=8)[:, :, MAC - 1]))
        rgtok = self.tok("vw")

        @blk.gpsimd
        def _(g):
            sb.wait(g, "vw", rgtok)
            Rg = self.V(OA["Rg"], 128, 8 * MAC)
            op = g.dma_start(
                out=Dr["sc_rg"].rearrange("p (g t) -> p g t", g=8)[:, :, c0:c0 + MAC],
                in_=Rg.rearrange("p (g w) -> p g w", g=8))
            sb.inc(op, "dgs", 16)

        @blk.vector
        def _(v):
            sb.wait(v, "vw")
            Fg = self.V(OA["Fg"], 128, 8 * MAC)
            tmp = self.V(OA["lzr"], 128, 4 * MAC)
            HW = 4 * MAC
            for hh in range(2):
                sl = slice(hh * HW, (hh + 1) * HW)
                self.vsync(v, v.tensor_scalar(tmp[:, 0:HW], Fg[:, sl],
                                              1.0 / (2 * np.pi), MAGIC,
                                              ALU.mult, ALU.add))
                self.vsync(v, v.tensor_scalar(tmp[:, 0:HW], tmp[:, 0:HW],
                                              MAGIC, None, ALU.subtract))
                self.vsync(v, v.scalar_tensor_tensor(Fg[:, sl], tmp[:, 0:HW],
                                                     -2 * np.pi, Fg[:, sl],
                                                     ALU.mult, ALU.add))
        fgtok = self.tok("vw")

        @blk.gpsimd
        def _(g):
            sb.wait(g, "vw", fgtok)
            Fg = self.V(OA["Fg"], 128, 8 * MAC)
            op = g.dma_start(
                out=Dr["sc_fg"].rearrange("p (g t) -> p g t", g=8)[:, :, c0:c0 + MAC],
                in_=Fg.rearrange("p (g w) -> p g w", g=8))
            sb.inc(op, "dgs", 16)
        self.mark(("spill", m), "dgs")

    def pA_buildA(self, blk, cidx):
        sb, S, PS, Dr = self, self.sb, self.ps, self.dram
        m, pc = cidx // CPM, cidx % CPM
        w0 = pc * CH
        cv = self.chvars(cidx)
        Rg = self.V(OA["Rg"], 128, 8 * MAC)
        Fg = self.V(OA["Fg"], 128, 8 * MAC)
        rl, flc = cv["rl"], cv["flc"]
        coss, sins = cv["coss"], cv["sins"]
        pmag, qmag = cv["pmag"], cv["qmag"]
        pex, qex = cv["pex"], cv["qex"]
        prc, pic, cer, cei = cv["prc"], cv["pic"], cv["cer"], cv["cei"]
        bvr, bvn, crep = cv["bvr"], cv["bvn"], cv["crep"]
        qrv, qinv, dtB = cv["qrv"], cv["qinv"], cv["dtB"]
        cl = S["cols"]
        co = 32 * pc

        @blk.vector
        def _(v):
            sb.wait(v, "vw")
            self.wtk(v, "sw", ("exp", cidx - 1))
            r3m = Rg.rearrange("p (g w) -> p g w", g=8)
            f3m = Fg.rearrange("p (g w) -> p g w", g=8)
            for gb in range(8):
                if cidx == 0:
                    sb.inc(v.tensor_copy(rl[:, gb * CH:(gb + 1) * CH],
                                         Rg[:, gb * MAC + w0:gb * MAC + w0 + CH]), "vw")
                    sb.inc(v.tensor_copy(flc[:, gb * CH:(gb + 1) * CH],
                                         Fg[:, gb * MAC + w0:gb * MAC + w0 + CH]), "vw")
                else:
                    roff = (S["rcarry"][:, 16 + gb:17 + gb] if pc == 0
                            else r3m[:, gb, w0 - 1:w0])
                    foff = (S["rcarry"][:, 24 + gb:25 + gb] if pc == 0
                            else f3m[:, gb, w0 - 1:w0])
                    sb.inc(v.tensor_scalar(rl[:, gb * CH:(gb + 1) * CH],
                                           Rg[:, gb * MAC + w0:gb * MAC + w0 + CH],
                                           roff, None, ALU.subtract), "vw")
                    sb.inc(v.tensor_scalar(flc[:, gb * CH:(gb + 1) * CH],
                                           Fg[:, gb * MAC + w0:gb * MAC + w0 + CH],
                                           foff, None, ALU.subtract), "vw")
            self.mark(("Aflc", cidx), "vw")
            sb.wait(v, "vw")
            if pc == CPM - 1:
                self.vsync(v, v.tensor_copy(S["rcarry"][:, 16:24], r3m[:, :, MAC - 1]))
                self.vsync(v, v.tensor_copy(S["rcarry"][:, 24:32], f3m[:, :, MAC - 1]))
            r3 = rl.rearrange("p (g w) -> p g w", g=8)
            tA = S["tmpA"]
            self.vsync(v, v.memset(tA[:, 32:40], 0.0))
            self.vsync(v, v.tensor_copy(
                tA[:, 40:64].rearrange("p (i g) -> p g i", i=3),
                r3[:, :, 31:CH - 1:32]))
            self.vsync(v, v.tensor_tensor(
                tA[:, 64:88].rearrange("p (i g) -> p g i", i=3),
                tA[:, 40:64].rearrange("p (i g) -> p g i", i=3),
                tA[:, 32:56].rearrange("p (i g) -> p g i", i=3), ALU.subtract))
            p3 = pex.rearrange("p (g w) -> p g w", g=8)
            for i in range(4):
                sb.inc(v.tensor_tensor(
                    p3[:, :, 32 * i:32 * i + 32], r3[:, :, 32 * i:32 * i + 32],
                    tA[:, 32 + 8 * i:40 + 8 * i].rearrange(
                        "p (g o) -> p g o", o=1).broadcast_to([128, 8, 32]),
                    ALU.subtract), "vw")
            sb.wait(v, "vw")
            op = v.tensor_scalar(qex[:, :], pex[:, :], -1.0, 87.0,
                                 ALU.mult, ALU.min)
            sb.inc(op, "vw")
        self.mark(("Aall", cidx), "vw")

        @blk.scalar
        def _(s):
            self.wtk(s, "vw", ("Aflc", cidx))
            self.wtk(s, "vw", ("bldA", cidx - 1))
            sb.wait(s, "sw")
            self.ssync(s, s.activation(coss[:, :], flc[:, :], AF.Sin, scale=0.25))
            self.ssync(s, s.activation(sins[:, :], flc[:, :], AF.Sin, scale=0.25,
                                       bias=S["consts"][:, 1:2]))
        self.mark(("trig", cidx), "sw")

        @blk.vector
        def _(v):
            self.wtk(v, "sw", ("trig", cidx))
            self.wtk(v, "pw", ("Yl", cidx - 2))
            sb.wait(v, "vw")
            self.vsync(v, v.tensor_mul(pmag[:, :], coss[:, :], coss[:, :]))
            self.vsync(v, v.tensor_mul(qmag[:, :], coss[:, :], sins[:, :]))
            self.vsync(v, v.tensor_scalar(cer[:, :], pmag[:, :], -2.0, 1.0,
                                          ALU.mult, ALU.add))
            self.vsync(v, v.tensor_scalar_mul(cei[:, :], qmag[:, :], 2.0))
            self.vsync(v, v.tensor_mul(pmag[:, :], cei[:, :], cei[:, :]))
            self.vsync(v, v.tensor_mul(qmag[:, :], cei[:, :], cer[:, :]))
            self.vsync(v, v.tensor_scalar(coss[:, :], pmag[:, :], -2.0, 1.0,
                                          ALU.mult, ALU.add))
            self.vsync(v, v.tensor_scalar_mul(sins[:, :], qmag[:, :], 2.0))
        self.mark(("dbl", cidx), "vw")

        @blk.scalar
        def _(s):
            self.wtk(s, "vw", ("Aall", cidx))
            self.wtk(s, "vw", ("dbl", cidx))
            self.wtk(s, "vw", ("bldB", cidx - 1))
            sb.wait(s, "sw")
            self.ssync(s, s.activation(pmag[:, :], pex[:, :], AF.Exp))
            self.ssync(s, s.activation(qmag[:, :], qex[:, :], AF.Exp))
            self.ssync(s, s.activation(S["tmpA"][:, 64:88], S["tmpA"][:, 64:88],
                                       AF.Exp))
            self.ssync(s, s.activation(S["tmpA"][:, 96:128], S["tmpA"][:, 32:64],
                                       AF.Exp))
        self.mark(("exp", cidx), "sw")

        @blk.vector
        def _(v):
            self.wtk(v, "sw", ("exp", cidx))
            sb.wait(v, "vw")
            c3 = coss.rearrange("p (g w) -> p g w", g=8)
            s3 = sins.rearrange("p (g w) -> p g w", g=8)
            e3 = pmag.rearrange("p (g w) -> p g w", g=8)
            o1 = lambda x: x.rearrange("p (g o) -> p g o", o=1)
            tmp = S["tmpA"]
            self.vsync(v, v.tensor_tensor(o1(cl[:, co + 24:co + 32]),
                                          e3[:, :, CH - 1:CH],
                                          c3[:, :, CH - 1:CH], ALU.mult))
            self.vsync(v, v.tensor_tensor(o1(tmp[:, 8:16]),
                                          e3[:, :, CH - 1:CH],
                                          s3[:, :, CH - 1:CH], ALU.mult))
            et = S["etot"]
            self.vsync(v, v.tensor_mul(et[:, pc * 16:pc * 16 + 8],
                                       cl[:, co + 24:co + 32], tmp[:, 120:128]))
            self.vsync(v, v.tensor_mul(et[:, pc * 16 + 8:pc * 16 + 16],
                                       tmp[:, 8:16], tmp[:, 120:128]))

        @blk.tensor
        def _(t):
            self.wtk(t, "sw", ("convS", m))
            self.guard(t, "wideB")
            bc = S["bcall"][:, m * MAC + w0:m * MAC + w0 + CH]
            t.matmul(PS["wide"][:, 512:640], S["cbf"][:, 640:768], bc[:, :],
                     start=True, stop=True)
            t.matmul(PS["wide"][:, 640:768], S["cbf"][:, 768:896], bc[:, :],
                     start=True, stop=True)
            t.matmul(PS["wide"][:, 768:896], S["cbf"][:, 896:1024], bc[:, :],
                     start=True, stop=True)
            op = t.matmul(PS["wide"][:, 896:1024], S["cbf"][:, 1024:1152],
                          bc[:, :], start=True, stop=True)
            sb.inc(op, "pw")
        self.mark(("crepF", cidx), "pw")

        @blk.scalar
        def _(s):
            sb.wait(s, "pw", self.tk2[("crepF", cidx)])
            self.wtk(s, "vw", ("bldA", cidx - 1))
            op = s.activation(crep[:, :], PS["wide"][:, 512:1024], AF.Copy)
            sb.inc(op, "sw")
        self.mark(("crepS", cidx), "sw")
        self.setguard("wideB", "sw")

        @blk.vector
        def _(v):
            sb.wait(v, "vw")
            self.wtk(v, "sw", ("exp", cidx))
            self.wtk(v, "sw", ("crepS", cidx))
            self.wtk(v, "sw", ("dtBS", m))
            self.wtk(v, "pw", ("Gl", cidx - 2))
            self.wtk(v, "pw", ("Yl", cidx - 2))
            self.wtk(v, "pw", ("bvT", cidx - 1))
            BrR, BiR = crep[:, 0:CH], crep[:, CH:2 * CH]
            CrR, CiR = crep[:, 2 * CH:3 * CH], crep[:, 3 * CH:4 * CH]

            def rep3(x):
                return x.rearrange("p (o w) -> p o w", o=1).broadcast_to([128, 8, CH])
            g3 = lambda x: x.rearrange("p (g w) -> p g w", g=8)
            tmp = S["tmpA"]
            sb.inc(v.tensor_mul(cer[:, :], pmag[:, :], coss[:, :]), "vw")
            sb.inc(v.tensor_mul(cei[:, :], pmag[:, :], sins[:, :]), "vw")
            sb.wait(v, "vw")
            sb.inc(v.tensor_tensor(g3(prc), g3(cer), rep3(CrR), ALU.mult), "vw")
            sb.inc(v.tensor_tensor(g3(bvr), g3(cei), rep3(CiR), ALU.mult), "vw")
            sb.inc(v.tensor_tensor(g3(pic), g3(cei), rep3(CrR), ALU.mult), "vw")
            sb.inc(v.tensor_tensor(g3(bvn), g3(cer), rep3(CiR), ALU.mult), "vw")
            sb.wait(v, "vw")
            sb.inc(v.tensor_sub(prc[:, :], prc[:, :], bvr[:, :]), "vw")
            sb.inc(v.tensor_add(pic[:, :], pic[:, :], bvn[:, :]), "vw")
            dt3 = dtB.rearrange("p (g w) -> p g w", g=8)[:, :, w0:w0 + CH]
            sb.inc(v.tensor_tensor(g3(qmag), g3(qmag), dt3, ALU.mult), "vw")
            sb.wait(v, "vw")
            p3r, p3i = g3(prc), g3(pic)
            ce3r, ce3i = g3(cer), g3(cei)
            for i in range(4):
                ebcol = tmp[:, 96 + 8 * i:104 + 8 * i].rearrange(
                    "p (g o) -> p g o", o=1).broadcast_to([128, 8, 32])
                sb.inc(v.tensor_tensor(ce3r[:, :, 32 * i:32 * i + 32],
                                       p3r[:, :, 32 * i:32 * i + 32], ebcol,
                                       ALU.mult), "vw")
                sb.inc(v.tensor_tensor(ce3i[:, :, 32 * i:32 * i + 32],
                                       p3i[:, :, 32 * i:32 * i + 32], ebcol,
                                       ALU.mult), "vw")
            ur, ui = cv["wcrB"], cv["wciB"]
            sb.inc(v.tensor_tensor(g3(bvr), g3(coss), rep3(BrR), ALU.mult), "vw")
            sb.inc(v.tensor_tensor(g3(ur), g3(sins), rep3(BiR), ALU.mult), "vw")
            sb.inc(v.tensor_tensor(g3(bvn), g3(sins), rep3(BrR), ALU.mult), "vw")
            sb.inc(v.tensor_tensor(g3(ui), g3(coss), rep3(BiR), ALU.mult), "vw")
            sb.wait(v, "vw")
            sb.inc(v.tensor_add(bvr[:, :], bvr[:, :], ur[:, :]), "vw")
            sb.inc(v.tensor_sub(bvn[:, :], bvn[:, :], ui[:, :]), "vw")
        self.mark(("bldA", cidx), "vw")

    def pA_buildB(self, blk, cidx):
        sb, S, PS = self, self.sb, self.ps
        m, pc = cidx // CPM, cidx % CPM
        cv = self.chvars(cidx)
        qrv, qinv = cv["qrv"], cv["qinv"]
        bvr, bvn = cv["bvr"], cv["bvn"]
        qmag, pmag = cv["qmag"], cv["pmag"]
        wcrB, wciB = cv["wcrB"], cv["wciB"]
        cl = S["cols"]
        co = 32 * pc

        @blk.vector
        def _(v):
            sb.wait(v, "vw")
            self.wtk(v, "pw", ("Gl", cidx - 1))
            g3 = lambda x: x.rearrange("p (g w) -> p g w", g=8)
            tmp = S["tmpA"]
            q3m = g3(qmag)
            for i in range(4):
                sb.inc(v.tensor_tensor(
                    g3(qrv[i])[:, :, 32 * i:32 * i + 32],
                    g3(bvr)[:, :, 32 * i:32 * i + 32],
                    q3m[:, :, 32 * i:32 * i + 32], ALU.mult), "vw")
                sb.inc(v.tensor_tensor(
                    g3(qinv[i])[:, :, 32 * i:32 * i + 32],
                    g3(bvn)[:, :, 32 * i:32 * i + 32],
                    q3m[:, :, 32 * i:32 * i + 32], ALU.mult), "vw")
            sb.wait(v, "vw")
            for i in range(1, 4):
                edcol = tmp[:, 64 + 8 * (i - 1):72 + 8 * (i - 1)].rearrange(
                    "p (g o) -> p g o", o=1).broadcast_to([128, 8, 32 * i])
                sb.inc(v.tensor_tensor(
                    g3(qrv[i])[:, :, 0:32 * i], g3(qrv[i - 1])[:, :, 0:32 * i],
                    edcol, ALU.mult), "vw")
                sb.inc(v.tensor_tensor(
                    g3(qinv[i])[:, :, 0:32 * i], g3(qinv[i - 1])[:, :, 0:32 * i],
                    edcol, ALU.mult), "vw")
                sb.wait(v, "vw")
            sb.inc(v.tensor_copy(g3(wcrB), cl[:, co + 24:co + 32].rearrange(
                "p (g o) -> p g o", o=1).broadcast_to([128, 8, CH])), "vw")
            sb.inc(v.tensor_copy(g3(wciB), tmp[:, 8:16].rearrange(
                "p (g o) -> p g o", o=1).broadcast_to([128, 8, CH])), "vw")
            sb.wait(v, "vw")
            sb.inc(v.tensor_mul(bvr[:, :], qrv[3][:, :], wcrB[:, :]), "vw")
            sb.inc(v.tensor_mul(pmag[:, :], qinv[3][:, :], wciB[:, :]), "vw")
            sb.inc(v.tensor_mul(bvn[:, :], qinv[3][:, :], wcrB[:, :]), "vw")
            sb.inc(v.tensor_mul(qmag[:, :], qrv[3][:, :], wciB[:, :]), "vw")
            sb.wait(v, "vw")
            sb.inc(v.tensor_add(bvr[:, :], bvr[:, :], pmag[:, :]), "vw")
            sb.inc(v.tensor_sub(bvn[:, :], bvn[:, :], qmag[:, :]), "vw")
        self.mark(("bldB", cidx), "vw")

    def pA_gyG(self, blk, cidx):
        sb, S, PS = self, self.sb, self.ps
        m, pc = cidx // CPM, cidx % CPM
        w0 = pc * CH
        cv = self.chvars(cidx)
        xss, xd = cv["xss"], cv["xd"]
        bvr, bvn = cv["bvr"], cv["bvn"]
        bvrT, bvnT = cv["bvrT"], cv["bvnT"]

        for src, dst in [(bvr, bvrT), (bvn, bvnT)]:
            for grp in range(2):
                @blk.tensor
                def _(t, src=src, grp=grp):
                    self.wtk(t, "vw", ("bldB", cidx))
                    self.guard(t, "wideA")
                    for q in range(4):
                        idx = grp * 4 + q
                        op = t.transpose(self.ptv(q * CH, (q + 1) * CH),
                                         src[:, idx * CH:(idx + 1) * CH],
                                         S["cbf"][:, 0:128])
                    sb.inc(op, "pw")
                tk = self.tok("pw")

                @blk.scalar
                def _(s, dst=dst, grp=grp, tk=tk):
                    sb.wait(s, "pw", tk)
                    self.wtk(s, "pw", ("Sinc", cidx - 1))
                    op = s.activation(dst[:, grp * 512:(grp + 1) * 512],
                                      self.ptv(0, 512), AF.Copy)
                    sb.inc(op, "sw")
                self.setguard("wideA", "sw")
        self.mark(("bvT", cidx), "pw")
        self.mark(("bvTS", cidx), "sw")

        if pc == 0:
            self.pA_xdT(blk, cidx)
            self.pA_xdT(blk, cidx + 1)

        @blk.tensor
        def _(t):
            self.wtk(t, "sw", ("xdS", cidx))
            self.wtk(t, "sw", ("bvTS", cidx))
            self.guard(t, "st")
            for jq in range(8):
                for s4 in range(4):
                    h = 4 * jq + s4
                    base = (h // 4) * CH + (h % 4) * 32
                    lr = bvrT[:, base:base + 32]
                    ln2 = bvnT[:, base:base + 32]
                    rhs = xd[:, h * 64:(h + 1) * 64]
                    self.mm(t, self.ps["st"][32 * s4:32 * s4 + 32,
                                              jq * 128:jq * 128 + 64],
                            lr, rhs, True, True, 0, 32 * s4)
                    op = self.mm(t, self.ps["st"][32 * s4:32 * s4 + 32,
                                                  jq * 128 + 64:jq * 128 + 128],
                                 ln2, rhs, True, True, 0, 32 * s4)
            sb.inc(op, "pw")
        self.mark(("Sinc", cidx), "pw")

        for j in range(4):
            self._g_fill(blk, cidx, j)

    def pA_xdT(self, blk, cidx):
        sb, S = self, self.sb
        m, pc = cidx // CPM, cidx % CPM
        w0 = pc * CH
        cv = self.chvars(cidx)
        xss, xd = cv["xss"], cv["xd"]
        for grp in range(4):
            @blk.tensor
            def _(t, grp=grp):
                self.wtk(t, "sw", ("convS", m))
                self.guard(t, "wideA")
                for q in range(4):
                    idx = grp * 4 + q
                    op = t.transpose(self.ptv(q * CH, (q + 1) * CH),
                                     xss[:, idx * MAC + w0:idx * MAC + w0 + CH],
                                     S["cbf"][:, 0:128])
                sb.inc(op, "pw")
            tk = self.tok("pw")

            @blk.scalar
            def _(s, grp=grp, tk=tk):
                sb.wait(s, "pw", tk)
                self.wtk(s, "pw", ("Yl", cidx - 2))
                op = s.activation(xd[:, grp * 512:(grp + 1) * 512],
                                  self.ptv(0, 512), AF.Copy)
                sb.inc(op, "sw")
            self.setguard("wideA", "sw")
        self.mark(("xdT", cidx), "pw")
        self.mark(("xdS", cidx), "sw")

    def _g_fill(self, blk, cidx, j):
        sb, S, PS = self, self.sb, self.ps
        cv = self.chvars(cidx)
        prc, pic = cv["prc"], cv["pic"]
        qrv, qinv = cv["qrv"], cv["qinv"]
        s4, half = j // 2, j % 2
        heads = [4 * jj + s4 for jj in range(8)]
        hgrp = heads[half * 4:(half + 1) * 4]
        gbank = ["a", "b", "c", "d"][j % 4]

        @blk.tensor
        def _(t):
            self.wtk(t, "vw", ("bldA", cidx))
            self.wtk(t, "vw", ("bldB", cidx))
            self.guard(t, gbank)
            for i, h in enumerate(hgrp):
                gb, rr = h // 4, (h % 4) * 32
                for a in range(4):
                    osl = PS[gbank][:, i * CH + 32 * a:i * CH + 32 * a + 32]
                    psl = slice(gb * CH + 32 * a, gb * CH + 32 * a + 32)
                    self.mm(t, osl,
                            qrv[a][rr:rr + 32, gb * CH:(gb + 1) * CH],
                            prc[rr:rr + 32, psl], a == 0, False, rr, 0)
                    self.mm(t, osl,
                            qinv[a][rr:rr + 32, gb * CH:(gb + 1) * CH],
                            pic[rr:rr + 32, psl], False, False, rr, 0)
                op = self.mm(t, PS[gbank][:, i * CH:(i + 1) * CH],
                             S["cbf"][:, 0:128],
                             self.V(DIS_OFF, 128, 8 * 512, BF16)[
                                 :, (half * 4 + s4) * 512 + i * CH:
                                 (half * 4 + s4) * 512 + (i + 1) * CH],
                             False, True)
            sb.inc(op, "pw")
        self.mark(("Gf", cidx, j), "pw")

    def _gm_drain(self, blk, cidx, j):
        sb, S, PS = self, self.sb, self.ps
        gbank = ["a", "b", "c", "d"][j % 4]
        gm8 = self.V(GM8 + j * 1024, 128, 512, BF16)

        @blk.vector
        def _(v):
            sb.wait(v, "vw")
            sb.wait(v, "pw", self.tk2[("Gf", cidx, j)])
            self.wtk(v, "pw", ("Yf", cidx - 1, j))
            op = v.tensor_tensor(gm8[:, :], PS[gbank][:, 0:512],
                                 S["cbf"][:, 128:640], ALU.mult)
            sb.inc(op, "vw")
        self.mark(("gm", cidx, j), "vw")
        self.setguard(gbank, "vw")

    def pA_gyGdrain(self, blk, cidx):
        for j in range(4):
            self._gm_drain(blk, cidx, j)
        for j in range(4, 8):
            self._g_fill(blk, cidx, j)
        for j in range(4, 8):
            self._gm_drain(blk, cidx, j)
        self.mark(("Gl", cidx), "pw")

    def _y_fill(self, blk, cidx, j):
        sb, S, PS = self, self.sb, self.ps
        m, pc = cidx // CPM, cidx % CPM
        cv = self.chvars(cidx)
        xd, cer, cei = cv["xd"], cv["cer"], cv["cei"]
        s4, half = j // 2, j % 2
        heads = [4 * jj + s4 for jj in range(8)]
        hgrp = heads[half * 4:(half + 1) * 4]
        ybank = ["c", "d"][j % 2]
        ro = (s4 % 2) * 64
        gm8 = self.V(GM8 + j * 1024, 128, 512, BF16)

        @blk.tensor
        def _(t):
            self.wtk(t, "vw", ("gm", cidx, j))
            self.wtk(t, "vw", ("bldA", cidx))
            self.wtk(t, "sw", ("tailS", cidx - 1))
            self.wtk(t, "sw", ("xdS", cidx))
            self.guard(t, ybank)
            for i, h in enumerate(hgrp):
                gb, rr = h // 4, (h % 4) * 32
                jq = h // 4
                self.mm(t, PS[ybank][ro:ro + 64, i * CH:(i + 1) * CH],
                        xd[:, h * 64:(h + 1) * 64],
                        gm8[:, i * CH:(i + 1) * CH],
                        True, False)
                self.mm(t, PS[ybank][ro:ro + 64, i * CH:(i + 1) * CH],
                        S["stateb"][32 * s4:32 * s4 + 32, jq * 128:jq * 128 + 64],
                        cer[rr:rr + 32, gb * CH:(gb + 1) * CH],
                        False, False, 32 * s4, ro)
                op = self.mm(t, PS[ybank][ro:ro + 64, i * CH:(i + 1) * CH],
                             S["stateb"][32 * s4:32 * s4 + 32,
                                         jq * 128 + 64:jq * 128 + 128],
                             cei[rr:rr + 32, gb * CH:(gb + 1) * CH],
                             False, True, 32 * s4, ro)
            sb.inc(op, "pw")
        self.mark(("Yf", cidx, j), "pw")

    def _yt_drain(self, blk, cidx, j):
        sb, S, PS = self, self.sb, self.ps
        m, pc = cidx // CPM, cidx % CPM
        w0 = pc * CH
        ytile = self.chvars(cidx)["ytile"]
        s4, half = j // 2, j % 2
        heads = [4 * jj + s4 for jj in range(8)]
        hgrp = heads[half * 4:(half + 1) * 4]
        ybank = ["c", "d"][j % 2]
        ro = (s4 % 2) * 64

        @blk.scalar
        def _(s):
            sb.wait(s, "pw", self.tk2[("Yf", cidx, j)])
            self.wtk(s, "ds", ("yspill", m - 1))
            h0 = hgrp[0]
            rbase = (h0 % 2) * 64
            yv = ytile.rearrange("p (g w) -> p g w", g=NIB)[
                rbase:rbase + 64, h0 // 2:NIB:2, w0:w0 + CH][:, 0:4, :]
            pv = PS[ybank][ro:ro + 64, 0:512].rearrange("p (i w) -> p i w", i=4)
            op = s.activation(yv, pv, AF.Copy)
            sb.inc(op, "sw")
        self.setguard(ybank, "sw")

    def pA_gyY(self, blk, cidx):
        sb, S, PS, Dr = self, self.sb, self.ps, self.dram
        m, pc = cidx // CPM, cidx % CPM
        c0 = m * MAC
        for j in range(4):
            self._y_fill(blk, cidx, j)
        for j in range(4):
            self._yt_drain(blk, cidx, j)
        for j in range(4, 8):
            self._y_fill(blk, cidx, j)
        self.mark(("Yl", cidx), "pw")
        for j in range(4, 8):
            self._yt_drain(blk, cidx, j)
        self.mark(("yt", cidx), "sw")

        cur = S["state0"] if cidx % 2 == 0 else S["state1"]
        nxt = S["state1"] if cidx % 2 == 0 else S["state0"]

        @blk.vector
        def _(v):
            sb.wait(v, "vw")
            sb.wait(v, "pw", self.tk2[("Sinc", cidx)])
            sb.wait(v, "pw", self.tk2[("Yl", cidx)])
            self.wtk(v, "sw", ("convS", m + 1))
            et = S["etot"]
            erB = et[:, pc * 16:pc * 16 + 8].rearrange(
                "p (g o) -> p g o", o=1).broadcast_to([128, 8, 64])
            eiB = et[:, pc * 16 + 8:pc * 16 + 16].rearrange(
                "p (g o) -> p g o", o=1).broadcast_to([128, 8, 64])
            curv = cur[:, :].rearrange("p (j w) -> p j w", j=8)
            nxtv = nxt[:, :].rearrange("p (j w) -> p j w", j=8)
            psv = self.ps["st"][:, :].rearrange("p (j w) -> p j w", j=8)
            curS, curT = curv[:, :, 0:64], curv[:, :, 64:128]
            nxtS, nxtT = nxtv[:, :, 0:64], nxtv[:, :, 64:128]
            if cidx == 0:
                self.vsync(v, v.tensor_copy(nxtS, psv[:, :, 0:64]))
                self.vsync(v, v.tensor_copy(nxtT, psv[:, :, 64:128]))
            else:
                tA = S["tmpB"][:, 0:512].rearrange("p (j w) -> p j w", j=8)
                self.vsync(v, v.tensor_tensor(nxtS, curS, erB, ALU.mult))
                self.vsync(v, v.tensor_tensor(tA, curT, eiB, ALU.mult))
                self.vsync(v, v.tensor_tensor(nxtS, nxtS, tA, ALU.add))
                self.vsync(v, v.tensor_tensor(nxtS, nxtS, psv[:, :, 0:64], ALU.add))
                self.vsync(v, v.tensor_tensor(nxtT, curT, erB, ALU.mult))
                self.vsync(v, v.tensor_tensor(tA, curS, eiB, ALU.mult))
                self.vsync(v, v.tensor_sub(nxtT, nxtT, tA))
                self.vsync(v, v.tensor_tensor(nxtT, nxtT, psv[:, :, 64:128], ALU.add))
        self.mark(("tail", cidx), "vw")
        self.setguard("st", "vw")

        @blk.scalar
        def _(s):
            self.wtk(s, "vw", ("tail", cidx))
            self.wtk(s, "pw", ("Yl", cidx))
            op = s.activation(S["stateb"][:, :], nxt[:, :], AF.Copy)
            sb.inc(op, "sw")
        self.mark(("tailS", cidx), "sw")

        if pc == CPM - 1:
            @blk.sync
            def _(s):
                sb.wait(s, "sw", self.tk2[("yt", cidx)])
                yt = self.V(OA["ytile"], 128, NIB * MAC, BF16)
                op = s.dma_start(
                    out=Dr["sc_y"].rearrange("p (g t) -> p g t", g=NIB)[
                        :, :, c0:c0 + MAC],
                    in_=yt.rearrange("p (g w) -> p g w", g=NIB))
                sb.inc(op, "ds", 16)
            self.mark(("yspill", m), "ds")

    def passA_run(self, blk):
        NCH = NMAC * CPM
        self.pA_xbct(blk, 0)
        self.pA_conv(blk, 0)
        self.pA_xbct(blk, 1)
        self.pA_theta_scans(blk, 0)
        self.pA_buildA(blk, 0)
        self.pA_buildB(blk, 0)
        for k in range(NCH):
            self.pA_gyG(blk, k)
            if k % 2 == 1 and (k + 3) // 2 < NMAC:
                self.pA_xbct(blk, (k + 3) // 2)
            nxt = k + 1
            if nxt < NCH and nxt % CPM == 0:
                self.pA_gyGdrain(blk, k)
                self.pA_theta_scans(blk, nxt // CPM)
                self.pA_buildA(blk, nxt)
                self.pA_buildB(blk, nxt)
            else:
                if nxt < NCH:
                    self.pA_buildA(blk, nxt)
                self.pA_gyGdrain(blk, k)
                if nxt < NCH:
                    self.pA_buildB(blk, nxt)
            if k % CPM == 0 and k // CPM + 1 < NMAC:
                self.pA_conv(blk, k // CPM + 1)
            self.pA_gyY(blk, k)
        self.setguard("wide", "sw")

    def collective2(self, blk):
        sb, S, Dr = self, self.sb, self.dram
        final = S["state0"] if (NMAC * CPM) % 2 == 0 else S["state1"]

        @blk.gpsimd
        def _(g):
            sb.wait(g, "vw")
            sb.inc(g.dma_start(out=Dr["cc2_in"], in_=final[:, :]), "dg", 16)
            sb.wait(g, "dg")
            op = g.collective_compute(
                "AllGather", ALU.bypass,
                replica_groups=[[0, 1], [2, 3], [4, 5], [6, 7]],
                ins=[Dr["cc2_in"].opt()],
                outs=[Dr["cc2_out"].opt()])
            sb.inc(op, "cc")
            sb.wait(g, "cc")
            sb.inc(g.dma_start(out=S["state0"][:, :], in_=Dr["cc2_out"][0:128, :]),
                   "dg", 16)
            sb.inc(g.dma_start(out=S["state1"][:, :], in_=Dr["cc2_out"][128:256, :]),
                   "dg", 16)

        @blk.vector
        def _(v):
            sb.wait(v, "dg")
            self.wtk(v, "sw", ("tailS", NMAC * CPM - 1))
            self.vsync(v, v.tensor_scalar(S["statein"][:, :], S["state0"][:, :],
                                          S["sel2c"][:, 0:1], None, ALU.mult))
            self.vsync(v, v.scalar_tensor_tensor(
                S["statein"][:, :], S["state1"][:, :],
                S["sel2c"][:, 1:2], S["statein"][:, :], ALU.mult, ALU.add))
            self.vsync(v, v.tensor_copy(S["stateb"][:, :], S["statein"][:, :]))

    def _passB_load(self, g, m, boff):
        sb, Dr = self, self.dram
        c0 = m * MAC
        sem = "dgm0" if m % 2 == 0 else "dgm1"
        sb.wait(g, sem)
        sb.wait(g, "dgs")
        sb.wait(g, "ds")
        for scn, off, nb, dt in [("sc_y", IB_YT, NIB, BF16), ("sc_z", IB_ZT, NIB, BF16),
                                 ("sc_rg", IB_RG, 8, BF16), ("sc_fg", IB_FG, 8, BF16)]:
            op = g.dma_start(
                out=self.V(boff + off, 128, nb * MAC, dt).rearrange(
                    "p (g w) -> p g w", g=nb),
                in_=Dr[scn].rearrange("p (g t) -> p g t", g=nb)[:, :, c0:c0 + MAC])
            sb.inc(op, sem, 16)
        op = g.dma_start(
            out=self.V(boff + IB_XR, 128, NDB * MAC).rearrange("p (g w) -> p g w", g=NDB),
            in_=Dr["sc_xr"].rearrange("p (g t) -> p g t", g=NDB)[:, :, c0:c0 + MAC])
        sb.inc(op, sem, 16)
        self.wldtok[("B", m)] = self.tok(sem)

    def passB(self, blk):
        sb, Dr = self, self.dram

        @blk.gpsimd
        def _(g):
            sb.wait(g, "vw")
            sb.wait(g, "dgs")
            op = g.dma_start(
                out=self.V(OB["opw"], 128, NIB * D, BF16).rearrange(
                    "p (g f) -> p g f", g=NIB),
                in_=Dr["w_oproj"].rearrange("(g p) f -> p g f", p=128))
            sb.inc(op, "dg", 16)
            op = g.dma_start(
                out=self.V(OF["wexp0"], 128, 8 * D, BF16).rearrange(
                    "p (g f) -> p g f", g=NDB),
                in_=Dr["w_exp"][0].rearrange("(g p) f -> p g f", p=128))
            sb.inc(op, "dg0", 16)
            self.wldtok[0] = self.tok("dg0")
            self._passB_load(g, 0, OB["in0"])

        for m in range(NMAC):
            self.passB_macro(blk, m)
        self.passB2(blk)

    def passB_macro(self, blk, m):
        sb, S, PS, Dr = self, self.sb, self.ps, self.dram
        c0 = m * MAC
        boff = OB["in0"] if m % 2 == 0 else OB["in1"]
        nxtoff = OB["in1"] if m % 2 == 0 else OB["in0"]
        yt = self.V(boff + IB_YT, 128, NIB * MAC, BF16)
        zt = self.V(boff + IB_ZT, 128, NIB * MAC, BF16)
        rg = self.V(boff + IB_RG, 128, 8 * MAC, BF16)
        fg = self.V(boff + IB_FG, 128, 8 * MAC, BF16)
        xroff = boff + IB_XR
        xr = self.V(xroff, 128, NDB * MAC)
        cosg = self.V(OB["cosg"], 128, 8 * MAC, BF16)
        sing = self.V(OB["sing"], 128, 8 * MAC, BF16)
        cegr = self.V(OB["cegr"], 128, 8 * MAC, BF16)
        cegi = self.V(OB["cegi"], 128, 8 * MAC, BF16)

        if m + 1 < NMAC:
            @blk.gpsimd
            def _(g, m=m, nxtoff=nxtoff):
                sb.wait(g, "vw", self.wtok.get(("Bbuf", (m + 1) % 2), 0))
                self._passB_load(g, m + 1, nxtoff)

        @blk.scalar
        def _(s):
            sb.wait(s, "dgm0" if m % 2 == 0 else "dgm1", self.wldtok[("B", m)])
            sb.wait(s, "vw")
            if m == 0:
                sb.wait(s, "pw")
            self.ssync(s, s.activation(cosg[:, :], fg[:, :], AF.Sin, scale=0.5))
            self.ssync(s, s.activation(sing[:, :], fg[:, :], AF.Sin, scale=0.5,
                                       bias=S["consts"][:, 1:2]))
            self.ssync(s, s.activation(cegr[:, :], rg[:, :], AF.Exp))

        @blk.vector
        def _(v):
            sb.wait(v, "sw")
            self.vsync(v, v.tensor_mul(cegi[:, :], cosg[:, :], sing[:, :]))
            self.vsync(v, v.tensor_mul(cosg[:, :], cosg[:, :], cosg[:, :]))
            self.vsync(v, v.tensor_scalar(cosg[:, :], cosg[:, :], -2.0, 1.0,
                                          ALU.mult, ALU.add))
            self.vsync(v, v.tensor_scalar_mul(sing[:, :], cegi[:, :], 2.0))
            self.vsync(v, v.tensor_mul(cosg[:, :], cosg[:, :], cegr[:, :]))
            self.vsync(v, v.tensor_mul(sing[:, :], sing[:, :], cegr[:, :]))

        for pc in range(CPM):
            w0 = pc * CH

            @blk.tensor
            def _(t, pc=pc, w0=w0):
                sb.wait(t, "vw")
                self.guard(t, "wide")
                bc = S["bcall"][:, c0 + w0:c0 + w0 + CH]
                t.matmul(PS["wide"][:, 0:128], S["cbf"][:, 896:1024],
                         bc[:, :], start=True, stop=True)
                op = t.matmul(PS["wide"][:, 128:256], S["cbf"][:, 1024:1152],
                              bc[:, :], start=True, stop=True)
                sb.inc(op, "pw")

            @blk.vector
            def _(v, pc=pc, w0=w0):
                sb.wait(v, "pw")
                self.vsync(v, v.tensor_copy(S["gm"][:, 0:256], PS["wide"][:, 0:256]))
            self.setguard("wide", "vw")

            @blk.vector
            def _(v, pc=pc, w0=w0):
                CrR, CiR = S["gm"][:, 0:CH], S["gm"][:, CH:2 * CH]

                def rep3(x):
                    return x.rearrange("p (o w) -> p o w", o=1).broadcast_to([128, 8, CH])
                g3 = lambda x: x.rearrange("p (g w) -> p g w", g=8)[:, :, w0:w0 + CH]
                tb = S["tmpB"][:, 0:512].bitcast(BF16)
                self.vsync(v, v.tensor_tensor(tb[:, 0:8 * CH].rearrange(
                    "p (g w) -> p g w", g=8), g3(sing), rep3(CiR), ALU.mult))
                self.vsync(v, v.tensor_tensor(g3(cegr), g3(cosg), rep3(CrR), ALU.mult))
                self.vsync(v, v.tensor_tensor(
                    g3(cegr), g3(cegr),
                    tb[:, 0:8 * CH].rearrange("p (g w) -> p g w", g=8), ALU.subtract))
                self.vsync(v, v.tensor_tensor(tb[:, 0:8 * CH].rearrange(
                    "p (g w) -> p g w", g=8), g3(cosg), rep3(CiR), ALU.mult))
                self.vsync(v, v.tensor_tensor(g3(cegi), g3(sing), rep3(CrR), ALU.mult))
                self.vsync(v, v.tensor_tensor(
                    g3(cegi), g3(cegi),
                    tb[:, 0:8 * CH].rearrange("p (g w) -> p g w", g=8), ALU.add))

            for s4 in range(4):
                heads = [4 * jj + s4 for jj in range(8)]
                for half in range(2):
                    hgrp = heads[half * 4:(half + 1) * 4]
                    ybank = ["c", "d"][(s4 * 2 + half) % 2]

                    @blk.tensor
                    def _(t, hgrp=hgrp, ybank=ybank, s4=s4, w0=w0):
                        sb.wait(t, "vw")
                        self.guard(t, ybank)
                        ro = (s4 % 2) * 64
                        for i, h in enumerate(hgrp):
                            gb, rr = h // 4, (h % 4) * 32
                            jq = h // 4
                            self.mm(t, PS[ybank][ro:ro + 64, i * CH:(i + 1) * CH],
                                    S["stateb"][32 * s4:32 * s4 + 32,
                                                jq * 128:jq * 128 + 64],
                                    cegr[rr:rr + 32, gb * MAC + w0:gb * MAC + w0 + CH],
                                    True, False, 32 * s4, ro)
                            op = self.mm(t, PS[ybank][ro:ro + 64, i * CH:(i + 1) * CH],
                                         S["stateb"][32 * s4:32 * s4 + 32,
                                                     jq * 128 + 64:jq * 128 + 128],
                                         cegi[rr:rr + 32, gb * MAC + w0:gb * MAC + w0 + CH],
                                         False, True, 32 * s4, ro)
                        sb.inc(op, "pw")
                    ytk = self.tok("pw")

                    @blk.vector
                    def _(v, hgrp=hgrp, ybank=ybank, s4=s4, ytk=ytk, w0=w0):
                        sb.wait(v, "pw", ytk)
                        ro = (s4 % 2) * 64
                        h0 = hgrp[0]
                        rbase = (h0 % 2) * 64
                        yv = yt.rearrange("p (g w) -> p g w", g=NIB)[
                            rbase:rbase + 64, h0 // 2:NIB:2, w0:w0 + CH][:, 0:4, :]
                        pv = PS[ybank][ro:ro + 64, 0:512].rearrange("p (i w) -> p i w", i=4)
                        op = v.tensor_tensor(yv, yv, pv, ALU.add)
                        sb.inc(op, "vw")
                    self.setguard(ybank, "vw")

        gt = self.V(OB["gt"], 128, NIB * MAC, BF16)
        sq = self.V(OB["sq"], 128, NIB * MAC, BF16)

        @blk.vector
        def _(v):
            sb.wait(v, "vw")
            self.vsync(v, v.tensor_mul(gt[:, :], yt[:, :], zt[:, :]))

        @blk.scalar
        def _(s):
            sb.wait(s, "vw")
            sb.inc(s.activation(sq[:, :], gt[:, :], AF.Square), "sw")

        @blk.tensor
        def _(t):
            sb.wait(t, "sw")
            self.guard(t, "a")
            for bb in range(NIB):
                op = t.matmul(PS["a"][0:1, 0:MAC], S["cbf"][:, 1152:1153],
                              sq[:, bb * MAC:(bb + 1) * MAC],
                              start=(bb == 0), stop=(bb == NIB - 1))
            sb.inc(op, "pw")

        @blk.vector
        def _(v):
            sb.wait(v, "pw")
            self.vsync(v, v.tensor_scalar_mul(S["stat"][:, 0:MAC],
                                              PS["a"][0:1, 0:MAC], 1.0 / DIN))
        self.setguard("a", "vw")

        @blk.scalar
        def _(s):
            sb.wait(s, "vw")
            self.ssync(s, s.activation(S["stat"][:, 0:MAC], S["stat"][:, 0:MAC],
                                       AF.Ln, bias=S["consts"][0:1, 3:4]))
            self.ssync(s, s.activation(S["stat"][:, 0:MAC], S["stat"][:, 0:MAC],
                                       AF.Exp, scale=-0.5))

        @blk.tensor
        def _(t):
            sb.wait(t, "sw")
            self.guard(t, "b")
            op = t.matmul(PS["b"][:, 0:MAC], S["o_row"][:, :], S["stat"][:, 0:MAC],
                          start=True, stop=True)
            sb.inc(op, "pw")

        @blk.vector
        def _(v):
            sb.wait(v, "pw")
            for bb in range(NIB):
                op = v.scalar_tensor_tensor(sq[:, bb * MAC:(bb + 1) * MAC],
                                            gt[:, bb * MAC:(bb + 1) * MAC],
                                            S["nwcol"][:, bb:bb + 1],
                                            PS["b"][:, 0:MAC], ALU.mult, ALU.mult)
            sb.inc(op, "vw")
        self.setguard("b", "vw")

        @blk.tensor
        def _(t):
            sb.wait(t, "vw")
            sb.wait(t, "dg")
            for nm in ["a", "b", "c", "d"]:
                self.guard(t, nm)
            opw = self.V(OB["opw"], 128, NIB * D, BF16)
            for do in range(NDB):
                bank = PS[["a", "b", "c", "d"][do // 2]][:, (do % 2) * MAC:(do % 2 + 1) * MAC]
                for di2 in range(NIB):
                    op = t.matmul(bank,
                                  opw[:, di2 * D + do * 128:di2 * D + (do + 1) * 128],
                                  sq[:, di2 * MAC:(di2 + 1) * MAC],
                                  start=(di2 == 0), stop=(di2 == NIB - 1))
            sb.inc(op, "pw")

        @blk.vector
        def _(v):
            sb.wait(v, "pw")
            for do in range(NDB):
                bank = PS[["a", "b", "c", "d"][do // 2]][:, (do % 2) * MAC:(do % 2 + 1) * MAC]
                op = v.tensor_add(xr[:, do * MAC:(do + 1) * MAC],
                                  xr[:, do * MAC:(do + 1) * MAC], bank)
            sb.inc(op, "vw")
        for nm in ["a", "b", "c", "d"]:
            self.setguard(nm, "vw")

        if PASSB_TRUNC:
            @blk.sync
            def _(s, m=m):
                sb.wait(s, "vw")
                op = s.dma_start(
                    out=Dr["out_x"].rearrange("(g p) t -> p g t", p=128)[
                        :, :, c0:c0 + MAC],
                    in_=xr.rearrange("p (g w) -> p g w", g=NDB))
                sb.inc(op, "ds", 16)
            self.wtok[("Bbuf", m % 2)] = self.tok("vw")
            return

        self.ln_stats(blk, xroff, 0, MAC, NDB, OB["sq2"], xcols=MAC)
        self.ln_norm(blk, xroff, OB["xh2"], 0, MAC, NDB, OB["sq2"],
                     xcols=MAC, hatcols=MAC)
        self.routing(blk, 0, MAC, gbase=c0, xoff=xroff, xcols=MAC)

        @blk.sync
        def _(s, m=m):
            sb.wait(s, "vw")
            op = s.dma_start(
                out=Dr["sc_xh"].rearrange("p (g t) -> p g t", g=NDB)[:, :, c0:c0 + MAC],
                in_=self.V(OB["xh2"], 128, NDB * MAC, BF16).rearrange(
                    "p (g w) -> p g w", g=NDB))
            sb.inc(op, "ds", 16)
            op = s.dma_start(
                out=Dr["sc_xr"].rearrange("p (g t) -> p g t", g=NDB)[:, :, c0:c0 + MAC],
                in_=xr.rearrange("p (g w) -> p g w", g=NDB))
            sb.inc(op, "ds", 16)
        self.wtok[("Bbuf", m % 2)] = self.tok("vw")

    def passB2(self, blk):
        sb, Dr = self, self.dram
        XB, HB, MB = 0, 16384, 24576
        SETS = [32768, 65536]
        dstok_b1 = self.tok("ds")

        def loader(j4):
            soff = SETS[j4 % 2]
            sem = "dgm0" if j4 % 2 == 0 else "dgm1"

            @blk.gpsimd
            def _(g, j4=j4, soff=soff, sem=sem):
                sb.wait(g, sem)
                sb.wait(g, "ds", dstok_b1)
                osem = "dso0" if j4 % 2 == 0 else "dso1"
                sb.wait(g, osem, self.wtok.get(("B2out", j4 % 2), 0))
                sb.wait(g, "vw", self.wtok.get(("B2buf", j4 % 2), 0))
                c0 = j4 * 512
                op = g.dma_start(
                    out=self.V(soff + XB, 128, NDB * 512).rearrange(
                        "p (g w) -> p g w", g=NDB),
                    in_=Dr["sc_xr"].rearrange("p (g t) -> p g t", g=NDB)[
                        :, :, c0:c0 + 512])
                sb.inc(op, sem, 16)
                op = g.dma_start(
                    out=self.V(soff + HB, 128, NDB * 512, BF16).rearrange(
                        "p (g w) -> p g w", g=NDB),
                    in_=Dr["sc_xh"].rearrange("p (g t) -> p g t", g=NDB)[
                        :, :, c0:c0 + 512])
                sb.inc(op, sem, 16)
                self.wldtok[("B2", j4)] = self.tok(sem)

        loader(0)
        for j4 in range(4):
            soff = SETS[j4 % 2]
            sem = "dgm0" if j4 % 2 == 0 else "dgm1"
            if j4 + 1 < 4:
                loader(j4 + 1)

            @blk.vector
            def _(v, j4=j4, sem=sem):
                sb.wait(v, sem, self.wldtok[("B2", j4)])

            self.ffn_experts(blk, 2, soff + HB, soff + XB, 0, 512,
                             gbase=j4 * 512, xr_cols=512, maskoff=soff + MB,
                             hatcols=512)

            @blk.sync
            def _(s, j4=j4, soff=soff):
                sb.wait(s, "vw")
                op = s.dma_start(
                    out=Dr["out_x"].rearrange("(g p) t -> p g t", p=128)[
                        :, :, j4 * 512:(j4 + 1) * 512],
                    in_=self.V(soff + XB, 128, NDB * 512).rearrange(
                        "p (g w) -> p g w", g=NDB))
                sb.inc(op, "dso0" if j4 % 2 == 0 else "dso1", 16)
            self.wtok[("B2out", j4 % 2)] = self.tok("dso0" if j4 % 2 == 0
                                                    else "dso1")
            self.wtok[("B2buf", j4 % 2)] = self.tok("vw")

    def finish(self, blk):
        sb = self

        @blk.sync
        def _(s):
            sb.wait(s, "vw")
            sb.wait(s, "pw")
            sb.wait(s, "sw")
            for nm in ["dg", "dg0", "dg1", "dgx", "dgs"]:
                sb.wait(s, nm)
            op = s.dma_start(
                out=self.dram["out_x"].rearrange("(g p) t -> p g t", p=128),
                in_=self.V(OF["xres"], 128, NDB * T).rearrange("p (g t) -> p g t", g=NDB))
            sb.inc(op, "ds", 16)
            sb.wait(s, "ds")

    def finish_full(self, blk):
        sb = self

        @blk.sync
        def _(s):
            sb.wait(s, "vw")
            sb.wait(s, "pw")
            sb.wait(s, "sw")
            for nm in ["dg", "dg0", "dg1", "dgx", "dgs", "ds",
                       "dgm0", "dgm1", "dso0", "dso1"]:
                sb.wait(s, nm)

    def build(self):
        self.declare()
        with self.nc.Block() as blk:
            self.preload(blk)
            self.ffn1(blk)
            if self.stop_after == "ffn1":
                self.finish(blk)
                return self.nc
            self.inproj(blk)
            if self.stop_after == "inproj":
                self.finish(blk)
                return self.nc
            self.passA_prep(blk)
            self.passA_run(blk)
            self.collective2(blk)
            if self.stop_after == "passA":
                self.finish(blk)
                return self.nc
            self.passB(blk)
            self.finish_full(blk)
        return self.nc


def build_nc(stop_after=None, dbg=False):
    return Bld(stop_after, dbg).build()


KERNEL_STOP_AFTER = None
PASSB_TRUNC = False


def kernel(**inputs):
    shared = prep_shared(inputs)
    in_maps = []
    for c in range(8):
        mm = dict(shared)
        mm.update(prep_core(inputs, c))
        in_maps.append(mm)
    nc = build_nc(stop_after=KERNEL_STOP_AFTER)
    res = run_bass_kernel_spmd(nc, in_maps, core_ids=list(range(8)))
    out = np.zeros((B, L, D), np.float32)
    for c in range(8):
        b, hf = c // 2, c % 2
        out[b, hf * T:(hf + 1) * T, :] = res.results[c]["out_x"].T
    return out


# version 11 (final)
# speedup vs baseline: 1.0122x; 1.0122x over previous
import numpy as np
import ml_dtypes
import concourse.bass as bass
import concourse.mybir as mybir
from concourse.bass_utils import run_bass_kernel_spmd

F32 = mybir.dt.float32
BF16 = mybir.dt.bfloat16
AF = mybir.ActivationFunctionType
ALU = mybir.AluOpType
AX = mybir.AxisListType

B, L, D = 4, 4096, 1024
T = 2048
SUB, NSUB = 512, 4
CH = 128
MAC, NMAC = 256, 8
CPM = MAC // CH
H, K, P = 32, 32, 64
DIN = 2 * D
CONVD = DIN + 4 * K
DPROJ, DPROJP = 4256, 4352
NDB, NIB, NCB, NPB = 8, 16, 17, 34
E = 8
LORA = 48
RMS_EPS, LN_EPS = 1e-5, 1e-5
PIH = float(np.pi / 2)
MAGIC = 12582912.0

ARENA = 144 * 1024

OF = dict(xres=0, xhat=65536, wexp0=98304, wexp1=114688, poolb=131072,
          wip0=139264, wip1=141312, stg=143360)
OA = dict(th=0, dtB=4096, lzr=8192, Rg=12288, Fg=20480,
          rl0=28672, flc=30720, coss=34816, sins=36864, pmag=38912,
          qmag=40960, cer=43008, cei=45056, qr=47104, qin=49152,
          prc=51200, pic=53248, bvr=55296, bvn=57344, crep=59392,
          xd=60416, bvrT=64512, bvnT=66560, xss=68608, ytile=76800,
          xbct=84992,
          qr1=102400, qr2=104448, qr3=106496,
          qin1=108544, qin2=110592, qin3=112640,
          rl=114688, pex=118784, qex=122880)
OB = dict(opw=0, in0=32768, in1=65536,
          cosg=131072, sing=135168, cegr=139264, cegi=143360,
          gt=131072, sq=139264,
          xh2=114688, pb2=135168, sq2=139264)
PRC2, PIC2, CER2, CEI2 = 126976, 129024, 131072, 133120
GM8 = 135168
XD2 = 143360
IB_YT, IB_ZT, IB_RG, IB_FG, IB_XR = 0, 8192, 16384, 20480, 24576
IB_SZ = 32768
DIS_OFF = 93952


def _perm_rows():
    rows = list(range(0, DIN)) + list(range(DIN, 2 * DIN))
    bb = 2 * DIN
    rows += [bb + 2 * j for j in range(K)] + [bb + 2 * j + 1 for j in range(K)]
    cb = 2 * DIN + 2 * K
    rows += [cb + 2 * j for j in range(K)] + [cb + 2 * j + 1 for j in range(K)]
    rows += list(range(DPROJ - H, DPROJ))
    return np.array(rows)


def _col128(vec):
    v = np.asarray(vec)
    n = v.shape[0] // 128
    return np.ascontiguousarray(v.reshape(n, 128).T)


def prep_shared(W):
    d = {}
    perm = _perm_rows()
    wip = np.asarray(W["L0_attn_in_proj_w"])[perm]
    wip = np.concatenate([wip, np.zeros((DPROJP - DPROJ, D), np.float32)], 0)
    d["w_inproj"] = np.ascontiguousarray(wip.T).astype(ml_dtypes.bfloat16)
    xperm = perm[DIN:DIN + CONVD] - DIN
    cw = np.asarray(W["L0_attn_conv_w"])[xperm]
    cb = np.asarray(W["L0_attn_conv_b"])[xperm]
    d["conv_w"] = np.ascontiguousarray(
        cw.reshape(NCB, 128, 4).transpose(1, 0, 2).reshape(128, NCB * 4)).astype(np.float32)
    d["conv_b"] = _col128(cb).astype(np.float32)
    d["w_exp"] = np.ascontiguousarray(
        np.asarray(W["L0_ffn_expert_w"]).transpose(0, 2, 1)).astype(ml_dtypes.bfloat16)
    d["b_exp"] = np.asarray(W["L0_ffn_expert_b"]).astype(np.float32)
    d["w_rout"] = np.ascontiguousarray(np.asarray(W["L0_ffn_router_w"]).T).astype(np.float32)
    d["b_rout"] = np.asarray(W["L0_ffn_router_b"]).reshape(E, 1).astype(np.float32)
    d["w_oproj"] = np.ascontiguousarray(np.asarray(W["L0_attn_out_proj_w"]).T).astype(ml_dtypes.bfloat16)
    d["w_th1"] = np.asarray(W["L0_attn_theta_w1"]).astype(np.float32)
    d["w_th2"] = np.asarray(W["L0_attn_theta_w2"]).astype(np.float32)
    d["thbase"] = _col128(np.asarray(W["L0_attn_theta_base"]).reshape(H * K)).astype(np.float32)
    d["lamb"] = _col128(np.asarray(W["L0_attn_lambda_base"]).reshape(H * K)).astype(np.float32)
    d["eta"] = _col128(np.asarray(W["L0_attn_viscosity_eta"]).reshape(H * K)).astype(np.float32)
    d["dt_bias"] = np.asarray(W["L0_attn_dt_bias"]).reshape(H, 1).astype(np.float32)
    Dv = np.asarray(W["L0_attn_D"])
    dcol = np.zeros((128, NIB), np.float32)
    for j in range(NIB):
        for r in range(128):
            dcol[r, j] = Dv[(128 * j + r) // 64]
    d["Dcol"] = dcol
    drow = np.zeros((4, 1024), np.float32)
    for s in range(4):
        for jj in range(8):
            drow[s, jj * CH:(jj + 1) * CH] = Dv[4 * jj + s]
    d["Drow"] = drow.astype(ml_dtypes.bfloat16)
    d["norm_w"] = _col128(np.asarray(W["L0_attn_norm_w"])).astype(np.float32)

    d["identity"] = np.eye(128, dtype=ml_dtypes.bfloat16)
    m = np.triu(np.ones((128, 128), np.float32))
    d["mask4"] = np.ascontiguousarray(np.tile(m, (1, 4))).astype(ml_dtypes.bfloat16)
    sel = np.zeros((32, 8 * 128), np.float32)
    for gb in range(8):
        for r in range(128):
            sel[4 * gb + r // 32, gb * 128 + r] = 1.0
    d["seldt"] = sel.astype(ml_dtypes.bfloat16)

    def _rep(off):
        mm = np.zeros((128, 128), np.float32)
        for r in range(128):
            mm[off + (r % 32), r] = 1.0
        return mm.astype(ml_dtypes.bfloat16)
    d["repA"], d["repB"] = _rep(0), _rep(32)
    d["repC"], d["repD"] = _rep(64), _rep(96)
    d["ones_col"] = np.ones((128, 1), np.float32)
    d["ones_row"] = np.ones((1, 128), np.float32)
    d["onesb_row"] = np.ones((1, 128), dtype=ml_dtypes.bfloat16)
    d["onesb_col"] = np.ones((128, 1), dtype=ml_dtypes.bfloat16)
    d["i4"] = np.ascontiguousarray(
        np.tile(np.eye(128, dtype=np.float32), (1, 4))).astype(ml_dtypes.bfloat16)
    oh = np.zeros((8, 8 * 128), np.float32)
    for e in range(8):
        oh[e, e * 128:(e + 1) * 128] = 1.0
    d["oh8"] = oh.astype(ml_dtypes.bfloat16)
    d["i8f"] = np.eye(8, dtype=np.float32)
    d["ones8x128"] = np.ones((8, 128), np.float32)
    return d


def prep_core(W, core):
    b, hf = core // 2, core % 2
    x = np.asarray(W["L0_attn_x"])[b, hf * T:(hf + 1) * T, :]
    d = {"x_in": np.ascontiguousarray(x.T).astype(np.float32)}
    sel2 = np.zeros((128, 2), np.float32)
    if hf == 1:
        sel2[:, 0] = 1.0
    d["sel2"] = sel2
    return d


class Bld:
    def __init__(self, stop_after=None, dbg=False):
        self.nc = bass.Bass("TRN2", target_bir_lowering=False, debug=False)
        self.sems, self.cnt = {}, {}
        self.dram, self.sb, self.ps = {}, {}, {}
        self.bankguard = {}
        self.wtok, self.wldtok, self.wbtok = {}, {}, {}
        self.tk2 = {}
        self.stop_after = stop_after
        self.dbg = dbg

    def sem(self, name):
        if name not in self.sems:
            self.sems[name] = self.nc.alloc_semaphore(name)
            self.cnt[name] = 0
        return self.sems[name]

    def inc(self, op, name, k=1):
        op.then_inc(self.sem(name), k)
        self.cnt[name] += k
        return self.cnt[name]

    def tok(self, name):
        self.sem(name)
        return self.cnt[name]

    def wait(self, eng, name, val=None):
        self.sem(name)
        eng.wait_ge(self.sems[name], self.cnt[name] if val is None else val)

    def vsync(self, v, op):
        self.inc(op, "vw")
        self.wait(v, "vw")

    def ssync(self, s, op):
        self.inc(op, "sw")
        self.wait(s, "sw")

    def guard(self, t, psname):
        g = self.bankguard.get(psname)
        if g is not None:
            self.wait(t, g[0], g[1])

    def setguard(self, psname, tag):
        self.bankguard[psname] = (tag, self.tok(tag))

    def ptv(self, c0, c1, parts=128):
        return self.ps["wide"][0:parts, c0 // 2:(c1 + 1) // 2].bitcast(BF16)[:, 0:c1 - c0] \
            if c0 % 2 else self.ps["wide"][0:parts, c0 // 2:(c1 + 1) // 2].bitcast(BF16)[:, 0:c1 - c0]

    def mm(self, t, out, lhsT, rhs, start, stop, lb=0, ob=0):
        if lb == 96 or ob == 96:
            return t.matmul(out, lhsT, rhs, start=start, stop=stop,
                            tile_position=(lb, ob))
        return t.matmul(out, lhsT, rhs, start=start, stop=stop)

    def V(self, off, parts, cols, dtype=F32):
        nb = cols * (4 if dtype == F32 else 2)
        assert off % 4 == 0 and off + nb <= ARENA, (off, nb)
        ap = self.sb["arena"][0:parts, off // 4:(off + ((nb + 3) // 4) * 4) // 4]
        if dtype != F32:
            ap = ap.bitcast(dtype)[:, 0:cols]
        return ap

    def declare(self):
        nc = self.nc

        def di(name, shape, dt=F32):
            self.dram[name] = nc.dram_tensor(name, list(shape), dt,
                                             kind="ExternalInput").ap()
        di("x_in", [D, T]); di("w_inproj", [D, DPROJP], BF16)
        di("conv_w", [128, NCB * 4]); di("conv_b", [128, NCB])
        di("w_exp", [E, D, D], BF16); di("b_exp", [E, D])
        di("w_rout", [D, E]); di("b_rout", [E, 1])
        di("w_oproj", [DIN, D], BF16)
        di("w_th1", [D, LORA]); di("w_th2", [LORA, H * K])
        di("thbase", [128, 8]); di("lamb", [128, 8]); di("eta", [128, 8])
        di("dt_bias", [H, 1]); di("Dcol", [128, NIB]); di("Drow", [4, 1024], BF16)
        di("norm_w", [128, NIB])
        di("identity", [128, 128], BF16); di("mask4", [128, 512], BF16)
        di("seldt", [32, 8 * 128], BF16)
        for nm in ["repA", "repB", "repC", "repD"]:
            di(nm, [128, 128], BF16)
        di("ones_col", [128, 1]); di("ones_row", [1, 128])
        di("onesb_row", [1, 128], BF16); di("onesb_col", [128, 1], BF16)
        di("i4", [128, 512], BF16)
        di("oh8", [8, 8 * 128], BF16)
        di("i8f", [8, 8]); di("ones8x128", [8, 128])
        di("sel2", [128, 2])

        self.dram["out_x"] = nc.dram_tensor("out_x", [D, T], F32,
                                            kind="ExternalOutput").ap()
        skind = "ExternalOutput" if self.dbg else "Internal"

        def sc(name, shape, dt):
            h = nc.dram_tensor(name, list(shape), dt, kind=skind)
            self.dram[name] = h.ap() if skind == "ExternalOutput" else h.ap()
        sc("sc_xr", [128, NDB * T], F32)
        sc("sc_xh", [128, NDB * T], BF16)
        sc("sc_z", [128, NIB * T], BF16)
        sc("sc_xbc", [128, NCB * T], BF16)
        sc("sc_y", [128, NIB * T], BF16)
        sc("sc_rg", [128, 8 * T], BF16)
        sc("sc_fg", [128, 8 * T], BF16)
        self.dram["cc1_in"] = nc.dram_tensor("cc1_in", [128, NCB * 3], F32).ap()
        self.dram["cc1_out"] = nc.dram_tensor("cc1_out", [256, NCB * 3], F32).ap()
        self.dram["cc2_in"] = nc.dram_tensor("cc2_in", [128, 1024], F32).ap()
        self.dram["cc2_out"] = nc.dram_tensor("cc2_out", [256, 1024], F32).ap()

        A = nc.alloc_sbuf_tensor
        S = self.sb
        S["arena"] = A("arena", [128, ARENA // 4], F32)
        S["consts"] = A("consts", [128, 8], F32)
        S["o_row"] = A("o_row", [1, 128], F32)
        S["ob_row"] = A("ob_row", [1, 128], BF16)
        S["cbf"] = A("cbf", [128, 1280], BF16)
        S["i4c"] = A("i4c", [128, 512], BF16)
        S["oh8c"] = A("oh8c", [8, 8 * 128], BF16)
        S["i8f"] = A("i8f_sb", [8, 8], F32)
        S["o8x"] = A("o8x_sb", [8, 128], F32)
        S["selc"] = A("selc", [32, 8 * 128], BF16)
        S["routw"] = A("routw", [128, 8 * 8], BF16)
        S["routwf"] = A("routwf", [128, 8 * 8], F32)
        S["wsb"] = A("wsb", [128, 8], F32)
        S["biasB"] = A("biasB", [128, 8], F32)
        S["routb"] = A("routb", [8, 1], F32)
        S["bexp"] = A("bexp", [8, D], BF16)
        S["dcolb"] = A("dcolb", [128, NIB], F32)
        S["nwcol"] = A("nwcol", [128, NIB], F32)
        S["lamn"] = A("lamn", [128, 8], F32)
        S["etan"] = A("etan", [128, 8], F32)
        S["thb"] = A("thb", [128, 8], F32)
        S["dtbias"] = A("dtbias", [H, 1], F32)
        S["sel2c"] = A("sel2c", [128, 2], F32)
        S["drow"] = A("drow", [4, 1024], BF16)
        S["convw"] = A("convw", [128, NCB * 4], F32)
        S["convb"] = A("convb", [128, NCB], F32)
        S["th1sb"] = A("th1sb", [128, 8 * LORA], BF16)
        S["th2sb"] = A("th2sb", [LORA, H * K], BF16)
        S["gates"] = A("gates", [E, T], BF16)
        S["grow"] = A("grow", [128, 96], F32)
        S["growb"] = A("growb", [128, 32], BF16)
        S["stat"] = A("stat", [1, 2048], F32)
        S["gm"] = A("gm", [128, 512], BF16)
        S["halo"] = A("halo", [128, NCB * 3], F32)
        S["state0"] = A("state0", [128, 1024], F32)
        S["state1"] = A("state1", [128, 1024], F32)
        S["statein"] = A("statein", [128, 1024], F32)
        S["stateb"] = A("stateb", [128, 1024], BF16)
        S["rcarry"] = A("rcarry", [128, 32], F32)
        S["tmpA"] = A("tmpA", [128, 512], F32)
        S["tmpB"] = A("tmpB", [128, 512], F32)
        S["dtb16"] = A("dtb16", [H, T], BF16)
        S["lorab"] = A("lorab", [LORA, T], BF16)
        S["bcall"] = A("bcall", [128, T], BF16)
        S["cols"] = A("cols", [128, 64], F32)
        S["etot"] = A("etot", [128, 32], F32)

        PA = nc.alloc_psum_tensor
        self.ps["st"] = PA("ps_st", [128, 1024], F32)
        self.ps["wide"] = PA("ps_wide", [128, 1024], F32)
        for nm in ["a", "b", "c", "d"]:
            self.ps[nm] = PA("ps_" + nm, [128, 512], F32)

    def preload(self, blk):
        sb, S, Dr = self, self.sb, self.dram

        @blk.sync
        def _(s):
            sb.inc(s.dma_start(
                out=self.V(OF["xres"], 128, NDB * T).rearrange("p (g t) -> p g t", g=NDB),
                in_=Dr["x_in"].rearrange("(g p) t -> p g t", p=128)), "ds", 16)
            for dst, src in [
                (S["consts"][:, 0:1], Dr["ones_col"]),
                (S["o_row"][:, :], Dr["ones_row"]),
                (S["ob_row"][:, :], Dr["onesb_row"]),
                (S["cbf"][:, 0:128], Dr["identity"]),
                (S["cbf"][:, 128:640], Dr["mask4"]),
                (S["cbf"][:, 640:768], Dr["repA"]),
                (S["cbf"][:, 768:896], Dr["repB"]),
                (S["cbf"][:, 896:1024], Dr["repC"]),
                (S["cbf"][:, 1024:1152], Dr["repD"]),
                (S["cbf"][:, 1152:1153], Dr["onesb_col"]),
                (S["i4c"][:, :], Dr["i4"]),
                (S["oh8c"][:, :], Dr["oh8"]),
                (S["i8f"][:, :], Dr["i8f"]),
                (S["o8x"][:, :], Dr["ones8x128"]),
                (S["selc"][:, :], Dr["seldt"]),
                (S["routb"][:, :], Dr["b_rout"]),
                (S["dcolb"][:, :], Dr["Dcol"]),
                (S["nwcol"][:, :], Dr["norm_w"]),
                (S["lamn"][:, :], Dr["lamb"]),
                (S["etan"][:, :], Dr["eta"]),
                (S["thb"][:, :], Dr["thbase"]),
                (S["dtbias"][:, :], Dr["dt_bias"]),
                (S["sel2c"][:, :], Dr["sel2"]),
                (S["drow"][:, :], Dr["Drow"]),
                (S["convw"][:, :], Dr["conv_w"]),
                (S["convb"][:, :], Dr["conv_b"]),
            ]:
                sb.inc(s.dma_start(out=dst, in_=src), "ds", 16)

        @blk.sync
        def _(s):
            sb.inc(s.dma_start(
                out=S["routwf"][:, :].rearrange("p (g e) -> p g e", g=NDB),
                in_=Dr["w_rout"].rearrange("(g p) e -> p g e", p=128)), "ds", 16)

        @blk.gpsimd
        def _(g):
            sb.inc(g.dma_start(
                out=S["routw"][:, :].rearrange("p (g e) -> p g e", g=NDB),
                in_=Dr["w_rout"].rearrange("(g p) e -> p g e", p=128)), "dg", 16)
            sb.inc(g.dma_start(out=S["bexp"][:, :], in_=Dr["b_exp"]), "dg", 16)
            sb.inc(g.dma_start(
                out=S["th1sb"][:, :].rearrange("p (g l) -> p g l", g=NDB),
                in_=Dr["w_th1"].rearrange("(g p) l -> p g l", p=128)), "dg", 16)
            sb.inc(g.dma_start(out=S["th2sb"][:, :], in_=Dr["w_th2"]), "dg", 16)
            sb.inc(g.dma_start(
                out=self.V(OF["wexp0"], 128, 8 * D, BF16).rearrange("p (g f) -> p g f", g=NDB),
                in_=Dr["w_exp"][0].rearrange("(g p) f -> p g f", p=128)), "dg0", 16)
            self.wldtok[0] = self.tok("dg0")

        @blk.vector
        def _(v):
            sb.wait(v, "ds")
            self.vsync(v, v.tensor_scalar_mul(S["lamn"][:, :], S["lamn"][:, :], -1.0))
            self.vsync(v, v.tensor_scalar_mul(S["etan"][:, :], S["etan"][:, :], -1.0))
            sb.inc(v.memset(S["consts"][:, 1:2], PIH), "vw")
            sb.inc(v.memset(S["consts"][:, 2:3], LN_EPS), "vw")
            sb.inc(v.memset(S["consts"][:, 3:4], RMS_EPS), "vw")
            sb.inc(v.memset(S["state0"][:, :], 0.0), "vw")
            sb.inc(v.memset(S["stateb"][:, :], 0.0), "vw")
            sb.inc(v.memset(S["halo"][:, :], 0.0), "vw")
            sb.inc(v.memset(S["rcarry"][:, :], 0.0), "vw")

        self._preload_router(blk)

    def _preload_router(self, blk):
        sb, S = self, self.sb
        PS = self.ps

        @blk.tensor
        def _(t):
            sb.wait(t, "ds")
            self.guard(t, "a")
            for bb in range(NDB):
                op = t.matmul(PS["a"][0:8, 0:1],
                              S["routwf"][:, bb * 8:(bb + 1) * 8],
                              S["consts"][:, 0:1],
                              start=(bb == 0), stop=(bb == NDB - 1))
            sb.inc(op, "pw")

        @blk.vector
        def _(v):
            sb.wait(v, "pw")
            self.vsync(v, v.tensor_scalar(S["tmpB"][0:8, 0:8], S["i8f"][:, :],
                                          PS["a"][0:8, 0:1], None, ALU.mult))
            self.vsync(v, v.tensor_scalar(S["tmpB"][0:8, 8:16], S["i8f"][:, :],
                                          S["routb"][:, 0:1], None, ALU.mult))
        self.setguard("a", "vw")

        @blk.tensor
        def _(t):
            sb.wait(t, "vw")
            self.guard(t, "c")
            t.matmul(PS["c"][:, 0:8], S["o8x"][:, :], S["tmpB"][0:8, 0:8],
                     start=True, stop=True)
            op = t.matmul(PS["c"][:, 8:16], S["o8x"][:, :], S["tmpB"][0:8, 8:16],
                          start=True, stop=True)
            sb.inc(op, "pw")

        @blk.vector
        def _(v):
            sb.wait(v, "pw")
            self.vsync(v, v.tensor_copy(S["wsb"][:, :], PS["c"][:, 0:8]))
            self.vsync(v, v.tensor_copy(S["biasB"][:, :], PS["c"][:, 8:16]))
        self.setguard("c", "vw")

    def ln_stats(self, blk, xoff, c0, width, nblk, sq_off, xcols=None):
        if xcols is None:
            xcols = T
        sb, S, PS = self, self.sb, self.ps

        @blk.scalar
        def _(s):
            sb.wait(s, "vw")
            for bb in range(nblk):
                op = s.activation(
                    self.V(sq_off, 128, nblk * width, BF16)[:, bb * width:(bb + 1) * width],
                    self.V(xoff, 128, nblk * xcols)[
                        :, bb * xcols + c0:bb * xcols + c0 + width],
                    AF.Square)
            sb.inc(op, "sw")

        @blk.tensor
        def _(t):
            sb.wait(t, "sw")
            self.guard(t, "a"); self.guard(t, "b")
            for bb in range(nblk):
                t.matmul(PS["a"][0:1, 0:width], S["consts"][:, 0:1],
                         self.V(xoff, 128, nblk * xcols)[
                             :, bb * xcols + c0:bb * xcols + c0 + width],
                         start=(bb == 0), stop=(bb == nblk - 1))
            for bb in range(nblk):
                op = t.matmul(PS["b"][0:1, 0:width], S["cbf"][:, 1152:1153],
                              self.V(sq_off, 128, nblk * width, BF16)[:, bb * width:(bb + 1) * width],
                              start=(bb == 0), stop=(bb == nblk - 1))
            sb.inc(op, "pw")

        @blk.vector
        def _(v):
            sb.wait(v, "pw")
            n = nblk * 128.0
            st = S["stat"]
            self.vsync(v, v.tensor_scalar_mul(st[:, 0:width], PS["a"][0:1, 0:width], 1.0 / n))
            self.vsync(v, v.tensor_scalar_mul(st[:, 1536:1536 + width],
                                              PS["b"][0:1, 0:width], 1.0 / n))
            self.vsync(v, v.tensor_mul(st[:, 1024:1024 + width],
                                       st[:, 0:width], st[:, 0:width]))
            self.vsync(v, v.tensor_sub(st[:, 1024:1024 + width],
                                       st[:, 1536:1536 + width], st[:, 1024:1024 + width]))
        self.setguard("a", "vw")
        self.setguard("b", "vw")

        @blk.scalar
        def _(s):
            sb.wait(s, "vw")
            st = S["stat"]
            self.ssync(s, s.activation(st[:, 512:512 + width], st[:, 1024:1024 + width],
                                       AF.Ln, bias=S["consts"][0:1, 2:3]))
            self.ssync(s, s.activation(st[:, 512:512 + width], st[:, 512:512 + width],
                                       AF.Exp, scale=-0.5))

    def ln_norm(self, blk, xoff, hatoff, c0, width, nblk, sq_off,
                xcols=None, hatcols=None):
        if xcols is None:
            xcols = T
        if hatcols is None:
            hatcols = T
        sb, S, PS = self, self.sb, self.ps

        @blk.tensor
        def _(t):
            sb.wait(t, "sw")
            self.guard(t, "c"); self.guard(t, "d")
            t.matmul(PS["c"][:, 0:width], S["o_row"][:, :], S["stat"][:, 0:width],
                     start=True, stop=True)
            op = t.matmul(PS["d"][:, 0:width], S["o_row"][:, :],
                          S["stat"][:, 512:512 + width], start=True, stop=True)
            sb.inc(op, "pw")

        @blk.vector
        def _(v):
            sb.wait(v, "pw")
            for bb in range(nblk):
                sb.inc(v.tensor_sub(
                    self.V(sq_off, 128, nblk * width, BF16)[:, bb * width:(bb + 1) * width],
                    self.V(xoff, 128, nblk * xcols)[
                        :, bb * xcols + c0:bb * xcols + c0 + width],
                    PS["c"][:, 0:width]), "vw")
            sb.wait(v, "vw")
            for bb in range(nblk):
                op = v.tensor_mul(
                    self.V(hatoff, 128, nblk * hatcols, BF16)[
                        :, bb * hatcols + c0:bb * hatcols + c0 + width],
                    self.V(sq_off, 128, nblk * width, BF16)[:, bb * width:(bb + 1) * width],
                    PS["d"][:, 0:width])
            sb.inc(op, "vw")
        self.setguard("c", "vw")
        self.setguard("d", "vw")

    def routing(self, blk, c0, width, gbase=0, xoff=None, xcols=None):
        sb, S, PS = self, self.sb, self.ps
        nq = width // 128
        if xoff is None:
            xoff = OF["xres"]
        if xcols is None:
            xcols = T

        @blk.tensor
        def _(t):
            sb.wait(t, "vw")
            sb.wait(t, "sw")
            self.guard(t, "a")
            for bb in range(NDB):
                op = t.matmul(PS["a"][0:8, 0:width],
                              S["routwf"][:, bb * 8:(bb + 1) * 8],
                              self.V(xoff, 128, NDB * xcols)[
                                  :, bb * xcols + c0:bb * xcols + c0 + width],
                              start=(bb == 0), stop=(bb == NDB - 1))
            sb.inc(op, "pw")

        @blk.vector
        def _(v):
            sb.wait(v, "pw")
            self.vsync(v, v.tensor_copy(S["tmpB"][0:8, 0:width], PS["a"][0:8, 0:width]))
        self.setguard("a", "vw")

        @blk.tensor
        def _(t):
            sb.wait(t, "vw")
            self.guard(t, "wide")
            for q in range(nq):
                t.transpose(PS["wide"][:, q * 8:(q + 1) * 8],
                            S["tmpB"][0:8, q * 128:(q + 1) * 128],
                            S["i8f"][:, :])
            for q in range(nq):
                t.transpose(PS["wide"][:, 32 + q:33 + q],
                            S["stat"][0:1, q * 128:(q + 1) * 128],
                            S["consts"][0:1, 0:1])
                op = t.transpose(PS["wide"][:, 36 + q:37 + q],
                                 S["stat"][0:1, 512 + q * 128:512 + (q + 1) * 128],
                                 S["consts"][0:1, 0:1])
            sb.inc(op, "pw")

        @blk.vector
        def _(v):
            sb.wait(v, "pw")
            gw = S["grow"]
            self.vsync(v, v.tensor_copy(gw[:, 0:40], PS["wide"][:, 0:40]))
            lg3 = gw[:, 0:nq * 8].rearrange("p (q e) -> p q e", q=nq)
            m4b = gw[:, 32:32 + nq].rearrange("p (q o) -> p q o", o=1).broadcast_to([128, nq, 8])
            r4b = gw[:, 36:36 + nq].rearrange("p (q o) -> p q o", o=1).broadcast_to([128, nq, 8])
            wsbb = S["wsb"][:, :].rearrange("p (o e) -> p o e", o=1).broadcast_to([128, nq, 8])
            bbb = S["biasB"][:, :].rearrange("p (o e) -> p o e", o=1).broadcast_to([128, nq, 8])
            t1 = gw[:, 40:40 + nq * 8].rearrange("p (q e) -> p q e", q=nq)
            self.vsync(v, v.tensor_tensor(t1, wsbb, m4b, ALU.mult))
            self.vsync(v, v.tensor_tensor(lg3, lg3, t1, ALU.subtract))
            self.vsync(v, v.tensor_tensor(lg3, lg3, r4b, ALU.mult))
            self.vsync(v, v.tensor_tensor(lg3, lg3, bbb, ALU.add))
            self.vsync(v, v.tensor_reduce(
                gw[:, 72:72 + nq].rearrange("p (q o) -> p q o", o=1),
                lg3, AX.X, ALU.max))
            mxb = gw[:, 72:72 + nq].rearrange("p (q o) -> p q o", o=1).broadcast_to([128, nq, 8])
            self.vsync(v, v.tensor_tensor(t1, lg3, mxb, ALU.is_equal))
            self.vsync(v, v.tensor_tensor(lg3, lg3, mxb, ALU.subtract))
        self.setguard("wide", "vw")

        @blk.scalar
        def _(s):
            sb.wait(s, "vw")
            self.ssync(s, s.activation(S["grow"][:, 0:nq * 8], S["grow"][:, 0:nq * 8],
                                       AF.Exp))

        @blk.vector
        def _(v):
            sb.wait(v, "sw")
            gw = S["grow"]
            lg3 = gw[:, 0:nq * 8].rearrange("p (q e) -> p q e", q=nq)
            t1 = gw[:, 40:40 + nq * 8].rearrange("p (q e) -> p q e", q=nq)
            self.vsync(v, v.tensor_reduce(
                gw[:, 76:76 + nq].rearrange("p (q o) -> p q o", o=1),
                lg3, AX.X, ALU.add))
            self.vsync(v, v.reciprocal(gw[:, 76:76 + nq], gw[:, 76:76 + nq]))
            wb = gw[:, 76:76 + nq].rearrange("p (q o) -> p q o", o=1).broadcast_to([128, nq, 8])
            self.vsync(v, v.tensor_tensor(t1, t1, wb, ALU.mult))
            self.vsync(v, v.tensor_scalar_mul(S["growb"][:, 0:nq * 8],
                                              gw[:, 40:40 + nq * 8], 0.5))

        @blk.tensor
        def _(t):
            sb.wait(t, "vw")
            self.guard(t, "wide")
            for q in range(nq):
                op = t.transpose(self.ptv(1024 + q * 128, 1024 + (q + 1) * 128, parts=8),
                                 S["growb"][:, q * 8:(q + 1) * 8],
                                 S["cbf"][:, 0:128])
            sb.inc(op, "pw")

        @blk.vector
        def _(v):
            sb.wait(v, "pw")
            sb.inc(v.tensor_copy(S["gates"][0:8, gbase + c0:gbase + c0 + width],
                                 self.ptv(1024, 1024 + width, parts=8)), "vw")
        self.setguard("wide", "vw")

    def ffn_experts(self, blk, which, hatoff, xroff, c0, width, gbase=0,
                    xr_cols=None, maskoff=None, first_sub=False,
                    hatcols=None):
        if hatcols is None:
            hatcols = T
        sb, S, PS = self, self.sb, self.ps
        Dr = self.dram
        if xr_cols is None:
            xr_cols = T
        if maskoff is None:
            maskoff = OF["poolb"]
        GBALL = 139264

        def bankap(do):
            if width == 512:
                if do < 4:
                    return PS[["a", "b", "c", "d"][do]][:, 0:512]
                if do < 6:
                    return PS["st"][:, (do - 4) * 512:(do - 3) * 512]
                return PS["wide"][:, (do - 6) * 512:(do - 5) * 512]
            if do < 4:
                return PS[["a", "b", "c", "d"][do]][:, 0:width]
            if do < 6:
                return PS["st"][:, (do - 4) * 512:(do - 4) * 512 + width]
            return PS["wide"][:, (do - 6) * 512:(do - 6) * 512 + width]

        npair = 1024 // width
        for eh in range(E // npair):

            @blk.tensor
            def _(t, eh=eh):
                sb.wait(t, "vw")
                self.guard(t, "wide")
                for i in range(npair):
                    e = eh * npair + i
                    op = t.matmul(PS["wide"][:, i * width:(i + 1) * width],
                                  S["oh8c"][:, e * 128:(e + 1) * 128],
                                  S["gates"][0:8, gbase + c0:gbase + c0 + width],
                                  start=True, stop=True)
                sb.inc(op, "pw")
            tk = self.tok("pw")

            @blk.vector
            def _(v, eh=eh, tk=tk):
                sb.wait(v, "pw", tk)
                op = v.tensor_copy(
                    self.V(GBALL, 128, 8 * width, BF16)[
                        :, eh * npair * width:(eh + 1) * npair * width],
                    PS["wide"][:, 0:npair * width])
                sb.inc(op, "vw")
            self.setguard("wide", "vw")

        for e in range(E):
            woff = OF["wexp0"] if e % 2 == 0 else OF["wexp1"]
            nxtoff = OF["wexp1"] if e % 2 == 0 else OF["wexp0"]
            src_e = (e + 1) % E

            @blk.gpsimd
            def _(g, e=e, src_e=src_e, nxtoff=nxtoff):
                sb.wait(g, "pw", self.wtok.get(("wexp", (e + 1) % 2), 0))
                sem = "dg0" if src_e % 2 == 0 else "dg1"
                sb.wait(g, sem)
                nxt = self.V(nxtoff, 128, 8 * D, BF16).rearrange("p (g f) -> p g f", g=NDB)
                op = g.dma_start(out=nxt, in_=Dr["w_exp"][src_e].rearrange(
                    "(g p) f -> p g f", p=128))
                sb.inc(op, sem, 16)
                self.wldtok[src_e] = self.tok(sem)

            vtok = [0, 0]
            for hf in range(2):
                @blk.vector
                def _(v, e=e, hf=hf):
                    sb.wait(v, "vw")
                    if e > 0:
                        sb.wait(v, "pw", self.masktok[hf])
                    hat = self.V(hatoff, 128, NDB * hatcols, BF16)
                    mask = self.V(maskoff, 128, NDB * width, BF16)
                    gb = self.V(GBALL, 128, 8 * width, BF16)[:, e * width:(e + 1) * width]
                    for bb in range(4 * hf, 4 * hf + 4):
                        op = v.tensor_mul(mask[:, bb * width:(bb + 1) * width],
                                          hat[:, bb * hatcols + c0:bb * hatcols + c0 + width],
                                          gb)
                    sb.inc(op, "vw")
                vtok[hf] = self.tok("vw")

            self.masktok = [0, 0]
            for hf in range(2):
                @blk.tensor
                def _(t, e=e, woff=woff, hf=hf, vtok=vtok):
                    sb.wait(t, "vw", vtok[hf])
                    if hf == 0:
                        sb.wait(t, "dg0" if e % 2 == 0 else "dg1", self.wldtok[e])
                    if e == 0 and hf == 0:
                        sb.wait(t, "dg")
                        for nm in ["a", "b", "c", "d", "st", "wide"]:
                            self.guard(t, nm)
                        for do in range(NDB):
                            t.matmul(bankap(do), S["bexp"][:, do * 128:(do + 1) * 128],
                                     S["gates"][0:8, gbase + c0:gbase + c0 + width],
                                     start=True, stop=False)
                    wbuf = self.V(woff, 128, 8 * D, BF16)
                    mask = self.V(maskoff, 128, NDB * width, BF16)
                    for di2 in range(4 * hf, 4 * hf + 4):
                        for do in range(NDB):
                            op = t.matmul(bankap(do),
                                          wbuf[:, di2 * D + do * 128:di2 * D + (do + 1) * 128],
                                          mask[:, di2 * width:(di2 + 1) * width],
                                          start=False,
                                          stop=(e == E - 1 and di2 == NDB - 1))
                    sb.inc(op, "pw")
                    self.masktok[hf] = self.tok("pw")
                if hf == 1:
                    self.wtok[("wexp", e % 2)] = self.tok("pw")

        @blk.vector
        def _(v):
            sb.wait(v, "pw")
            xr = self.V(xroff, 128, NDB * xr_cols)
            for do in range(NDB):
                op = v.tensor_add(xr[:, do * xr_cols + c0:do * xr_cols + c0 + width],
                                  xr[:, do * xr_cols + c0:do * xr_cols + c0 + width],
                                  bankap(do))
            sb.inc(op, "vw")
        for nm in ["a", "b", "c", "d", "st", "wide"]:
            self.setguard(nm, "vw")

    def ffn1(self, blk):
        for j in range(NSUB):
            c0 = j * SUB
            self.ln_stats(blk, OF["xres"], c0, SUB, NDB, OF["poolb"])
            self.ln_norm(blk, OF["xres"], OF["xhat"], c0, SUB, NDB, OF["poolb"])
            self.routing(blk, c0, SUB)
        for j in range(NSUB):
            self.ffn_experts(blk, 1, OF["xhat"], OF["xres"], j * SUB, SUB,
                             first_sub=(j == 0))

    def inproj(self, blk):
        sb, S, PS, Dr = self, self.sb, self.ps, self.dram

        for j in range(NSUB):
            c0 = j * SUB
            self.ln_stats(blk, OF["xres"], c0, SUB, NDB, OF["poolb"])
            self.ln_norm(blk, OF["xres"], OF["xhat"], c0, SUB, NDB, OF["poolb"])

        @blk.sync
        def _(s):
            sb.wait(s, "vw")
            sb.inc(s.dma_start(
                out=Dr["sc_xr"].rearrange("p (g t) -> p g t", g=NDB),
                in_=self.V(OF["xres"], 128, NDB * T).rearrange("p (g t) -> p g t", g=NDB)),
                "ds", 16)

        for j in range(NSUB):
            @blk.tensor
            def _(t, j=j):
                sb.wait(t, "vw")
                sb.wait(t, "dg")
                self.guard(t, "c")
                for bb in range(NDB):
                    op = t.matmul(PS["c"][0:LORA, 0:SUB],
                                  S["th1sb"][:, bb * LORA:(bb + 1) * LORA],
                                  self.V(OF["xhat"], 128, NDB * T, BF16)[
                                      :, bb * T + j * SUB:bb * T + (j + 1) * SUB],
                                  start=(bb == 0), stop=(bb == NDB - 1))
                sb.inc(op, "pw")
            tk = self.tok("pw")

            @blk.scalar
            def _(s, j=j, tk=tk):
                sb.wait(s, "pw", tk)
                sb.inc(s.activation(S["lorab"][:, j * SUB:(j + 1) * SUB],
                                    PS["c"][0:LORA, 0:SUB], AF.Tanh), "sw")
            self.setguard("c", "sw")

        for do in range(NPB):
            woff = OF["wip0"] if do % 2 == 0 else OF["wip1"]

            @blk.gpsimd
            def _(g, do=do, woff=woff):
                sb.wait(g, "pw", self.wtok.get(("wip", do % 2), 0))
                sem = "dg0" if do % 2 == 0 else "dg1"
                sb.wait(g, sem)
                op = g.dma_start(
                    out=self.V(woff, 128, 8 * 128, BF16).rearrange("p (g f) -> p g f", g=NDB),
                    in_=Dr["w_inproj"][:, do * 128:(do + 1) * 128].rearrange(
                        "(g p) f -> p g f", p=128))
                sb.inc(op, sem, 16)
                self.wldtok[("wip", do)] = self.tok(sem)

            for j in range(NSUB):
                bank = ["a", "b"][(do * NSUB + j) % 2]

                @blk.tensor
                def _(t, do=do, j=j, bank=bank, woff=woff):
                    sb.wait(t, "dg0" if do % 2 == 0 else "dg1",
                            self.wldtok[("wip", do)])
                    self.guard(t, bank)
                    if do == 0 and j == 0:
                        sb.wait(t, "vw")
                        sb.wait(t, "sw")
                    w = self.V(woff, 128, 8 * 128, BF16)
                    for bb in range(NDB):
                        op = t.matmul(PS[bank][:, 0:SUB],
                                      w[:, bb * 128:(bb + 1) * 128],
                                      self.V(OF["xhat"], 128, NDB * T, BF16)[
                                          :, bb * T + j * SUB:bb * T + (j + 1) * SUB],
                                      start=(bb == 0), stop=(bb == NDB - 1))
                    sb.inc(op, "pw")
                    if j == NSUB - 1:
                        self.wtok[("wip", do % 2)] = self.tok("pw")
                tk = self.tok("pw")
                stg = (do * NSUB + j) % 4
                stgoff = OF["stg"] + stg * 1024

                if do < NIB:
                    @blk.scalar
                    def _(s, do=do, j=j, bank=bank, tk=tk, stg=stg, stgoff=stgoff):
                        sb.wait(s, "pw", tk)
                        sb.wait(s, "ds")
                        op = s.activation(self.V(stgoff, 128, SUB, BF16),
                                          PS[bank][:, 0:SUB], AF.Silu)
                        sb.inc(op, "sw")
                    self.setguard(bank, "sw")

                    @blk.sync
                    def _(s, do=do, j=j, stg=stg, stgoff=stgoff):
                        sb.wait(s, "sw")
                        op = s.dma_start(
                            out=Dr["sc_z"][:, do * T + j * SUB:do * T + (j + 1) * SUB],
                            in_=self.V(stgoff, 128, SUB, BF16))
                        sb.inc(op, "ds", 16)
                elif do < NIB + NCB:
                    xb = do - NIB

                    @blk.vector
                    def _(v, do=do, j=j, bank=bank, tk=tk, stg=stg, stgoff=stgoff, xb=xb):
                        sb.wait(v, "pw", tk)
                        sb.wait(v, "ds")
                        op = v.tensor_copy(self.V(stgoff, 128, SUB, BF16),
                                           PS[bank][:, 0:SUB])
                        sb.inc(op, "vw")
                        if j == NSUB - 1:
                            op = v.tensor_copy(S["halo"][:, xb * 3:xb * 3 + 3],
                                               PS[bank][:, SUB - 3:SUB])
                            sb.inc(op, "vw")
                    self.setguard(bank, "vw")

                    @blk.sync
                    def _(s, xb=xb, j=j, stg=stg, stgoff=stgoff):
                        sb.wait(s, "vw")
                        op = s.dma_start(
                            out=Dr["sc_xbc"][:, xb * T + j * SUB:xb * T + (j + 1) * SUB],
                            in_=self.V(stgoff, 128, SUB, BF16))
                        sb.inc(op, "ds", 16)
                else:
                    @blk.vector
                    def _(v, j=j, bank=bank, tk=tk):
                        sb.wait(v, "pw", tk)
                        sb.wait(v, "vw")
                        sb.inc(v.tensor_copy(S["tmpA"][0:H, 0:SUB],
                                             PS[bank][0:H, 0:SUB]), "vw")
                    self.setguard(bank, "vw")

                    @blk.scalar
                    def _(s, j=j):
                        sb.wait(s, "vw")
                        tb = S["tmpB"]
                        self.ssync(s, s.activation(tb[0:H, 0:SUB], S["tmpA"][0:H, 0:SUB],
                                                   AF.Abs, bias=S["dtbias"][:, 0:1]))
                        self.ssync(s, s.activation(tb[0:H, 0:SUB], tb[0:H, 0:SUB],
                                                   AF.Exp, scale=-1.0))
                        self.ssync(s, s.activation(tb[0:H, 0:SUB], tb[0:H, 0:SUB],
                                                   AF.Ln, bias=S["consts"][0:H, 0:1]))
                        self.ssync(s, s.activation(S["tmpA"][0:H, 0:SUB],
                                                   S["tmpA"][0:H, 0:SUB],
                                                   AF.Relu, bias=S["dtbias"][:, 0:1]))

                    @blk.vector
                    def _(v, j=j):
                        sb.wait(v, "sw")
                        sb.inc(v.tensor_add(S["dtb16"][:, j * SUB:(j + 1) * SUB],
                                            S["tmpA"][0:H, 0:SUB],
                                            S["tmpB"][0:H, 0:SUB]), "vw")

        @blk.gpsimd
        def _(g):
            sb.wait(g, "vw")
            sb.inc(g.dma_start(out=Dr["cc1_in"], in_=S["halo"][:, :]), "dg", 16)
            sb.wait(g, "dg")
            op = g.collective_compute(
                "AllGather", ALU.bypass,
                replica_groups=[[0, 1], [2, 3], [4, 5], [6, 7]],
                ins=[Dr["cc1_in"].opt()],
                outs=[Dr["cc1_out"].opt()])
            sb.inc(op, "cc")
            sb.wait(g, "cc")
            sb.inc(g.dma_start(out=S["tmpA"][:, 0:NCB * 3],
                               in_=Dr["cc1_out"][0:128, :]), "dg", 16)
            sb.inc(g.dma_start(out=S["tmpB"][:, 0:NCB * 3],
                               in_=Dr["cc1_out"][128:256, :]), "dg", 16)

        @blk.vector
        def _(v):
            sb.wait(v, "dg")
            self.vsync(v, v.tensor_scalar(S["halo"][:, :], S["tmpA"][:, 0:NCB * 3],
                                          S["sel2c"][:, 0:1], None, ALU.mult))
            self.vsync(v, v.scalar_tensor_tensor(
                S["halo"][:, :], S["tmpB"][:, 0:NCB * 3],
                S["sel2c"][:, 1:2], S["halo"][:, :], ALU.mult, ALU.add))

        self.mark("lorab", "sw")
        self.mark("dtb16", "vw")
        self.mark("xbcsp", "ds")

    def passA_prep(self, blk):
        sb, S, PS = self, self.sb, self.ps
        dis = self.V(DIS_OFF, 128, 8 * 512, BF16)

        @blk.vector
        def _(v):
            sb.wait(v, "vw")
            for k in ["qr", "qr1", "qr2", "qr3", "qin", "qin1", "qin2", "qin3"]:
                sb.inc(v.memset(self.V(OA[k], 128, 8 * CH, BF16)[:, :], 0.0), "vw")

        for grp8 in range(8):
            s4, half = grp8 % 4, grp8 // 4
            bank = ["a", "b"][grp8 % 2]

            @blk.tensor
            def _(t, s4=s4, half=half, bank=bank):
                sb.wait(t, "vw")
                self.guard(t, bank)
                op = t.matmul(PS[bank][:, 0:512],
                              S["oh8c"][0:4, s4 * 128:(s4 + 1) * 128],
                              S["drow"][0:4, half * 512:(half + 1) * 512],
                              start=True, stop=True)
                sb.inc(op, "pw")
            tk = self.tok("pw")

            @blk.vector
            def _(v, grp8=grp8, bank=bank, tk=tk, dis=dis):
                sb.wait(v, "pw", tk)
                op = v.tensor_tensor(dis[:, grp8 * 512:(grp8 + 1) * 512],
                                     S["i4c"][:, :], PS[bank][:, 0:512], ALU.mult)
                sb.inc(op, "vw")
            self.setguard(bank, "vw")

    def mark(self, name, sem):
        self.tk2[name] = self.cnt.get(sem, 0)

    def wtk(self, eng, sem, name):
        self.sem(sem)
        eng.wait_ge(self.sems[sem], self.tk2.get(name, 0))

    def chvars(self, cidx):
        alt = cidx % 2
        d = {}
        d["prc"] = self.V(OA["prc"] if alt == 0 else PRC2, 128, 8 * CH, BF16)
        d["pic"] = self.V(OA["pic"] if alt == 0 else PIC2, 128, 8 * CH, BF16)
        d["cer"] = self.V(OA["cer"] if alt == 0 else CER2, 128, 8 * CH, BF16)
        d["cei"] = self.V(OA["cei"] if alt == 0 else CEI2, 128, 8 * CH, BF16)
        d["qrv"] = [self.V(OA[k2], 128, 8 * CH, BF16)
                    for k2 in ["qr", "qr1", "qr2", "qr3"]]
        d["qinv"] = [self.V(OA[k2], 128, 8 * CH, BF16)
                     for k2 in ["qin", "qin1", "qin2", "qin3"]]
        d["bvr"] = self.V(OA["bvr"], 128, 8 * CH, BF16)
        d["bvn"] = self.V(OA["bvn"], 128, 8 * CH, BF16)
        d["bvrT"] = self.V(OA["bvrT"], 128, 8 * CH, BF16)
        d["bvnT"] = self.V(OA["bvnT"], 128, 8 * CH, BF16)
        d["coss"] = self.V(OA["coss"], 128, 8 * CH, BF16)
        d["sins"] = self.V(OA["sins"], 128, 8 * CH, BF16)
        d["pmag"] = self.V(OA["pmag"], 128, 8 * CH, BF16)
        d["qmag"] = self.V(OA["qmag"], 128, 8 * CH, BF16)
        d["pex"] = self.V(OA["pex"], 128, 8 * CH)
        d["qex"] = self.V(OA["qex"], 128, 8 * CH)
        d["rl"] = self.V(OA["rl"], 128, 8 * CH)
        d["flc"] = self.V(OA["flc"], 128, 8 * CH)
        d["crep"] = self.V(OA["crep"], 128, 512, BF16)
        d["xd"] = self.V(OA["xd"] if alt == 0 else XD2, 128, NIB * CH, BF16)
        d["xss"] = self.V(OA["xss"], 128, NIB * MAC, BF16)
        d["ytile"] = self.V(OA["ytile"], 128, NIB * MAC, BF16)
        d["dtB"] = self.V(OA["dtB"], 128, 8 * MAC, BF16)
        d["wcrB"] = self.V(OA["lzr"], 128, 8 * CH, BF16)
        d["wciB"] = self.V(OA["lzr"] + 2048, 128, 8 * CH, BF16)
        return d

    def _xbct_v(self):
        return self.V(OA["xbct"], 128, NCB * 260, BF16).rearrange(
            "p (c w) -> p c w", c=NCB)

    def pA_xbct(self, blk, m):
        sb, Dr = self, self.dram
        c0 = m * MAC
        xbct = self._xbct_v

        @blk.sync
        def _(g):
            self.wtk(g, "vw", ("conv", m - 1))
            self.wtk(g, "ds", "xbcsp")
            op = g.dma_start(
                out=xbct()[:, :, 3:3 + MAC],
                in_=Dr["sc_xbc"].rearrange("p (c t) -> p c t", c=NCB)[:, :, c0:c0 + MAC])
            sb.inc(op, "dgx", 16)
        self.mark(("xbct", m), "dgx")

    def pA_conv(self, blk, m):
        sb, S, PS, Dr = self, self.sb, self.ps, self.dram
        c0 = m * MAC
        xbct = self._xbct_v

        @blk.vector
        def _(v):
            sb.wait(v, "vw")
            sb.wait(v, "dgx", self.tk2[("xbct", m)])
            if m == 0:
                self.vsync(v, v.tensor_copy(
                    xbct()[:, :, 0:3],
                    S["halo"][:, :].rearrange("p (c w) -> p c w", c=NCB)))
            xb = xbct()
            for cb in range(NCB):
                acc = (S["tmpA"][:, 256:256 + MAC] if cb % 2 == 0
                       else S["tmpB"][:, 0:MAC])
                if cb >= 2:
                    self.wtk(v, "sw", ("silu", m, cb - 2))
                sb.inc(v.tensor_scalar(acc, xb[:, cb, 0:MAC],
                                       S["convw"][:, cb * 4:cb * 4 + 1],
                                       S["convb"][:, cb:cb + 1], ALU.mult, ALU.add), "vw")
                sb.wait(v, "vw")
                for k2 in range(1, 4):
                    sb.inc(v.scalar_tensor_tensor(
                        acc, xb[:, cb, k2:k2 + MAC],
                        S["convw"][:, cb * 4 + k2:cb * 4 + k2 + 1],
                        acc, ALU.mult, ALU.add), "vw")
                    sb.wait(v, "vw")
                tk = self.tok("vw")

                @blk.scalar
                def _(s, cb=cb, tk=tk, acc=acc):
                    sb.wait(s, "vw", tk)
                    if cb < NIB:
                        self.wtk(s, "pw", ("xdT", 2 * m - 1))
                        dst = self.V(OA["xss"], 128, NIB * MAC, BF16)[
                            :, cb * MAC:(cb + 1) * MAC]
                    else:
                        self.wtk(s, "pw", ("crepF", 2 * m - 1))
                        dst = S["bcall"][:, c0:c0 + MAC]
                    sb.inc(s.activation(dst, acc, AF.Silu), "sw")
                    self.mark(("silu", m, cb), "sw")
            op = v.tensor_copy(xb[:, :, 0:3], xb[:, :, MAC:MAC + 3])
            sb.inc(op, "vw")
        self.mark(("conv", m), "vw")
        self.mark(("convS", m), "sw")

    def pA_theta_scans(self, blk, m):
        sb, S, PS, Dr = self, self.sb, self.ps, self.dram
        c0 = m * MAC

        for half in range(4):
            @blk.tensor
            def _(t, half=half):
                self.wtk(t, "sw", "lorab")
                self.guard(t, "wideA")
                for q in range(4):
                    idx = half * 4 + q
                    gb, jj = idx // 2, idx % 2
                    op = t.matmul(PS["wide"][:, q * 128:(q + 1) * 128],
                                  S["th2sb"][:, gb * 128:(gb + 1) * 128],
                                  S["lorab"][:, c0 + jj * 128:c0 + (jj + 1) * 128],
                                  start=True, stop=True)
                sb.inc(op, "pw")
            tk = self.tok("pw")

            @blk.vector
            def _(v, half=half, tk=tk):
                sb.wait(v, "pw", tk)
                sb.wait(v, "vw")
                th = self.V(OA["th"], 128, 8 * MAC, BF16)
                for q in range(4):
                    idx = half * 4 + q
                    gb, jj = idx // 2, idx % 2
                    sb.inc(v.tensor_scalar(
                        th[:, gb * MAC + jj * 128:gb * MAC + (jj + 1) * 128],
                        PS["wide"][:, q * 128:(q + 1) * 128],
                        S["thb"][:, gb:gb + 1], PIH, ALU.add, ALU.min), "vw")
                sb.wait(v, "vw")
                for q in range(4):
                    idx = half * 4 + q
                    gb, jj = idx // 2, idx % 2
                    op = v.tensor_scalar_max(
                        th[:, gb * MAC + jj * 128:gb * MAC + (jj + 1) * 128],
                        th[:, gb * MAC + jj * 128:gb * MAC + (jj + 1) * 128], -PIH)
                sb.inc(op, "vw")
            self.setguard("wideA", "vw")
        self.mark(("thclip", m), "vw")

        for half in range(4):
            @blk.tensor
            def _(t, half=half):
                self.wtk(t, "vw", "dtb16")
                self.guard(t, "wideA")
                for q in range(2):
                    gb = half * 2 + q
                    op = t.matmul(PS["wide"][:, q * MAC:(q + 1) * MAC],
                                  S["selc"][:, gb * 128:(gb + 1) * 128],
                                  S["dtb16"][:, c0:c0 + MAC],
                                  start=True, stop=True)
                sb.inc(op, "pw")
            tk = self.tok("pw")

            @blk.scalar
            def _(s, half=half, tk=tk):
                sb.wait(s, "pw", tk)
                self.wtk(s, "vw", ("bldA", 2 * m - 1))
                dtB = self.V(OA["dtB"], 128, 8 * MAC, BF16)
                op = s.activation(dtB[:, half * 2 * MAC:(half + 1) * 2 * MAC],
                                  PS["wide"][:, 0:2 * MAC], AF.Copy)
                sb.inc(op, "sw")
            self.setguard("wideA", "sw")
        self.mark(("dtBS", m), "sw")

        @blk.scalar
        def _(s):
            self.wtk(s, "vw", ("thclip", m))
            self.wtk(s, "dgs", ("spill", m - 1))
            th = self.V(OA["th"], 128, 8 * MAC, BF16)
            t2 = self.V(OA["Rg"], 128, 8 * MAC)
            sb.inc(s.activation(t2[:, :], th[:, :], AF.Square), "sw")
        self.mark(("th2", m), "sw")

        @blk.vector
        def _(v):
            self.wtk(v, "sw", ("th2", m))
            self.wtk(v, "sw", ("dtBS", m))
            self.wtk(v, "sw", ("exp", 2 * m - 1))
            self.wtk(v, "dgs", ("spill", m - 1))
            sb.wait(v, "vw")
            t2 = self.V(OA["Rg"], 128, 8 * MAC)
            lzr = self.V(OA["lzr"], 128, 8 * MAC, BF16)
            dtB = self.V(OA["dtB"], 128, 8 * MAC, BF16)
            th = self.V(OA["th"], 128, 8 * MAC, BF16)
            for gb in range(8):
                sb.inc(v.tensor_scalar(lzr[:, gb * MAC:(gb + 1) * MAC],
                                       t2[:, gb * MAC:(gb + 1) * MAC],
                                       S["etan"][:, gb:gb + 1], S["lamn"][:, gb:gb + 1],
                                       ALU.mult, ALU.add), "vw")
            sb.wait(v, "vw")
            self.vsync(v, v.tensor_mul(lzr[:, :], lzr[:, :], dtB[:, :]))
            self.vsync(v, v.tensor_mul(th[:, :], th[:, :], dtB[:, :]))
            Rg = self.V(OA["Rg"], 128, 8 * MAC)
            Fg = self.V(OA["Fg"], 128, 8 * MAC)
            zed = S["tmpA"]
            sb.inc(v.memset(zed[:, 0:MAC], 0.0), "vw")
            sb.wait(v, "vw")
            for gb in range(8):
                sb.inc(v.tensor_tensor_scan(Rg[:, gb * MAC:(gb + 1) * MAC],
                                            lzr[:, gb * MAC:(gb + 1) * MAC],
                                            zed[:, 0:MAC],
                                            S["rcarry"][:, gb:gb + 1] if m > 0 else 0.0,
                                            ALU.add, ALU.add), "vw")
                sb.inc(v.tensor_tensor_scan(Fg[:, gb * MAC:(gb + 1) * MAC],
                                            th[:, gb * MAC:(gb + 1) * MAC],
                                            zed[:, 0:MAC],
                                            S["rcarry"][:, 8 + gb:9 + gb] if m > 0 else 0.0,
                                            ALU.add, ALU.add), "vw")
            sb.wait(v, "vw")
            self.vsync(v, v.tensor_copy(S["rcarry"][:, 0:8],
                          Rg.rearrange("p (g w) -> p g w", g=8)[:, :, MAC - 1]))
            self.vsync(v, v.tensor_copy(S["rcarry"][:, 8:16],
                               Fg.rearrange("p (g w) -> p g w", g=8)[:, :, MAC - 1]))
        rgtok = self.tok("vw")

        @blk.gpsimd
        def _(g):
            sb.wait(g, "vw", rgtok)
            Rg = self.V(OA["Rg"], 128, 8 * MAC)
            op = g.dma_start(
                out=Dr["sc_rg"].rearrange("p (g t) -> p g t", g=8)[:, :, c0:c0 + MAC],
                in_=Rg.rearrange("p (g w) -> p g w", g=8))
            sb.inc(op, "dgs", 16)

        @blk.vector
        def _(v):
            sb.wait(v, "vw")
            Fg = self.V(OA["Fg"], 128, 8 * MAC)
            tmp = self.V(OA["lzr"], 128, 4 * MAC)
            HW = 4 * MAC
            for hh in range(2):
                sl = slice(hh * HW, (hh + 1) * HW)
                self.vsync(v, v.tensor_scalar(tmp[:, 0:HW], Fg[:, sl],
                                              1.0 / (2 * np.pi), MAGIC,
                                              ALU.mult, ALU.add))
                self.vsync(v, v.tensor_scalar(tmp[:, 0:HW], tmp[:, 0:HW],
                                              MAGIC, None, ALU.subtract))
                self.vsync(v, v.scalar_tensor_tensor(Fg[:, sl], tmp[:, 0:HW],
                                                     -2 * np.pi, Fg[:, sl],
                                                     ALU.mult, ALU.add))
        fgtok = self.tok("vw")

        @blk.gpsimd
        def _(g):
            sb.wait(g, "vw", fgtok)
            Fg = self.V(OA["Fg"], 128, 8 * MAC)
            op = g.dma_start(
                out=Dr["sc_fg"].rearrange("p (g t) -> p g t", g=8)[:, :, c0:c0 + MAC],
                in_=Fg.rearrange("p (g w) -> p g w", g=8))
            sb.inc(op, "dgs", 16)
        self.mark(("spill", m), "dgs")

    def pA_buildA(self, blk, cidx):
        sb, S, PS, Dr = self, self.sb, self.ps, self.dram
        m, pc = cidx // CPM, cidx % CPM
        w0 = pc * CH
        cv = self.chvars(cidx)
        Rg = self.V(OA["Rg"], 128, 8 * MAC)
        Fg = self.V(OA["Fg"], 128, 8 * MAC)
        rl, flc = cv["rl"], cv["flc"]
        coss, sins = cv["coss"], cv["sins"]
        pmag, qmag = cv["pmag"], cv["qmag"]
        pex, qex = cv["pex"], cv["qex"]
        prc, pic, cer, cei = cv["prc"], cv["pic"], cv["cer"], cv["cei"]
        bvr, bvn, crep = cv["bvr"], cv["bvn"], cv["crep"]
        qrv, qinv, dtB = cv["qrv"], cv["qinv"], cv["dtB"]
        cl = S["cols"]
        co = 32 * pc

        @blk.vector
        def _(v):
            sb.wait(v, "vw")
            self.wtk(v, "sw", ("exp", cidx - 1))
            r3m = Rg.rearrange("p (g w) -> p g w", g=8)
            f3m = Fg.rearrange("p (g w) -> p g w", g=8)
            for gb in range(8):
                if cidx == 0:
                    sb.inc(v.tensor_copy(rl[:, gb * CH:(gb + 1) * CH],
                                         Rg[:, gb * MAC + w0:gb * MAC + w0 + CH]), "vw")
                    sb.inc(v.tensor_copy(flc[:, gb * CH:(gb + 1) * CH],
                                         Fg[:, gb * MAC + w0:gb * MAC + w0 + CH]), "vw")
                else:
                    roff = (S["rcarry"][:, 16 + gb:17 + gb] if pc == 0
                            else r3m[:, gb, w0 - 1:w0])
                    foff = (S["rcarry"][:, 24 + gb:25 + gb] if pc == 0
                            else f3m[:, gb, w0 - 1:w0])
                    sb.inc(v.tensor_scalar(rl[:, gb * CH:(gb + 1) * CH],
                                           Rg[:, gb * MAC + w0:gb * MAC + w0 + CH],
                                           roff, None, ALU.subtract), "vw")
                    sb.inc(v.tensor_scalar(flc[:, gb * CH:(gb + 1) * CH],
                                           Fg[:, gb * MAC + w0:gb * MAC + w0 + CH],
                                           foff, None, ALU.subtract), "vw")
            self.mark(("Aflc", cidx), "vw")
            sb.wait(v, "vw")
            if pc == CPM - 1:
                self.vsync(v, v.tensor_copy(S["rcarry"][:, 16:24], r3m[:, :, MAC - 1]))
                self.vsync(v, v.tensor_copy(S["rcarry"][:, 24:32], f3m[:, :, MAC - 1]))
            r3 = rl.rearrange("p (g w) -> p g w", g=8)
            tA = S["tmpA"]
            self.vsync(v, v.memset(tA[:, 32:40], 0.0))
            self.vsync(v, v.tensor_copy(
                tA[:, 40:64].rearrange("p (i g) -> p g i", i=3),
                r3[:, :, 31:CH - 1:32]))
            self.vsync(v, v.tensor_tensor(
                tA[:, 64:88].rearrange("p (i g) -> p g i", i=3),
                tA[:, 40:64].rearrange("p (i g) -> p g i", i=3),
                tA[:, 32:56].rearrange("p (i g) -> p g i", i=3), ALU.subtract))
            p3 = pex.rearrange("p (g w) -> p g w", g=8)
            for i in range(4):
                sb.inc(v.tensor_tensor(
                    p3[:, :, 32 * i:32 * i + 32], r3[:, :, 32 * i:32 * i + 32],
                    tA[:, 32 + 8 * i:40 + 8 * i].rearrange(
                        "p (g o) -> p g o", o=1).broadcast_to([128, 8, 32]),
                    ALU.subtract), "vw")
            sb.wait(v, "vw")
            op = v.tensor_scalar(qex[:, :], pex[:, :], -1.0, 87.0,
                                 ALU.mult, ALU.min)
            sb.inc(op, "vw")
        self.mark(("Aall", cidx), "vw")

        @blk.scalar
        def _(s):
            self.wtk(s, "vw", ("Aflc", cidx))
            self.wtk(s, "vw", ("bldA", cidx - 1))
            sb.wait(s, "sw")
            self.ssync(s, s.activation(coss[:, :], flc[:, :], AF.Sin, scale=0.25))
            self.ssync(s, s.activation(sins[:, :], flc[:, :], AF.Sin, scale=0.25,
                                       bias=S["consts"][:, 1:2]))
        self.mark(("trig", cidx), "sw")

        @blk.vector
        def _(v):
            self.wtk(v, "sw", ("trig", cidx))
            self.wtk(v, "pw", ("Yl", cidx - 2))
            sb.wait(v, "vw")
            self.vsync(v, v.tensor_mul(pmag[:, :], coss[:, :], coss[:, :]))
            self.vsync(v, v.tensor_mul(qmag[:, :], coss[:, :], sins[:, :]))
            self.vsync(v, v.tensor_scalar(cer[:, :], pmag[:, :], -2.0, 1.0,
                                          ALU.mult, ALU.add))
            self.vsync(v, v.tensor_scalar_mul(cei[:, :], qmag[:, :], 2.0))
            self.vsync(v, v.tensor_mul(pmag[:, :], cei[:, :], cei[:, :]))
            self.vsync(v, v.tensor_mul(qmag[:, :], cei[:, :], cer[:, :]))
            self.vsync(v, v.tensor_scalar(coss[:, :], pmag[:, :], -2.0, 1.0,
                                          ALU.mult, ALU.add))
            self.vsync(v, v.tensor_scalar_mul(sins[:, :], qmag[:, :], 2.0))
        self.mark(("dbl", cidx), "vw")

        @blk.scalar
        def _(s):
            self.wtk(s, "vw", ("Aall", cidx))
            self.wtk(s, "vw", ("dbl", cidx))
            self.wtk(s, "vw", ("bldB", cidx - 1))
            sb.wait(s, "sw")
            self.ssync(s, s.activation(pmag[:, :], pex[:, :], AF.Exp))
            self.ssync(s, s.activation(qmag[:, :], qex[:, :], AF.Exp))
            self.ssync(s, s.activation(S["tmpA"][:, 64:88], S["tmpA"][:, 64:88],
                                       AF.Exp))
            self.ssync(s, s.activation(S["tmpA"][:, 96:128], S["tmpA"][:, 32:64],
                                       AF.Exp))
        self.mark(("exp", cidx), "sw")

        @blk.vector
        def _(v):
            self.wtk(v, "sw", ("exp", cidx))
            sb.wait(v, "vw")
            c3 = coss.rearrange("p (g w) -> p g w", g=8)
            s3 = sins.rearrange("p (g w) -> p g w", g=8)
            e3 = pmag.rearrange("p (g w) -> p g w", g=8)
            o1 = lambda x: x.rearrange("p (g o) -> p g o", o=1)
            tmp = S["tmpA"]
            self.vsync(v, v.tensor_tensor(o1(cl[:, co + 24:co + 32]),
                                          e3[:, :, CH - 1:CH],
                                          c3[:, :, CH - 1:CH], ALU.mult))
            self.vsync(v, v.tensor_tensor(o1(tmp[:, 8:16]),
                                          e3[:, :, CH - 1:CH],
                                          s3[:, :, CH - 1:CH], ALU.mult))
            et = S["etot"]
            self.vsync(v, v.tensor_mul(et[:, pc * 16:pc * 16 + 8],
                                       cl[:, co + 24:co + 32], tmp[:, 120:128]))
            self.vsync(v, v.tensor_mul(et[:, pc * 16 + 8:pc * 16 + 16],
                                       tmp[:, 8:16], tmp[:, 120:128]))

        @blk.tensor
        def _(t):
            self.wtk(t, "sw", ("convS", m))
            self.guard(t, "wideB")
            bc = S["bcall"][:, m * MAC + w0:m * MAC + w0 + CH]
            t.matmul(PS["wide"][:, 512:640], S["cbf"][:, 640:768], bc[:, :],
                     start=True, stop=True)
            t.matmul(PS["wide"][:, 640:768], S["cbf"][:, 768:896], bc[:, :],
                     start=True, stop=True)
            t.matmul(PS["wide"][:, 768:896], S["cbf"][:, 896:1024], bc[:, :],
                     start=True, stop=True)
            op = t.matmul(PS["wide"][:, 896:1024], S["cbf"][:, 1024:1152],
                          bc[:, :], start=True, stop=True)
            sb.inc(op, "pw")
        self.mark(("crepF", cidx), "pw")

        @blk.scalar
        def _(s):
            sb.wait(s, "pw", self.tk2[("crepF", cidx)])
            self.wtk(s, "vw", ("bldA", cidx - 1))
            op = s.activation(crep[:, :], PS["wide"][:, 512:1024], AF.Copy)
            sb.inc(op, "sw")
        self.mark(("crepS", cidx), "sw")
        self.setguard("wideB", "sw")

        @blk.vector
        def _(v):
            sb.wait(v, "vw")
            self.wtk(v, "sw", ("exp", cidx))
            self.wtk(v, "sw", ("crepS", cidx))
            self.wtk(v, "sw", ("dtBS", m))
            self.wtk(v, "pw", ("Gl", cidx - 2))
            self.wtk(v, "pw", ("Yl", cidx - 2))
            self.wtk(v, "pw", ("bvT", cidx - 1))
            BrR, BiR = crep[:, 0:CH], crep[:, CH:2 * CH]
            CrR, CiR = crep[:, 2 * CH:3 * CH], crep[:, 3 * CH:4 * CH]

            def rep3(x):
                return x.rearrange("p (o w) -> p o w", o=1).broadcast_to([128, 8, CH])
            g3 = lambda x: x.rearrange("p (g w) -> p g w", g=8)
            tmp = S["tmpA"]
            sb.inc(v.tensor_mul(cer[:, :], pmag[:, :], coss[:, :]), "vw")
            sb.inc(v.tensor_mul(cei[:, :], pmag[:, :], sins[:, :]), "vw")
            sb.wait(v, "vw")
            sb.inc(v.tensor_tensor(g3(prc), g3(cer), rep3(CrR), ALU.mult), "vw")
            sb.inc(v.tensor_tensor(g3(bvr), g3(cei), rep3(CiR), ALU.mult), "vw")
            sb.inc(v.tensor_tensor(g3(pic), g3(cei), rep3(CrR), ALU.mult), "vw")
            sb.inc(v.tensor_tensor(g3(bvn), g3(cer), rep3(CiR), ALU.mult), "vw")
            sb.wait(v, "vw")
            sb.inc(v.tensor_sub(prc[:, :], prc[:, :], bvr[:, :]), "vw")
            sb.inc(v.tensor_add(pic[:, :], pic[:, :], bvn[:, :]), "vw")
            dt3 = dtB.rearrange("p (g w) -> p g w", g=8)[:, :, w0:w0 + CH]
            sb.inc(v.tensor_tensor(g3(qmag), g3(qmag), dt3, ALU.mult), "vw")
            sb.wait(v, "vw")
            p3r, p3i = g3(prc), g3(pic)
            ce3r, ce3i = g3(cer), g3(cei)
            for i in range(4):
                ebcol = tmp[:, 96 + 8 * i:104 + 8 * i].rearrange(
                    "p (g o) -> p g o", o=1).broadcast_to([128, 8, 32])
                sb.inc(v.tensor_tensor(ce3r[:, :, 32 * i:32 * i + 32],
                                       p3r[:, :, 32 * i:32 * i + 32], ebcol,
                                       ALU.mult), "vw")
                sb.inc(v.tensor_tensor(ce3i[:, :, 32 * i:32 * i + 32],
                                       p3i[:, :, 32 * i:32 * i + 32], ebcol,
                                       ALU.mult), "vw")
            ur, ui = cv["wcrB"], cv["wciB"]
            sb.inc(v.tensor_tensor(g3(bvr), g3(coss), rep3(BrR), ALU.mult), "vw")
            sb.inc(v.tensor_tensor(g3(ur), g3(sins), rep3(BiR), ALU.mult), "vw")
            sb.inc(v.tensor_tensor(g3(bvn), g3(sins), rep3(BrR), ALU.mult), "vw")
            sb.inc(v.tensor_tensor(g3(ui), g3(coss), rep3(BiR), ALU.mult), "vw")
            sb.wait(v, "vw")
            sb.inc(v.tensor_add(bvr[:, :], bvr[:, :], ur[:, :]), "vw")
            sb.inc(v.tensor_sub(bvn[:, :], bvn[:, :], ui[:, :]), "vw")
        self.mark(("bldA", cidx), "vw")

    def pA_buildB(self, blk, cidx):
        sb, S, PS = self, self.sb, self.ps
        m, pc = cidx // CPM, cidx % CPM
        cv = self.chvars(cidx)
        qrv, qinv = cv["qrv"], cv["qinv"]
        bvr, bvn = cv["bvr"], cv["bvn"]
        qmag, pmag = cv["qmag"], cv["pmag"]
        wcrB, wciB = cv["wcrB"], cv["wciB"]
        cl = S["cols"]
        co = 32 * pc

        @blk.vector
        def _(v):
            sb.wait(v, "vw")
            self.wtk(v, "pw", ("Gl", cidx - 1))
            g3 = lambda x: x.rearrange("p (g w) -> p g w", g=8)
            tmp = S["tmpA"]
            q3m = g3(qmag)
            for i in range(4):
                sb.inc(v.tensor_tensor(
                    g3(qrv[i])[:, :, 32 * i:32 * i + 32],
                    g3(bvr)[:, :, 32 * i:32 * i + 32],
                    q3m[:, :, 32 * i:32 * i + 32], ALU.mult), "vw")
                sb.inc(v.tensor_tensor(
                    g3(qinv[i])[:, :, 32 * i:32 * i + 32],
                    g3(bvn)[:, :, 32 * i:32 * i + 32],
                    q3m[:, :, 32 * i:32 * i + 32], ALU.mult), "vw")
            sb.wait(v, "vw")
            for i in range(1, 4):
                edcol = tmp[:, 64 + 8 * (i - 1):72 + 8 * (i - 1)].rearrange(
                    "p (g o) -> p g o", o=1).broadcast_to([128, 8, 32 * i])
                sb.inc(v.tensor_tensor(
                    g3(qrv[i])[:, :, 0:32 * i], g3(qrv[i - 1])[:, :, 0:32 * i],
                    edcol, ALU.mult), "vw")
                sb.inc(v.tensor_tensor(
                    g3(qinv[i])[:, :, 0:32 * i], g3(qinv[i - 1])[:, :, 0:32 * i],
                    edcol, ALU.mult), "vw")
                sb.wait(v, "vw")
            sb.inc(v.tensor_copy(g3(wcrB), cl[:, co + 24:co + 32].rearrange(
                "p (g o) -> p g o", o=1).broadcast_to([128, 8, CH])), "vw")
            sb.inc(v.tensor_copy(g3(wciB), tmp[:, 8:16].rearrange(
                "p (g o) -> p g o", o=1).broadcast_to([128, 8, CH])), "vw")
            sb.wait(v, "vw")
            sb.inc(v.tensor_mul(bvr[:, :], qrv[3][:, :], wcrB[:, :]), "vw")
            sb.inc(v.tensor_mul(pmag[:, :], qinv[3][:, :], wciB[:, :]), "vw")
            sb.inc(v.tensor_mul(bvn[:, :], qinv[3][:, :], wcrB[:, :]), "vw")
            sb.inc(v.tensor_mul(qmag[:, :], qrv[3][:, :], wciB[:, :]), "vw")
            sb.wait(v, "vw")
            sb.inc(v.tensor_add(bvr[:, :], bvr[:, :], pmag[:, :]), "vw")
            sb.inc(v.tensor_sub(bvn[:, :], bvn[:, :], qmag[:, :]), "vw")
        self.mark(("bldB", cidx), "vw")

    def pA_gyG(self, blk, cidx):
        sb, S, PS = self, self.sb, self.ps
        m, pc = cidx // CPM, cidx % CPM
        w0 = pc * CH
        cv = self.chvars(cidx)
        xss, xd = cv["xss"], cv["xd"]
        bvr, bvn = cv["bvr"], cv["bvn"]
        bvrT, bvnT = cv["bvrT"], cv["bvnT"]

        for src, dst in [(bvr, bvrT), (bvn, bvnT)]:
            for grp in range(2):
                @blk.tensor
                def _(t, src=src, grp=grp):
                    self.wtk(t, "vw", ("bldB", cidx))
                    self.guard(t, "wideA")
                    for q in range(4):
                        idx = grp * 4 + q
                        op = t.transpose(self.ptv(q * CH, (q + 1) * CH),
                                         src[:, idx * CH:(idx + 1) * CH],
                                         S["cbf"][:, 0:128])
                    sb.inc(op, "pw")
                tk = self.tok("pw")

                @blk.scalar
                def _(s, dst=dst, grp=grp, tk=tk):
                    sb.wait(s, "pw", tk)
                    self.wtk(s, "pw", ("Sinc", cidx - 1))
                    op = s.activation(dst[:, grp * 512:(grp + 1) * 512],
                                      self.ptv(0, 512), AF.Copy)
                    sb.inc(op, "sw")
                self.setguard("wideA", "sw")
        self.mark(("bvT", cidx), "pw")
        self.mark(("bvTS", cidx), "sw")

        if pc == 0:
            self.pA_xdT(blk, cidx)
            self.pA_xdT(blk, cidx + 1)

        @blk.tensor
        def _(t):
            self.wtk(t, "sw", ("xdS", cidx))
            self.wtk(t, "sw", ("bvTS", cidx))
            self.guard(t, "st")
            for jq in range(8):
                for s4 in range(4):
                    h = 4 * jq + s4
                    base = (h // 4) * CH + (h % 4) * 32
                    lr = bvrT[:, base:base + 32]
                    ln2 = bvnT[:, base:base + 32]
                    rhs = xd[:, h * 64:(h + 1) * 64]
                    self.mm(t, self.ps["st"][32 * s4:32 * s4 + 32,
                                              jq * 128:jq * 128 + 64],
                            lr, rhs, True, True, 0, 32 * s4)
                    op = self.mm(t, self.ps["st"][32 * s4:32 * s4 + 32,
                                                  jq * 128 + 64:jq * 128 + 128],
                                 ln2, rhs, True, True, 0, 32 * s4)
            sb.inc(op, "pw")
        self.mark(("Sinc", cidx), "pw")

        for j in range(4):
            self._g_fill(blk, cidx, j)

    def pA_xdT(self, blk, cidx):
        sb, S = self, self.sb
        m, pc = cidx // CPM, cidx % CPM
        w0 = pc * CH
        cv = self.chvars(cidx)
        xss, xd = cv["xss"], cv["xd"]
        for grp in range(4):
            @blk.tensor
            def _(t, grp=grp):
                self.wtk(t, "sw", ("convS", m))
                self.guard(t, "wideA")
                for q in range(4):
                    idx = grp * 4 + q
                    op = t.transpose(self.ptv(q * CH, (q + 1) * CH),
                                     xss[:, idx * MAC + w0:idx * MAC + w0 + CH],
                                     S["cbf"][:, 0:128])
                sb.inc(op, "pw")
            tk = self.tok("pw")

            @blk.scalar
            def _(s, grp=grp, tk=tk):
                sb.wait(s, "pw", tk)
                self.wtk(s, "pw", ("Yl", cidx - 2))
                op = s.activation(xd[:, grp * 512:(grp + 1) * 512],
                                  self.ptv(0, 512), AF.Copy)
                sb.inc(op, "sw")
            self.setguard("wideA", "sw")
        self.mark(("xdT", cidx), "pw")
        self.mark(("xdS", cidx), "sw")

    def _g_fill(self, blk, cidx, j):
        sb, S, PS = self, self.sb, self.ps
        cv = self.chvars(cidx)
        prc, pic = cv["prc"], cv["pic"]
        qrv, qinv = cv["qrv"], cv["qinv"]
        s4, half = j // 2, j % 2
        heads = [4 * jj + s4 for jj in range(8)]
        hgrp = heads[half * 4:(half + 1) * 4]
        gbank = ["a", "b", "c", "d"][j % 4]

        @blk.tensor
        def _(t):
            self.wtk(t, "vw", ("bldA", cidx))
            self.wtk(t, "vw", ("bldB", cidx))
            self.guard(t, gbank)
            for i, h in enumerate(hgrp):
                gb, rr = h // 4, (h % 4) * 32
                for a in range(4):
                    osl = PS[gbank][:, i * CH + 32 * a:i * CH + 32 * a + 32]
                    psl = slice(gb * CH + 32 * a, gb * CH + 32 * a + 32)
                    self.mm(t, osl,
                            qrv[a][rr:rr + 32, gb * CH:(gb + 1) * CH],
                            prc[rr:rr + 32, psl], a == 0, False, rr, 0)
                    self.mm(t, osl,
                            qinv[a][rr:rr + 32, gb * CH:(gb + 1) * CH],
                            pic[rr:rr + 32, psl], False, False, rr, 0)
                op = self.mm(t, PS[gbank][:, i * CH:(i + 1) * CH],
                             S["cbf"][:, 0:128],
                             self.V(DIS_OFF, 128, 8 * 512, BF16)[
                                 :, (half * 4 + s4) * 512 + i * CH:
                                 (half * 4 + s4) * 512 + (i + 1) * CH],
                             False, True)
            sb.inc(op, "pw")
        self.mark(("Gf", cidx, j), "pw")

    def _gm_drain(self, blk, cidx, j):
        sb, S, PS = self, self.sb, self.ps
        gbank = ["a", "b", "c", "d"][j % 4]
        gm8 = self.V(GM8 + j * 1024, 128, 512, BF16)

        @blk.vector
        def _(v):
            sb.wait(v, "vw")
            sb.wait(v, "pw", self.tk2[("Gf", cidx, j)])
            self.wtk(v, "pw", ("Yf", cidx - 1, j))
            op = v.tensor_tensor(gm8[:, :], PS[gbank][:, 0:512],
                                 S["cbf"][:, 128:640], ALU.mult)
            sb.inc(op, "vw")
        self.mark(("gm", cidx, j), "vw")
        self.setguard(gbank, "vw")

    def pA_gyGdrain(self, blk, cidx):
        for j in range(4):
            self._gm_drain(blk, cidx, j)
        for j in range(4, 8):
            self._g_fill(blk, cidx, j)
        for j in range(4, 8):
            self._gm_drain(blk, cidx, j)
        self.mark(("Gl", cidx), "pw")

    def _y_fill(self, blk, cidx, j):
        sb, S, PS = self, self.sb, self.ps
        m, pc = cidx // CPM, cidx % CPM
        cv = self.chvars(cidx)
        xd, cer, cei = cv["xd"], cv["cer"], cv["cei"]
        s4, half = j // 2, j % 2
        heads = [4 * jj + s4 for jj in range(8)]
        hgrp = heads[half * 4:(half + 1) * 4]
        ybank = ["c", "d"][j % 2]
        ro = (s4 % 2) * 64
        gm8 = self.V(GM8 + j * 1024, 128, 512, BF16)

        @blk.tensor
        def _(t):
            self.wtk(t, "vw", ("gm", cidx, j))
            self.wtk(t, "vw", ("bldA", cidx))
            self.wtk(t, "sw", ("tailS", cidx - 1))
            self.wtk(t, "sw", ("xdS", cidx))
            self.guard(t, ybank)
            for i, h in enumerate(hgrp):
                gb, rr = h // 4, (h % 4) * 32
                jq = h // 4
                self.mm(t, PS[ybank][ro:ro + 64, i * CH:(i + 1) * CH],
                        xd[:, h * 64:(h + 1) * 64],
                        gm8[:, i * CH:(i + 1) * CH],
                        True, False)
                self.mm(t, PS[ybank][ro:ro + 64, i * CH:(i + 1) * CH],
                        S["stateb"][32 * s4:32 * s4 + 32, jq * 128:jq * 128 + 64],
                        cer[rr:rr + 32, gb * CH:(gb + 1) * CH],
                        False, False, 32 * s4, ro)
                op = self.mm(t, PS[ybank][ro:ro + 64, i * CH:(i + 1) * CH],
                             S["stateb"][32 * s4:32 * s4 + 32,
                                         jq * 128 + 64:jq * 128 + 128],
                             cei[rr:rr + 32, gb * CH:(gb + 1) * CH],
                             False, True, 32 * s4, ro)
            sb.inc(op, "pw")
        self.mark(("Yf", cidx, j), "pw")

    def _yt_drain(self, blk, cidx, j):
        sb, S, PS = self, self.sb, self.ps
        m, pc = cidx // CPM, cidx % CPM
        w0 = pc * CH
        ytile = self.chvars(cidx)["ytile"]
        s4, half = j // 2, j % 2
        heads = [4 * jj + s4 for jj in range(8)]
        hgrp = heads[half * 4:(half + 1) * 4]
        ybank = ["c", "d"][j % 2]
        ro = (s4 % 2) * 64

        @blk.scalar
        def _(s):
            sb.wait(s, "pw", self.tk2[("Yf", cidx, j)])
            self.wtk(s, "ds", ("yspill", m - 1))
            h0 = hgrp[0]
            rbase = (h0 % 2) * 64
            yv = ytile.rearrange("p (g w) -> p g w", g=NIB)[
                rbase:rbase + 64, h0 // 2:NIB:2, w0:w0 + CH][:, 0:4, :]
            pv = PS[ybank][ro:ro + 64, 0:512].rearrange("p (i w) -> p i w", i=4)
            op = s.activation(yv, pv, AF.Copy)
            sb.inc(op, "sw")
        self.setguard(ybank, "sw")

    def pA_gyY(self, blk, cidx):
        sb, S, PS, Dr = self, self.sb, self.ps, self.dram
        m, pc = cidx // CPM, cidx % CPM
        c0 = m * MAC
        for j in range(4):
            self._y_fill(blk, cidx, j)
        for j in range(4):
            self._yt_drain(blk, cidx, j)
        for j in range(4, 8):
            self._y_fill(blk, cidx, j)
        self.mark(("Yl", cidx), "pw")
        for j in range(4, 8):
            self._yt_drain(blk, cidx, j)
        self.mark(("yt", cidx), "sw")

        cur = S["state0"] if cidx % 2 == 0 else S["state1"]
        nxt = S["state1"] if cidx % 2 == 0 else S["state0"]

        @blk.vector
        def _(v):
            sb.wait(v, "vw")
            sb.wait(v, "pw", self.tk2[("Sinc", cidx)])
            sb.wait(v, "pw", self.tk2[("Yl", cidx)])
            self.wtk(v, "sw", ("convS", m + 1))
            et = S["etot"]
            erB = et[:, pc * 16:pc * 16 + 8].rearrange(
                "p (g o) -> p g o", o=1).broadcast_to([128, 8, 64])
            eiB = et[:, pc * 16 + 8:pc * 16 + 16].rearrange(
                "p (g o) -> p g o", o=1).broadcast_to([128, 8, 64])
            curv = cur[:, :].rearrange("p (j w) -> p j w", j=8)
            nxtv = nxt[:, :].rearrange("p (j w) -> p j w", j=8)
            psv = self.ps["st"][:, :].rearrange("p (j w) -> p j w", j=8)
            curS, curT = curv[:, :, 0:64], curv[:, :, 64:128]
            nxtS, nxtT = nxtv[:, :, 0:64], nxtv[:, :, 64:128]
            if cidx == 0:
                self.vsync(v, v.tensor_copy(nxtS, psv[:, :, 0:64]))
                self.vsync(v, v.tensor_copy(nxtT, psv[:, :, 64:128]))
            else:
                tA = S["tmpB"][:, 0:512].rearrange("p (j w) -> p j w", j=8)
                self.vsync(v, v.tensor_tensor(nxtS, curS, erB, ALU.mult))
                self.vsync(v, v.tensor_tensor(tA, curT, eiB, ALU.mult))
                self.vsync(v, v.tensor_tensor(nxtS, nxtS, tA, ALU.add))
                self.vsync(v, v.tensor_tensor(nxtS, nxtS, psv[:, :, 0:64], ALU.add))
                self.vsync(v, v.tensor_tensor(nxtT, curT, erB, ALU.mult))
                self.vsync(v, v.tensor_tensor(tA, curS, eiB, ALU.mult))
                self.vsync(v, v.tensor_sub(nxtT, nxtT, tA))
                self.vsync(v, v.tensor_tensor(nxtT, nxtT, psv[:, :, 64:128], ALU.add))
        self.mark(("tail", cidx), "vw")
        self.setguard("st", "vw")

        @blk.scalar
        def _(s):
            self.wtk(s, "vw", ("tail", cidx))
            self.wtk(s, "pw", ("Yl", cidx))
            op = s.activation(S["stateb"][:, :], nxt[:, :], AF.Copy)
            sb.inc(op, "sw")
        self.mark(("tailS", cidx), "sw")

        if pc == CPM - 1:
            @blk.sync
            def _(s):
                sb.wait(s, "sw", self.tk2[("yt", cidx)])
                yt = self.V(OA["ytile"], 128, NIB * MAC, BF16)
                op = s.dma_start(
                    out=Dr["sc_y"].rearrange("p (g t) -> p g t", g=NIB)[
                        :, :, c0:c0 + MAC],
                    in_=yt.rearrange("p (g w) -> p g w", g=NIB))
                sb.inc(op, "ds", 16)
            self.mark(("yspill", m), "ds")

    def passA_run(self, blk):
        NCH = NMAC * CPM
        self.pA_xbct(blk, 0)
        self.pA_conv(blk, 0)
        self.pA_xbct(blk, 1)
        self.pA_theta_scans(blk, 0)
        self.pA_buildA(blk, 0)
        self.pA_buildB(blk, 0)
        for k in range(NCH):
            self.pA_gyG(blk, k)
            if k % 2 == 1 and (k + 3) // 2 < NMAC:
                self.pA_xbct(blk, (k + 3) // 2)
            nxt = k + 1
            if nxt < NCH:
                self.pA_buildA(blk, nxt)
            self.pA_gyGdrain(blk, k)
            if nxt < NCH:
                self.pA_buildB(blk, nxt)
            if k % CPM == 0 and k // CPM + 1 < NMAC:
                self.pA_theta_scans(blk, k // CPM + 1)
                self.pA_conv(blk, k // CPM + 1)
            self.pA_gyY(blk, k)
        self.setguard("wide", "sw")

    def collective2(self, blk):
        sb, S, Dr = self, self.sb, self.dram
        final = S["state0"] if (NMAC * CPM) % 2 == 0 else S["state1"]

        @blk.gpsimd
        def _(g):
            sb.wait(g, "vw")
            sb.inc(g.dma_start(out=Dr["cc2_in"], in_=final[:, :]), "dg", 16)
            sb.wait(g, "dg")
            op = g.collective_compute(
                "AllGather", ALU.bypass,
                replica_groups=[[0, 1], [2, 3], [4, 5], [6, 7]],
                ins=[Dr["cc2_in"].opt()],
                outs=[Dr["cc2_out"].opt()])
            sb.inc(op, "cc")
            sb.wait(g, "cc")
            sb.inc(g.dma_start(out=S["state0"][:, :], in_=Dr["cc2_out"][0:128, :]),
                   "dg", 16)
            sb.inc(g.dma_start(out=S["state1"][:, :], in_=Dr["cc2_out"][128:256, :]),
                   "dg", 16)

        @blk.vector
        def _(v):
            sb.wait(v, "dg")
            self.wtk(v, "sw", ("tailS", NMAC * CPM - 1))
            self.vsync(v, v.tensor_scalar(S["statein"][:, :], S["state0"][:, :],
                                          S["sel2c"][:, 0:1], None, ALU.mult))
            self.vsync(v, v.scalar_tensor_tensor(
                S["statein"][:, :], S["state1"][:, :],
                S["sel2c"][:, 1:2], S["statein"][:, :], ALU.mult, ALU.add))
            self.vsync(v, v.tensor_copy(S["stateb"][:, :], S["statein"][:, :]))

    def _passB_load(self, g, m, boff):
        sb, Dr = self, self.dram
        c0 = m * MAC
        sem = "dgm0" if m % 2 == 0 else "dgm1"
        sb.wait(g, sem)
        sb.wait(g, "dgs")
        sb.wait(g, "ds")
        for scn, off, nb, dt in [("sc_y", IB_YT, NIB, BF16), ("sc_z", IB_ZT, NIB, BF16),
                                 ("sc_rg", IB_RG, 8, BF16), ("sc_fg", IB_FG, 8, BF16)]:
            op = g.dma_start(
                out=self.V(boff + off, 128, nb * MAC, dt).rearrange(
                    "p (g w) -> p g w", g=nb),
                in_=Dr[scn].rearrange("p (g t) -> p g t", g=nb)[:, :, c0:c0 + MAC])
            sb.inc(op, sem, 16)
        op = g.dma_start(
            out=self.V(boff + IB_XR, 128, NDB * MAC).rearrange("p (g w) -> p g w", g=NDB),
            in_=Dr["sc_xr"].rearrange("p (g t) -> p g t", g=NDB)[:, :, c0:c0 + MAC])
        sb.inc(op, sem, 16)
        self.wldtok[("B", m)] = self.tok(sem)

    def passB(self, blk):
        sb, Dr = self, self.dram

        @blk.gpsimd
        def _(g):
            sb.wait(g, "vw")
            sb.wait(g, "dgs")
            op = g.dma_start(
                out=self.V(OB["opw"], 128, NIB * D, BF16).rearrange(
                    "p (g f) -> p g f", g=NIB),
                in_=Dr["w_oproj"].rearrange("(g p) f -> p g f", p=128))
            sb.inc(op, "dg", 16)
            op = g.dma_start(
                out=self.V(OF["wexp0"], 128, 8 * D, BF16).rearrange(
                    "p (g f) -> p g f", g=NDB),
                in_=Dr["w_exp"][0].rearrange("(g p) f -> p g f", p=128))
            sb.inc(op, "dg0", 16)
            self.wldtok[0] = self.tok("dg0")
            self._passB_load(g, 0, OB["in0"])

        for m in range(NMAC):
            self.passB_macro(blk, m)
        self.passB2(blk)

    def passB_macro(self, blk, m):
        sb, S, PS, Dr = self, self.sb, self.ps, self.dram
        c0 = m * MAC
        boff = OB["in0"] if m % 2 == 0 else OB["in1"]
        nxtoff = OB["in1"] if m % 2 == 0 else OB["in0"]
        yt = self.V(boff + IB_YT, 128, NIB * MAC, BF16)
        zt = self.V(boff + IB_ZT, 128, NIB * MAC, BF16)
        rg = self.V(boff + IB_RG, 128, 8 * MAC, BF16)
        fg = self.V(boff + IB_FG, 128, 8 * MAC, BF16)
        xroff = boff + IB_XR
        xr = self.V(xroff, 128, NDB * MAC)
        cosg = self.V(OB["cosg"], 128, 8 * MAC, BF16)
        sing = self.V(OB["sing"], 128, 8 * MAC, BF16)
        cegr = self.V(OB["cegr"], 128, 8 * MAC, BF16)
        cegi = self.V(OB["cegi"], 128, 8 * MAC, BF16)

        if m + 1 < NMAC:
            @blk.gpsimd
            def _(g, m=m, nxtoff=nxtoff):
                sb.wait(g, "vw", self.wtok.get(("Bbuf", (m + 1) % 2), 0))
                self._passB_load(g, m + 1, nxtoff)

        @blk.scalar
        def _(s):
            sb.wait(s, "dgm0" if m % 2 == 0 else "dgm1", self.wldtok[("B", m)])
            sb.wait(s, "vw")
            if m == 0:
                sb.wait(s, "pw")
            self.ssync(s, s.activation(cosg[:, :], fg[:, :], AF.Sin, scale=0.5))
            self.ssync(s, s.activation(sing[:, :], fg[:, :], AF.Sin, scale=0.5,
                                       bias=S["consts"][:, 1:2]))
            self.ssync(s, s.activation(cegr[:, :], rg[:, :], AF.Exp))

        @blk.vector
        def _(v):
            sb.wait(v, "sw")
            self.vsync(v, v.tensor_mul(cegi[:, :], cosg[:, :], sing[:, :]))
            self.vsync(v, v.tensor_mul(cosg[:, :], cosg[:, :], cosg[:, :]))
            self.vsync(v, v.tensor_scalar(cosg[:, :], cosg[:, :], -2.0, 1.0,
                                          ALU.mult, ALU.add))
            self.vsync(v, v.tensor_scalar_mul(sing[:, :], cegi[:, :], 2.0))
            self.vsync(v, v.tensor_mul(cosg[:, :], cosg[:, :], cegr[:, :]))
            self.vsync(v, v.tensor_mul(sing[:, :], sing[:, :], cegr[:, :]))

        for pc in range(CPM):
            w0 = pc * CH

            @blk.tensor
            def _(t, pc=pc, w0=w0):
                sb.wait(t, "vw")
                self.guard(t, "wide")
                bc = S["bcall"][:, c0 + w0:c0 + w0 + CH]
                t.matmul(PS["wide"][:, 0:128], S["cbf"][:, 896:1024],
                         bc[:, :], start=True, stop=True)
                op = t.matmul(PS["wide"][:, 128:256], S["cbf"][:, 1024:1152],
                              bc[:, :], start=True, stop=True)
                sb.inc(op, "pw")

            @blk.vector
            def _(v, pc=pc, w0=w0):
                sb.wait(v, "pw")
                self.vsync(v, v.tensor_copy(S["gm"][:, 0:256], PS["wide"][:, 0:256]))
            self.setguard("wide", "vw")

            @blk.vector
            def _(v, pc=pc, w0=w0):
                CrR, CiR = S["gm"][:, 0:CH], S["gm"][:, CH:2 * CH]

                def rep3(x):
                    return x.rearrange("p (o w) -> p o w", o=1).broadcast_to([128, 8, CH])
                g3 = lambda x: x.rearrange("p (g w) -> p g w", g=8)[:, :, w0:w0 + CH]
                tb = S["tmpB"][:, 0:512].bitcast(BF16)
                self.vsync(v, v.tensor_tensor(tb[:, 0:8 * CH].rearrange(
                    "p (g w) -> p g w", g=8), g3(sing), rep3(CiR), ALU.mult))
                self.vsync(v, v.tensor_tensor(g3(cegr), g3(cosg), rep3(CrR), ALU.mult))
                self.vsync(v, v.tensor_tensor(
                    g3(cegr), g3(cegr),
                    tb[:, 0:8 * CH].rearrange("p (g w) -> p g w", g=8), ALU.subtract))
                self.vsync(v, v.tensor_tensor(tb[:, 0:8 * CH].rearrange(
                    "p (g w) -> p g w", g=8), g3(cosg), rep3(CiR), ALU.mult))
                self.vsync(v, v.tensor_tensor(g3(cegi), g3(sing), rep3(CrR), ALU.mult))
                self.vsync(v, v.tensor_tensor(
                    g3(cegi), g3(cegi),
                    tb[:, 0:8 * CH].rearrange("p (g w) -> p g w", g=8), ALU.add))

            for s4 in range(4):
                heads = [4 * jj + s4 for jj in range(8)]
                for half in range(2):
                    hgrp = heads[half * 4:(half + 1) * 4]
                    ybank = ["c", "d"][(s4 * 2 + half) % 2]

                    @blk.tensor
                    def _(t, hgrp=hgrp, ybank=ybank, s4=s4, w0=w0):
                        sb.wait(t, "vw")
                        self.guard(t, ybank)
                        ro = (s4 % 2) * 64
                        for i, h in enumerate(hgrp):
                            gb, rr = h // 4, (h % 4) * 32
                            jq = h // 4
                            self.mm(t, PS[ybank][ro:ro + 64, i * CH:(i + 1) * CH],
                                    S["stateb"][32 * s4:32 * s4 + 32,
                                                jq * 128:jq * 128 + 64],
                                    cegr[rr:rr + 32, gb * MAC + w0:gb * MAC + w0 + CH],
                                    True, False, 32 * s4, ro)
                            op = self.mm(t, PS[ybank][ro:ro + 64, i * CH:(i + 1) * CH],
                                         S["stateb"][32 * s4:32 * s4 + 32,
                                                     jq * 128 + 64:jq * 128 + 128],
                                         cegi[rr:rr + 32, gb * MAC + w0:gb * MAC + w0 + CH],
                                         False, True, 32 * s4, ro)
                        sb.inc(op, "pw")
                    ytk = self.tok("pw")

                    @blk.vector
                    def _(v, hgrp=hgrp, ybank=ybank, s4=s4, ytk=ytk, w0=w0):
                        sb.wait(v, "pw", ytk)
                        ro = (s4 % 2) * 64
                        h0 = hgrp[0]
                        rbase = (h0 % 2) * 64
                        yv = yt.rearrange("p (g w) -> p g w", g=NIB)[
                            rbase:rbase + 64, h0 // 2:NIB:2, w0:w0 + CH][:, 0:4, :]
                        pv = PS[ybank][ro:ro + 64, 0:512].rearrange("p (i w) -> p i w", i=4)
                        op = v.tensor_tensor(yv, yv, pv, ALU.add)
                        sb.inc(op, "vw")
                    self.setguard(ybank, "vw")

        gt = self.V(OB["gt"], 128, NIB * MAC, BF16)
        sq = self.V(OB["sq"], 128, NIB * MAC, BF16)

        @blk.vector
        def _(v):
            sb.wait(v, "vw")
            self.vsync(v, v.tensor_mul(gt[:, :], yt[:, :], zt[:, :]))

        @blk.scalar
        def _(s):
            sb.wait(s, "vw")
            sb.inc(s.activation(sq[:, :], gt[:, :], AF.Square), "sw")

        @blk.tensor
        def _(t):
            sb.wait(t, "sw")
            self.guard(t, "a")
            for bb in range(NIB):
                op = t.matmul(PS["a"][0:1, 0:MAC], S["cbf"][:, 1152:1153],
                              sq[:, bb * MAC:(bb + 1) * MAC],
                              start=(bb == 0), stop=(bb == NIB - 1))
            sb.inc(op, "pw")

        @blk.vector
        def _(v):
            sb.wait(v, "pw")
            self.vsync(v, v.tensor_scalar_mul(S["stat"][:, 0:MAC],
                                              PS["a"][0:1, 0:MAC], 1.0 / DIN))
        self.setguard("a", "vw")

        @blk.scalar
        def _(s):
            sb.wait(s, "vw")
            self.ssync(s, s.activation(S["stat"][:, 0:MAC], S["stat"][:, 0:MAC],
                                       AF.Ln, bias=S["consts"][0:1, 3:4]))
            self.ssync(s, s.activation(S["stat"][:, 0:MAC], S["stat"][:, 0:MAC],
                                       AF.Exp, scale=-0.5))

        @blk.tensor
        def _(t):
            sb.wait(t, "sw")
            self.guard(t, "b")
            op = t.matmul(PS["b"][:, 0:MAC], S["o_row"][:, :], S["stat"][:, 0:MAC],
                          start=True, stop=True)
            sb.inc(op, "pw")

        @blk.vector
        def _(v):
            sb.wait(v, "pw")
            for bb in range(NIB):
                op = v.scalar_tensor_tensor(sq[:, bb * MAC:(bb + 1) * MAC],
                                            gt[:, bb * MAC:(bb + 1) * MAC],
                                            S["nwcol"][:, bb:bb + 1],
                                            PS["b"][:, 0:MAC], ALU.mult, ALU.mult)
            sb.inc(op, "vw")
        self.setguard("b", "vw")

        @blk.tensor
        def _(t):
            sb.wait(t, "vw")
            sb.wait(t, "dg")
            for nm in ["a", "b", "c", "d"]:
                self.guard(t, nm)
            opw = self.V(OB["opw"], 128, NIB * D, BF16)
            for do in range(NDB):
                bank = PS[["a", "b", "c", "d"][do // 2]][:, (do % 2) * MAC:(do % 2 + 1) * MAC]
                for di2 in range(NIB):
                    op = t.matmul(bank,
                                  opw[:, di2 * D + do * 128:di2 * D + (do + 1) * 128],
                                  sq[:, di2 * MAC:(di2 + 1) * MAC],
                                  start=(di2 == 0), stop=(di2 == NIB - 1))
            sb.inc(op, "pw")

        @blk.vector
        def _(v):
            sb.wait(v, "pw")
            for do in range(NDB):
                bank = PS[["a", "b", "c", "d"][do // 2]][:, (do % 2) * MAC:(do % 2 + 1) * MAC]
                op = v.tensor_add(xr[:, do * MAC:(do + 1) * MAC],
                                  xr[:, do * MAC:(do + 1) * MAC], bank)
            sb.inc(op, "vw")
        for nm in ["a", "b", "c", "d"]:
            self.setguard(nm, "vw")

        if PASSB_TRUNC:
            @blk.sync
            def _(s, m=m):
                sb.wait(s, "vw")
                op = s.dma_start(
                    out=Dr["out_x"].rearrange("(g p) t -> p g t", p=128)[
                        :, :, c0:c0 + MAC],
                    in_=xr.rearrange("p (g w) -> p g w", g=NDB))
                sb.inc(op, "ds", 16)
            self.wtok[("Bbuf", m % 2)] = self.tok("vw")
            return

        self.ln_stats(blk, xroff, 0, MAC, NDB, OB["sq2"], xcols=MAC)
        self.ln_norm(blk, xroff, OB["xh2"], 0, MAC, NDB, OB["sq2"],
                     xcols=MAC, hatcols=MAC)
        self.routing(blk, 0, MAC, gbase=c0, xoff=xroff, xcols=MAC)

        @blk.sync
        def _(s, m=m):
            sb.wait(s, "vw")
            op = s.dma_start(
                out=Dr["sc_xh"].rearrange("p (g t) -> p g t", g=NDB)[:, :, c0:c0 + MAC],
                in_=self.V(OB["xh2"], 128, NDB * MAC, BF16).rearrange(
                    "p (g w) -> p g w", g=NDB))
            sb.inc(op, "ds", 16)
            op = s.dma_start(
                out=Dr["sc_xr"].rearrange("p (g t) -> p g t", g=NDB)[:, :, c0:c0 + MAC],
                in_=xr.rearrange("p (g w) -> p g w", g=NDB))
            sb.inc(op, "ds", 16)
        self.wtok[("Bbuf", m % 2)] = self.tok("vw")

    def passB2(self, blk):
        sb, Dr = self, self.dram
        XB, HB, MB = 0, 16384, 24576
        SETS = [32768, 65536]
        dstok_b1 = self.tok("ds")

        def loader(j4):
            soff = SETS[j4 % 2]
            sem = "dgm0" if j4 % 2 == 0 else "dgm1"

            @blk.gpsimd
            def _(g, j4=j4, soff=soff, sem=sem):
                sb.wait(g, sem)
                sb.wait(g, "ds", dstok_b1)
                osem = "dso0" if j4 % 2 == 0 else "dso1"
                sb.wait(g, osem, self.wtok.get(("B2out", j4 % 2), 0))
                sb.wait(g, "vw", self.wtok.get(("B2buf", j4 % 2), 0))
                c0 = j4 * 512
                op = g.dma_start(
                    out=self.V(soff + XB, 128, NDB * 512).rearrange(
                        "p (g w) -> p g w", g=NDB),
                    in_=Dr["sc_xr"].rearrange("p (g t) -> p g t", g=NDB)[
                        :, :, c0:c0 + 512])
                sb.inc(op, sem, 16)
                op = g.dma_start(
                    out=self.V(soff + HB, 128, NDB * 512, BF16).rearrange(
                        "p (g w) -> p g w", g=NDB),
                    in_=Dr["sc_xh"].rearrange("p (g t) -> p g t", g=NDB)[
                        :, :, c0:c0 + 512])
                sb.inc(op, sem, 16)
                self.wldtok[("B2", j4)] = self.tok(sem)

        loader(0)
        for j4 in range(4):
            soff = SETS[j4 % 2]
            sem = "dgm0" if j4 % 2 == 0 else "dgm1"
            if j4 + 1 < 4:
                loader(j4 + 1)

            @blk.vector
            def _(v, j4=j4, sem=sem):
                sb.wait(v, sem, self.wldtok[("B2", j4)])

            self.ffn_experts(blk, 2, soff + HB, soff + XB, 0, 512,
                             gbase=j4 * 512, xr_cols=512, maskoff=soff + MB,
                             hatcols=512)

            @blk.sync
            def _(s, j4=j4, soff=soff):
                sb.wait(s, "vw")
                op = s.dma_start(
                    out=Dr["out_x"].rearrange("(g p) t -> p g t", p=128)[
                        :, :, j4 * 512:(j4 + 1) * 512],
                    in_=self.V(soff + XB, 128, NDB * 512).rearrange(
                        "p (g w) -> p g w", g=NDB))
                sb.inc(op, "dso0" if j4 % 2 == 0 else "dso1", 16)
            self.wtok[("B2out", j4 % 2)] = self.tok("dso0" if j4 % 2 == 0
                                                    else "dso1")
            self.wtok[("B2buf", j4 % 2)] = self.tok("vw")

    def finish(self, blk):
        sb = self

        @blk.sync
        def _(s):
            sb.wait(s, "vw")
            sb.wait(s, "pw")
            sb.wait(s, "sw")
            for nm in ["dg", "dg0", "dg1", "dgx", "dgs"]:
                sb.wait(s, nm)
            op = s.dma_start(
                out=self.dram["out_x"].rearrange("(g p) t -> p g t", p=128),
                in_=self.V(OF["xres"], 128, NDB * T).rearrange("p (g t) -> p g t", g=NDB))
            sb.inc(op, "ds", 16)
            sb.wait(s, "ds")

    def finish_full(self, blk):
        sb = self

        @blk.sync
        def _(s):
            sb.wait(s, "vw")
            sb.wait(s, "pw")
            sb.wait(s, "sw")
            for nm in ["dg", "dg0", "dg1", "dgx", "dgs", "ds",
                       "dgm0", "dgm1", "dso0", "dso1"]:
                sb.wait(s, nm)

    def build(self):
        self.declare()
        with self.nc.Block() as blk:
            self.preload(blk)
            self.ffn1(blk)
            if self.stop_after == "ffn1":
                self.finish(blk)
                return self.nc
            self.inproj(blk)
            if self.stop_after == "inproj":
                self.finish(blk)
                return self.nc
            self.passA_prep(blk)
            self.passA_run(blk)
            self.collective2(blk)
            if self.stop_after == "passA":
                self.finish(blk)
                return self.nc
            self.passB(blk)
            self.finish_full(blk)
        return self.nc


def build_nc(stop_after=None, dbg=False):
    return Bld(stop_after, dbg).build()


KERNEL_STOP_AFTER = None
PASSB_TRUNC = False


def kernel(**inputs):
    shared = prep_shared(inputs)
    in_maps = []
    for c in range(8):
        mm = dict(shared)
        mm.update(prep_core(inputs, c))
        in_maps.append(mm)
    nc = build_nc(stop_after=KERNEL_STOP_AFTER)
    res = run_bass_kernel_spmd(nc, in_maps, core_ids=list(range(8)))
    out = np.zeros((B, L, D), np.float32)
    for c in range(8):
        b, hf = c // 2, c % 2
        out[b, hf * T:(hf + 1) * T, :] = res.results[c]["out_x"].T
    return out
